# version 14
# baseline (speedup 1.0000x reference)
import contextlib
import numpy as np
import concourse.bass as bass
import concourse.mybir as mybir
from concourse.bass_utils import run_bass_kernel_spmd

F32 = mybir.dt.float32
BF16 = mybir.dt.bfloat16
ALU = mybir.AluOpType
ACT = mybir.ActivationFunctionType
AX = mybir.AxisListType.X

S = 2048
D = 2048
H = 16
I = 4096
NT = S // 128
DC = D // 128
IC = I // 128
SLOT = DC * 512
SOF = (0, 2, 1)
NQB = 4
NG = 4
M32 = float(np.float32(12582912.0))
M64 = float(np.float32(1.5 * 2 ** 52))
_SCL64 = 1.0 / np.sqrt(np.sqrt(128.0))
SCL = float(np.float32(_SCL64))
SCL_C = float(np.float32(_SCL64 - SCL))
EPS = 1.0e-5
INV_WN = float(np.float32(1.0 / 8388608.0))
INV127 = float(np.float32(1.0 / 127.0))


class Prog:
    ENGINES = ("sync", "gpsimd", "vector", "scalar", "tensor")

    def __init__(self, es, nc):
        self.nc = nc
        self.ops = {e: [] for e in self.ENGINES}
        self.sems = {}
        self.es = es
        self.last_wait = {e: {} for e in self.ENGINES}

    def sem(self, name):
        if name not in self.sems:
            h = self.es.enter_context(self.nc.semaphore(name))
            self.sems[name] = [h, 0]
        return name

    def n(self, s):
        self.sem(s)
        return self.sems[s][1]

    def wait(self, eng, s, val=None):
        if isinstance(s, tuple):
            s, val = s
        h, n = self.sems[self.sem(s)]
        if val is None:
            val = n
        assert val <= n, f"wait on future: {s} {val} > {n}"
        if val <= 0:
            return
        if self.last_wait[eng].get(s, 0) >= val:
            return
        self.last_wait[eng][s] = val
        self.ops[eng].append(lambda e, h=h, v=val: e.wait_ge(h, v))

    def op(self, eng, fn, inc=None, k=1):
        if inc is None:
            self.ops[eng].append(fn)
            return None
        h, _ = self.sems[self.sem(inc)]
        self.sems[inc][1] += k
        tgt = self.sems[inc][1]
        self.ops[eng].append(lambda e, fn=fn, h=h, k=k: fn(e).then_inc(h, k))
        return tgt

    def dma(self, eng, fn, stream):
        return (stream, self.op(eng, fn, inc=stream, k=16))

    def full_barrier(self, exclude=()):
        for eng in self.ENGINES:
            for s in list(self.sems):
                if any(s.startswith(x) for x in exclude):
                    continue
                self.wait(eng, s)

    def sig(self, eng, inc, k=1):
        if eng in ("vector", "scalar", "gpsimd"):
            self.ops[eng].append(lambda e: e.drain())
        return self.op(eng, lambda e: e.nop(), inc=inc, k=k)


def build_core():
    es = contextlib.ExitStack()
    nc = bass.Bass("TRN2", target_bir_lowering=False)

    x_in = nc.dram_tensor("x", [S, D], F32, kind="ExternalInput").ap()
    wqt = nc.dram_tensor("wqt", [D, D], F32, kind="ExternalInput").ap()
    wkt = nc.dram_tensor("wkt", [D, D], F32, kind="ExternalInput").ap()
    wvt = nc.dram_tensor("wvt", [D, D], F32, kind="ExternalInput").ap()
    wot = nc.dram_tensor("wot", [D, D], F32, kind="ExternalInput").ap()
    wupt = nc.dram_tensor("wupt", [D, I], F32, kind="ExternalInput").ap()
    wdnt = nc.dram_tensor("wdnt", [I, D], F32, kind="ExternalInput").ap()
    b_in = {n: nc.dram_tensor(n, [D], F32, kind="ExternalInput").ap()
            for n in ("bq", "bk", "bv", "bo")}
    out = nc.dram_tensor("out", [S, D], F32, kind="ExternalOutput").ap()

    x_sp = nc.dram_tensor("x_sp", [S, D], BF16, kind="Internal").ap()
    x2_sp = nc.dram_tensor("x2_sp", [S, D], BF16, kind="Internal").ap()
    wo_sp = nc.dram_tensor("wo_sp", [D, D], BF16, kind="Internal").ap()
    o_sp = nc.dram_tensor("o_sp", [H, 128, S], BF16, kind="Internal").ap()
    tup_sp = nc.dram_tensor("tup_sp", [I // 512, DC, 128, 512], BF16, kind="Internal").ap()
    tdn_sp = nc.dram_tensor("tdn_sp", [D // 512, IC, 128, 512], BF16, kind="Internal").ap()

    def sbuf(name, shape, dt=F32, st=es):
        return st.enter_context(nc.sbuf_tensor(name, shape, dt))

    p = Prog(es, nc)
    ST, SV, SA = (p.sem(s) for s in ("s_t", "s_v", "s_a"))
    SG = p.sem("s_g")

    mu = sbuf("mu", [128, 1])
    sc_qk = sbuf("sc_qk", [128, 1])
    eps_t = sbuf("eps_t", [128, 1])
    e35_t = sbuf("e35_t", [128, 1])
    c60_t = sbuf("c60_t", [128, 1])
    m64b = sbuf("m64b", [128, 1])
    nm64b = sbuf("nm64b", [128, 1])
    ones_bf = sbuf("ones_bf", [128, 128], BF16)
    ones_f = sbuf("ones_f", [128, 128])
    strip = sbuf("strip", [128, 896], BF16)
    bq_s = sbuf("bq_s", [128, H])
    bk_s = sbuf("bk_s", [128, H])
    bv_s = sbuf("bv_s", [128, H])
    wst_up = sbuf("wst_up", [128, 128])
    wst_dn = sbuf("wst_dn", [128, 128])
    th_up = sbuf("th_up", [128, 1])
    th_dn = sbuf("th_dn", [128, 1])
    nth_up = sbuf("nth_up", [128, 1])
    nth_dn = sbuf("nth_dn", [128, 1])
    s_upt = sbuf("s_upt", [128, 1])
    s_dnt = sbuf("s_dnt", [128, 1])
    nk2 = sbuf("nk2", [128, H])
    smallA = sbuf("smallA", [128, 4])
    negC = sbuf("negC", [128, 1])
    tinyA = sbuf("tinyA", [128, 1])
    tinyB = sbuf("tinyB", [128, 1])

    PSA = es.enter_context(nc.psum_tensor("psA", [128, 2048], F32))
    PSB = es.enter_context(nc.psum_tensor("psB", [128, 2048], F32))
    PS = [None, None, None, None] + [PSB[:, i * 512:(i + 1) * 512] for i in range(4)]
    S2 = [PSA[:, 0:1024], PSA[:, 1024:2048]]
    S4 = [PSA[:, j * 512:(j + 1) * 512] for j in range(4)]

    for nm, t in (("bq", bq_s), ("bk", bk_s), ("bv", bv_s)):
        p.op("gpsimd", lambda e, nm=nm, t=t: e.dma_start(
            out=t[:], in_=b_in[nm].rearrange("(h e) -> e h", e=128),
            allow_slow_non_contiguous=True), inc="bia", k=16)
    w_bias = ("bia", p.n("bia"))

    ph_y1 = contextlib.ExitStack()
    y1_fm = sbuf("y1_fm", [128, DC * S], BF16, ph_y1)
    ph1 = contextlib.ExitStack()
    xt = [sbuf(f"xt{i}", [128, D], F32, ph1) for i in range(2)]
    y1t = [sbuf(f"y1t{i}", [128, D], BF16, ph1) for i in range(2)]
    stats = sbuf("stats", [128, 24], F32, ph1)
    mv = sbuf("mv", [128, 2], F32, ph1)
    rs_t = sbuf("rs_t", [128, 1], F32, ph1)
    nb_t = sbuf("nb_t", [128, 1], F32, ph1)
    wch = [sbuf(f"wch{i}", [128, I], F32, ph1) for i in range(2)]
    strip_f = sbuf("strip_f", [128, 896], F32, ph1)

    g_strip = p.op("gpsimd", lambda e: e.iota(
        strip_f[:], pattern=[[1, 896]], base=-384, channel_multiplier=-1,
        allow_small_or_imprecise_dtypes=True), inc=SG)

    p.wait("vector", w_bias)
    p.wait("vector", SG, g_strip)

    def vconst(e):
        e.memset(mu[:], 0.75)
        e.memset(eps_t[:], EPS)
        e.memset(e35_t[:], 1.0e-35)
        e.memset(c60_t[:], -60.0)
        e.memset(m64b[:], M64)
        e.memset(nm64b[:], -M64)
        e.memset(ones_f[:], 1.0)
        e.memset(ones_bf[:], 1.0)
        e.drain()
        e.tensor_scalar(out=mu[:], in0=mu[:], scalar1=M64, scalar2=M64,
                        op0=ALU.add, op1=ALU.subtract)
        e.drain()
        e.tensor_scalar(out=mu[:], in0=mu[:], scalar1=-1.0, scalar2=1.0,
                        op0=ALU.mult, op1=ALU.add)
        e.drain()
        e.tensor_scalar(out=sc_qk[:], in0=mu[:], scalar1=-SCL_C, scalar2=SCL + SCL_C,
                        op0=ALU.mult, op1=ALU.add)
        e.tensor_scalar(out=strip[:], in0=strip_f[:], scalar1=0.0, scalar2=None,
                        op0=ALU.is_ge)
        e.drain()
        e.tensor_scalar(out=bq_s[:], in0=bq_s[:], scalar1=sc_qk[:], scalar2=None,
                        op0=ALU.mult)
        e.tensor_scalar(out=bk_s[:], in0=bk_s[:], scalar1=sc_qk[:], scalar2=None,
                        op0=ALU.mult)
        e.drain()
    p.op("vector", vconst)
    v_const = p.sig("vector", SV)

    x_ld, a_apply, tr_done, sp_done = [], [], [], []
    for i in range(NT):
        if i >= 2:
            p.wait("sync", SA, a_apply[i - 2])
            p.wait("sync", sp_done[i - 2])
        x_ld.append(p.dma("sync", lambda e, i=i: e.dma_start(
            out=xt[i % 2][:], in_=x_in[i * 128:(i + 1) * 128, :]), f"xl{i % 2}"))

        p.wait("vector", x_ld[i])

        def vstats(e, i=i):
            for gi in range(4):
                e.bn_stats(out=stats[:, gi * 6:(gi + 1) * 6],
                           in_=xt[i % 2][:, gi * 512:(gi + 1) * 512])
            e.drain()
            e.bn_aggr(out=mv[:], in_=stats[:])
            e.drain()
        p.op("vector", vstats)
        v_st = p.sig("vector", SV)

        p.wait("scalar", SV, v_st)
        if i >= 2:
            p.wait("scalar", tr_done[i - 2])

        def sapply(e):
            e.activation(out=rs_t[:], in_=mv[:, 1:2], func=ACT.Ln, bias=eps_t[:], scale=1.0)
            e.drain()
            e.activation(out=rs_t[:], in_=rs_t[:], func=ACT.Exp, bias=0.0, scale=-0.5)
            e.drain()
            e.activation(out=nb_t[:], in_=mv[:, 0:1], func=ACT.Identity, bias=0.0,
                         scale=rs_t[:])
            e.drain()
            e.activation(out=nb_t[:], in_=nb_t[:], func=ACT.Identity, bias=0.0, scale=-1.0)
            e.drain()
        p.op("scalar", sapply)
        a_apply.append(p.op("scalar", lambda e, i=i: e.activation(
            out=y1t[i % 2][:], in_=xt[i % 2][:], func=ACT.Identity,
            bias=nb_t[:], scale=rs_t[:]), inc=SA))

        p.wait("sync", SA, a_apply[i])
        t = p.dma("sync", lambda e, i=i: e.dma_start_transpose(
            out=y1_fm[:].rearrange("p (c s) -> p c s", s=S)[:, :, i * 128:(i + 1) * 128],
            in_=y1t[i % 2][:].rearrange("p (c f) -> p c f", f=128)), f"p1tr{i % 2}")
        tr_done.append(t)

        p.wait("gpsimd", SA, a_apply[i])
        sp_done.append(p.dma("gpsimd", lambda e, i=i: e.dma_start(
            out=x_sp[i * 128:(i + 1) * 128, :], in_=xt[i % 2][:]), f"xsp{i % 2}"))

    NWCH = DC + IC
    wst_ld, wst_v = [], []
    for j in range(NWCH):
        if j >= 2:
            p.wait("gpsimd", SV, wst_v[j - 2])
        if j < DC:
            wst_ld.append(p.dma("gpsimd", lambda e, j=j: e.dma_start(
                out=wch[j % 2][:], in_=wupt[j * 128:(j + 1) * 128, :]), f"ws{j % 2}"))
        else:
            wst_ld.append(p.dma("gpsimd", lambda e, c=j - DC, j=j: e.dma_start(
                out=wch[j % 2][:, 0:D], in_=wdnt[c * 128:(c + 1) * 128, :]), f"ws{j % 2}"))
        p.wait("vector", wst_ld[j])

        def vred(e, j=j):
            if j < DC:
                for gi in range(8):
                    e.tensor_reduce(out=wst_up[:, j * 8 + gi: j * 8 + gi + 1],
                                    in_=wch[j % 2][:, gi * 512:(gi + 1) * 512],
                                    axis=AX, op=ALU.add, apply_absolute_value=True)
            else:
                c = j - DC
                for gi in range(4):
                    e.tensor_reduce(out=wst_dn[:, c * 4 + gi: c * 4 + gi + 1],
                                    in_=wch[j % 2][:, gi * 512:(gi + 1) * 512],
                                    axis=AX, op=ALU.add, apply_absolute_value=True)
        p.op("vector", vred)
        wst_v.append(p.sig("vector", SV))

    def vtot(e):
        e.drain()
        e.tensor_reduce(out=smallA[:, 0:1], in_=wst_up[:], axis=AX, op=ALU.add)
        e.tensor_reduce(out=smallA[:, 1:2], in_=wst_dn[:], axis=AX, op=ALU.add)
        e.memset(smallA[:, 2:4], 0.0)
        e.drain()
    p.op("vector", vtot)
    v_tot = p.sig("vector", SV)
    p.wait("tensor", SV, v_tot)
    t_bc1 = p.op("tensor", lambda e: e.matmul(
        PS[5][0:1, 0:4], ones_f[:, 0:1], smallA[:, 0:4], start=True, stop=True), inc=ST)
    p.wait("vector", ST, t_bc1)
    p.op("vector", lambda e: e.tensor_copy(out=smallA[0:1, 0:4], in_=PS[5][0:1, 0:4]))
    v_bc1 = p.sig("vector", SV)
    p.wait("tensor", SV, v_bc1)
    t_bc = p.op("tensor", lambda e: e.matmul(
        PS[4][:, 0:4], ones_f[0:1, :], smallA[0:1, 0:4], start=True, stop=True), inc=ST)
    p.wait("vector", ST, t_bc)

    def vth(e):
        e.tensor_scalar(out=s_upt[:], in0=PS[4][:, 0:1], scalar1=INV_WN, scalar2=EPS,
                        op0=ALU.mult, op1=ALU.max)
        e.tensor_scalar(out=s_dnt[:], in0=PS[4][:, 1:2], scalar1=INV_WN, scalar2=EPS,
                        op0=ALU.mult, op1=ALU.max)
        e.drain()
        e.tensor_scalar(out=th_up[:], in0=s_upt[:], scalar1=0.5, scalar2=None, op0=ALU.mult)
        e.tensor_scalar(out=th_dn[:], in0=s_dnt[:], scalar1=0.5, scalar2=None, op0=ALU.mult)
        e.drain()
        e.tensor_scalar(out=nth_up[:], in0=th_up[:], scalar1=-1.0, scalar2=None, op0=ALU.mult)
        e.tensor_scalar(out=nth_dn[:], in0=th_dn[:], scalar1=-1.0, scalar2=None, op0=ALU.mult)
        e.drain()
    p.op("vector", vth)
    v_th = p.sig("vector", SV)

    ph1.close()
    p.full_barrier()

    for c in range(DC):
        p.dma("gpsimd", lambda e, c=c: e.dma_start(
            out=wo_sp[c * 128:(c + 1) * 128, :],
            in_=wot[c * 128:(c + 1) * 128, :]), "woc")

    ph2 = contextlib.ExitStack()
    kg = sbuf("kg", [128, 4 * S], BF16, ph2)
    vg = sbuf("vg", [128, NT * 512], BF16, ph2)
    wsl2 = [sbuf(f"wsl{i}", [128, DC * 512], BF16, ph2) for i in range(2)]
    qh_t = [sbuf(f"qh{i}", [128, 4 * 512], BF16, ph2) for i in range(2)]
    sq2_t = sbuf("sq2_t", [128, 512], F32, ph2)
    p_t = [sbuf(f"pt{i}", [128, 1024], BF16, ph2) for i in range(2)]
    rcp_t = [sbuf(f"rcp{i}", [128, 512], F32, ph2) for i in range(2)]
    sq_t = sbuf("sq_t", [128, 512], F32, ph2)
    ot_b = [sbuf(f"otb{i}", [128, 512], BF16, ph2) for i in range(2)]
    bv_b = sbuf("bv_b", [128, 512], F32, ph2)
    bv_row = sbuf("bv_row", [1, 512], F32, ph2)
    wqch = [sbuf(f"wqch{i}", [128, D], F32, ph2) for i in range(2)]
    wq_p = sbuf("wq_p", [128, D], F32, ph2)
    wq_n = sbuf("wq_n", [128, D], F32, ph2)
    tq_bf = sbuf("tq_bf", [128, D], BF16, ph2)

    wq_units = ([("up", c, hf) for c in range(DC) for hf in range(2)]
                + [("dn", c, 0) for c in range(IC)])
    wq_ld, wq_vdone, wq_spdone = {}, {}, {}
    wq_state = {"ld": 0, "q": 0}

    def wq_emit_load(n=1):
        for _ in range(n):
            u = wq_state["ld"]
            if u >= len(wq_units) or u >= wq_state["q"] + 2:
                return
            kind, c, hf = wq_units[u]
            if u >= 2:
                p.wait("gpsimd", SV, wq_vdone[u - 2])
            if kind == "up":
                wq_ld[u] = p.dma("gpsimd", lambda e, c=c, hf=hf, u=u: e.dma_start(
                    out=wqch[u % 2][:],
                    in_=wupt[c * 128:(c + 1) * 128, hf * 2048:(hf + 1) * 2048]),
                    f"wqs{u % 2}")
            else:
                wq_ld[u] = p.dma("gpsimd", lambda e, c=c, u=u: e.dma_start(
                    out=wqch[u % 2][:], in_=wdnt[c * 128:(c + 1) * 128, :]), f"wqs{u % 2}")
            wq_state["ld"] += 1

    def wq_emit_quant(n=1):
        for _ in range(n):
            u = wq_state["q"]
            if u >= len(wq_units) or u >= wq_state["ld"]:
                return
            kind, c, hf = wq_units[u]
            th, nth = (th_up, nth_up) if kind == "up" else (th_dn, nth_dn)
            p.wait("vector", SV, v_th)
            p.wait("vector", wq_ld[u])
            if u >= 1:
                p.wait("vector", wq_spdone[u - 1])

            def vq(e, u=u, th=th, nth=nth):
                e.tensor_scalar(out=wq_p[:], in0=wqch[u % 2][:], scalar1=th[:],
                                scalar2=None, op0=ALU.is_ge)
                e.tensor_scalar(out=wq_n[:], in0=wqch[u % 2][:], scalar1=nth[:],
                                scalar2=None, op0=ALU.is_le)
                e.drain()
                e.tensor_tensor(out=tq_bf[:], in0=wq_p[:], in1=wq_n[:], op=ALU.subtract)
            p.op("vector", vq)
            wq_vdone[u] = p.sig("vector", SV)
            p.wait("gpsimd", SV, wq_vdone[u])
            last = None
            for j in range(4):
                if kind == "up":
                    last = p.dma("gpsimd", lambda e, c=c, hf=hf, j=j: e.dma_start(
                        out=tup_sp[hf * 4 + j, c], in_=tq_bf[:, j * 512:(j + 1) * 512]),
                        f"stq{u % 2}")
                else:
                    last = p.dma("gpsimd", lambda e, c=c, j=j: e.dma_start(
                        out=tdn_sp[j, c], in_=tq_bf[:, j * 512:(j + 1) * 512]),
                        f"stq{u % 2}")
            wq_spdone[u] = last
            wq_state["q"] += 1

    ev_k, ev_v = {}, {}
    o_evac, o_spw, q_evac, a_rec = [], [], [], []
    exp_done, mask_done, pv_mm = {}, {}, {}
    t_sqr = [None]
    last_qev = [None]
    last_qproj = [None]
    lk_pf = {}
    q_proj_last = {}
    hq = 0
    for grp in range(NG):
        p.wait("tensor", SA, p.n(SA))
        p.wait("tensor", SV, p.n(SV))
        wsl = wsl2[grp % 2]
        if grp == 0:
            lk = None
            for c in range(DC):
                lk = p.dma("gpsimd", lambda e, c=c, grp=grp, wsl=wsl: e.dma_start(
                    out=wsl[:, c * 512:(c + 1) * 512],
                    in_=wkt[c * 128:(c + 1) * 128, grp * 512:(grp + 1) * 512]), "kvw")
        else:
            lk = lk_pf[grp]
        p.wait("tensor", lk)
        for comb in range(16):
            hl, r = divmod(comb, NQB)
            h = grp * 4 + hl
            par = 4 + comb % 2
            if comb >= 2:
                p.wait("tensor", SA, ev_k[(grp, comb - 2)])
            p.wait("tensor", tr_done[4 * r + 3])
            t_mm = None
            for c in range(DC):
                t_mm = p.op("tensor", lambda e, par=par, c=c, hl=hl, r=r, wsl=wsl: e.matmul(
                    PS[par][:, :], wsl[:, c * 512 + hl * 128: c * 512 + (hl + 1) * 128],
                    y1_fm[:, c * S + r * 512: c * S + (r + 1) * 512],
                    start=(c == 0), stop=(c == DC - 1)),
                    inc=ST if c == DC - 1 else None)
            p.wait("scalar", ST, t_mm)
            ev_k[(grp, comb)] = p.op("scalar", lambda e, par=par, hl=hl, h=h, r=r: e.activation(
                out=kg[:, hl * S + r * 512: hl * S + (r + 1) * 512],
                in_=PS[par][:, :], func=ACT.Identity,
                bias=bk_s[:, h:h + 1], scale=sc_qk[:]), inc=SA)
            k_last = t_mm

        lv = None
        p.wait("gpsimd", ST, k_last)
        for c in range(DC):
            lv = p.dma("gpsimd", lambda e, c=c, grp=grp, wsl=wsl: e.dma_start(
                out=wsl[:, c * 512:(c + 1) * 512],
                in_=wvt[c * 128:(c + 1) * 128, grp * 512:(grp + 1) * 512]), "kvw")
        bvr_ld = p.dma("gpsimd", lambda e, grp=grp: e.dma_start(
            out=bv_row[:], in_=b_in["bv"].rearrange("(o d) -> o d", o=4)[grp:grp + 1, :]),
            "bvr")
        p.wait("tensor", SV, v_const)
        p.wait("tensor", bvr_ld)
        tb = p.op("tensor", lambda e: e.matmul(
            PS[7][:, :], ones_f[0:1, :], bv_row[0:1, :], start=True, stop=True), inc=ST)
        p.wait("vector", ST, tb)
        p.op("vector", lambda e: e.tensor_copy(out=bv_b[:], in_=PS[7][:, :]))
        v_bvb = p.sig("vector", SV)

        p.wait("tensor", lv)
        for t in range(NT):
            par = 4 + t % 2
            if t < 2:
                p.wait("tensor", SA, ev_k[(grp, 14 + t)])
            else:
                p.wait("tensor", SV, ev_v[(grp, t - 2)])
            t_mm = None
            for c in range(DC):
                t_mm = p.op("tensor", lambda e, par=par, c=c, t=t, wsl=wsl: e.matmul(
                    PS[par][:, :], y1_fm[:, c * S + t * 128: c * S + (t + 1) * 128],
                    wsl[:, c * 512:(c + 1) * 512],
                    start=(c == 0), stop=(c == DC - 1)),
                    inc=ST if c == DC - 1 else None)
            v_last = t_mm
            p.wait("vector", ST, t_mm)
            p.wait("vector", SV, v_bvb)
            p.op("vector", lambda e, par=par, t=t: e.tensor_tensor(
                out=vg[:, t * 512:(t + 1) * 512], in0=PS[par][:, :], in1=bv_b[:],
                op=ALU.add))
            ev_v[(grp, t)] = p.sig("vector", SV)

        vnk_prev = None
        for hl in range(4):
            h = grp * 4 + hl

            def vnkinit(e, h=h):
                e.memset(nk2[:, h:h + 1], 0.0)
                e.drain()
            p.op("vector", vnkinit)
            for r in range(NQB):
                p.wait("vector", SA, ev_k[(grp, hl * NQB + r)])
                if t_sqr[0] is not None:
                    p.wait("vector", ST, t_sqr[0])
                p.op("vector", lambda e, hl=hl, r=r: e.tensor_tensor(
                    out=sq_t[:], in0=kg[:, hl * S + r * 512: hl * S + (r + 1) * 512],
                    in1=kg[:, hl * S + r * 512: hl * S + (r + 1) * 512], op=ALU.mult))
                v_sq = p.sig("vector", SV)
                p.wait("tensor", SV, v_sq)
                if vnk_prev is not None:
                    p.wait("tensor", SV, vnk_prev)
                t_nk = p.op("tensor", lambda e: e.matmul(
                    PS[6][:, :], ones_f[:, :], sq_t[:], start=True, stop=True), inc=ST)
                t_sqr[0] = t_nk
                p.wait("vector", ST, t_nk)

                def vnkacc(e, h=h):
                    e.tensor_reduce(out=tinyA[:], in_=PS[6][:, :], axis=AX, op=ALU.max)
                    e.drain()
                    e.tensor_tensor(out=nk2[:, h:h + 1], in0=nk2[:, h:h + 1],
                                    in1=tinyA[:], op=ALU.max)
                    e.drain()
                p.op("vector", vnkacc)
                vnk_prev = p.sig("vector", SV)

        p.wait("gpsimd", ST, v_last)
        lq = None
        for c in range(DC):
            lq = p.dma("gpsimd", lambda e, c=c, grp=grp, wsl=wsl: e.dma_start(
                out=wsl[:, c * 512:(c + 1) * 512],
                in_=wqt[c * 128:(c + 1) * 128, grp * 512:(grp + 1) * 512]), "kvw")
        if grp + 1 < NG:
            if grp >= 1:
                p.wait("gpsimd", ST, q_proj_last[grp - 1])
            t_pf = None
            for c in range(DC):
                t_pf = p.dma("gpsimd", lambda e, c=c, grp=grp: e.dma_start(
                    out=wsl2[(grp + 1) % 2][:, c * 512:(c + 1) * 512],
                    in_=wkt[c * 128:(c + 1) * 128, (grp + 1) * 512:(grp + 2) * 512]),
                    "kvwP")
            lk_pf[grp + 1] = t_pf

        for _ in range(3):
            wq_emit_load(1)
            wq_emit_quant(1)

        p.wait("tensor", SV, p.n(SV))
        p.wait("tensor", SA, p.n(SA))
        p.wait("tensor", lq)
        for hl in range(4):
            h = grp * 4 + hl
            hh = grp * 4 + hl
            hpar = hh % 2
            qev = []
            for r4 in range(4):
                if r4 > 0:
                    p.wait("tensor", SV, qev[r4 - 1])
                elif last_qev[0] is not None:
                    p.wait("tensor", SV, last_qev[0])
                if hh >= 2 and r4 == 0:
                    last_hq = (hh - 2) * NQB + NQB - 1
                    ns2 = 2 * NQB
                    p.wait("tensor", ST, pv_mm[(last_hq, ns2 - 1)])
                t_qmm = None
                for c in range(DC):
                    t_qmm = p.op("tensor", lambda e, c=c, hl=hl, r4=r4, wsl=wsl: e.matmul(
                        PS[4][:, :],
                        wsl[:, c * 512 + hl * 128: c * 512 + (hl + 1) * 128],
                        y1_fm[:, c * S + r4 * 512: c * S + (r4 + 1) * 512],
                        start=(c == 0), stop=(c == DC - 1)),
                        inc=ST if c == DC - 1 else None)
                p.wait("vector", ST, t_qmm)
                if t_sqr[0] is not None:
                    p.wait("vector", ST, t_sqr[0])

                def vqev(e, hpar=hpar, h=h, r4=r4):
                    e.tensor_scalar(out=qh_t[hpar][:, r4 * 512:(r4 + 1) * 512],
                                    in0=PS[4][:, :], scalar1=sc_qk[:],
                                    scalar2=bq_s[:, h:h + 1], op0=ALU.mult, op1=ALU.add)
                    e.drain()
                    e.tensor_tensor(out=(sq_t if r4 % 2 == 0 else sq2_t)[:],
                                    in0=qh_t[hpar][:, r4 * 512:(r4 + 1) * 512],
                                    in1=qh_t[hpar][:, r4 * 512:(r4 + 1) * 512],
                                    op=ALU.mult)
                p.op("vector", vqev)
                qev.append(p.sig("vector", SV))
                p.wait("tensor", SV, qev[r4])
                if r4 == 0 and hh >= 1:
                    p.wait("tensor", SA, a_rec[hh * NQB - 1])
                if r4 == 0 and hh >= 2:
                    p.wait("tensor", SV, o_evac[hh * NQB - 2])
                t_nq = p.op("tensor", lambda e, r4=r4: e.matmul(
                    PS[5][:, :], ones_f[:, :], (sq_t if r4 % 2 == 0 else sq2_t)[:],
                    start=(r4 == 0), stop=(r4 == 3)), inc=ST)
                t_sqr[0] = t_nq
                last_qproj[0] = t_qmm
            last_qev[0] = qev[3]
            p.wait("vector", ST, t_nq)

            def vcap(e, h=h):
                e.tensor_reduce(out=tinyA[:], in_=PS[5][:, :], axis=AX, op=ALU.max)
                e.drain()
                e.tensor_tensor(out=tinyB[:], in0=tinyA[:], in1=nk2[:, h:h + 1],
                                op=ALU.mult)
                e.drain()
            p.op("vector", vcap)
            cap_v = p.sig("vector", SV)
            p.wait("scalar", SV, cap_v)

            def scap(e):
                e.activation(out=negC[:], in_=tinyB[:], func=ACT.Ln, bias=0.0, scale=1.0)
                e.drain()
                e.activation(out=negC[:], in_=negC[:], func=ACT.Exp, bias=0.0, scale=0.5)
                e.drain()
                e.activation(out=negC[:], in_=negC[:], func=ACT.Relu, bias=c60_t[:],
                             scale=1.0)
                e.drain()
                e.activation(out=negC[:], in_=negC[:], func=ACT.Identity, bias=0.0,
                             scale=-1.0)
                e.drain()
            p.op("scalar", scap)

            def p4(j):
                return p_t[j // 2][:, (j % 2) * 512:(j % 2 + 1) * 512]

            for qb in range(NQB):
                nk = 4 * (qb + 1)
                qpar = hq % 2
                nk_prev = 4 * ((hq - 1) % NQB + 1) if hq >= 1 else 0
                qsl = qh_t[hpar][:, qb * 512:(qb + 1) * 512]
                for k in range(nk):
                    sl = k % 4
                    if k >= 4:
                        p.wait("tensor", SA, exp_done[(hq, k - 4)])
                    elif hq >= 1:
                        p.wait("tensor", SA, exp_done[(hq - 1, nk_prev - 4 + k)])
                    if k == 0 and hq >= 1:
                        p.wait("tensor", SA, a_rec[hq - 1])
                    if k == 0 and hq >= 2:
                        p.wait("tensor", SV, o_evac[hq - 2])
                    t_smm = p.op("tensor", lambda e, sl=sl, hl=hl, k=k, qsl=qsl: e.matmul(
                        S4[sl][:, :],
                        kg[:, hl * S + k * 128: hl * S + (k + 1) * 128],
                        qsl, start=True, stop=True), inc=ST)
                    p.wait("scalar", ST, t_smm)
                    if k >= 4:
                        p.wait("scalar", ST, pv_mm[(hq, k - 4)])
                    elif hq >= 1 and (hq - 1, nk_prev - 4 + k) in pv_mm:
                        p.wait("scalar", ST, pv_mm[(hq - 1, nk_prev - 4 + k)])
                    exp_done[(hq, k)] = p.op("scalar", lambda e, sl=sl: e.activation(
                        out=p4(sl), in_=S4[sl][:, :], func=ACT.Exp,
                        bias=negC[:], scale=1.0), inc=SA)
                    diag = k >= nk - 4
                    if diag:
                        m = k - 4 * qb
                        p.wait("vector", SA, exp_done[(hq, k)])
                        p.op("vector", lambda e, sl=sl, m=m: e.tensor_tensor(
                            out=p4(sl), in0=p4(sl),
                            in1=strip[:, 384 - 128 * m: 896 - 128 * m],
                            op=ALU.mult))
                        mask_done[(hq, k)] = p.sig("vector", SV)
                    p.wait("tensor", SA, exp_done[(hq, k)])
                    if diag:
                        p.wait("tensor", SV, mask_done[(hq, k)])
                    if k == 0 and qb == 0:
                        p.wait("tensor", SV, cap_v)
                    p.op("tensor", lambda e, sl=sl, k=k, nk=nk: e.matmul(
                        PS[5][:, :], ones_bf[:, :], p4(sl),
                        start=(k == 0), stop=(k == nk - 1)))
                    pv_mm[(hq, k)] = p.op("tensor",
                        lambda e, sl=sl, k=k, qpar=qpar, hl=hl, nk=nk: e.matmul(
                            PS[6 + qpar][:, :],
                            vg[:, k * 512 + hl * 128: k * 512 + (hl + 1) * 128],
                            p4(sl), start=(k == 0), stop=(k == nk - 1)),
                        inc=ST)

                p.wait("scalar", ST, pv_mm[(hq, nk - 1)])
                if hq >= 2:
                    p.wait("scalar", SV, o_evac[hq - 2])

                def srec(e, qpar=qpar):
                    e.activation(out=rcp_t[qpar][:], in_=PS[5][:, :], func=ACT.Ln,
                                 bias=e35_t[:], scale=1.0)
                    e.drain()
                    e.activation(out=rcp_t[qpar][:], in_=rcp_t[qpar][:], func=ACT.Exp,
                                 bias=0.0, scale=-1.0)
                    e.drain()
                p.op("scalar", srec)
                a_rec.append(p.sig("scalar", SA))
                p.wait("vector", SA, a_rec[hq])
                if hq >= 2:
                    p.wait("vector", o_spw[hq - 2])
                p.op("vector", lambda e, qpar=qpar: e.tensor_tensor(
                    out=ot_b[qpar][:], in0=PS[6 + qpar][:, :], in1=rcp_t[qpar][:],
                    op=ALU.mult))
                o_evac.append(p.sig("vector", SV))
                p.wait("gpsimd", SV, o_evac[hq])
                o_spw.append(p.dma("gpsimd", lambda e, h=h, qb=qb, qpar=qpar: e.dma_start(
                    out=o_sp[h, :, qb * 512:(qb + 1) * 512], in_=ot_b[qpar][:]),
                    f"osp{hq % 2}"))
                hq += 1
            for _ in range(3):
                wq_emit_load(1)
                wq_emit_quant(1)
        q_proj_last[grp] = last_qproj[0]

    while wq_state["q"] < len(wq_units):
        wq_emit_load(1)
        wq_emit_quant(1)
    o_all_sp = o_spw[-1]

    o_full = y1_fm
    o_rld = []
    for hh in range(H):
        p.wait("sync", o_spw[hh * NQB + NQB - 1])
        p.wait("sync", o_spw[hh * NQB + NQB - 2])
        p.wait("sync", ST, last_qproj[0])
        if hh >= 8:
            p.wait("sync", o_rld[hh - 8])
        o_rld.append(p.dma("sync", lambda e, hh=hh: e.dma_start(
            out=o_full[:, hh * S:(hh + 1) * S], in_=o_sp[hh]), f"orl{hh % 8}"))

    ph2.close()
    p.full_barrier(exclude=("orl",))

    ph3 = contextlib.ExitStack()
    wo_bf = sbuf("wo_bf", [128, DC * D], BF16, ph3)
    at_f = [sbuf(f"at_f{i}", [128, D], F32, ph3) for i in range(2)]
    xr_f = [sbuf(f"xr{i}", [128, D], BF16, ph3) for i in range(2)]
    x2t = [sbuf(f"x2t{i}", [128, D], BF16, ph3) for i in range(2)]
    bo_b = sbuf("bo_b", [128, D], F32, ph3)
    bo_1row = sbuf("bo_1row", [1, D], F32, ph3)
    stats3 = sbuf("stats3", [128, 24], F32, ph3)
    mv3 = sbuf("mv3", [128, 2], F32, ph3)
    rs3 = sbuf("rs3", [128, 1], F32, ph3)
    nb3 = sbuf("nb3", [128, 1], F32, ph3)

    bo_ld = p.dma("sync", lambda e: e.dma_start(
        out=bo_1row[:], in_=b_in["bo"].rearrange("(o d) -> o d", o=1)), "bol")
    wo_ld = []
    for c in range(DC):
        wo_ld.append(p.dma("sync", lambda e, c=c: e.dma_start(
            out=wo_bf[:, c * D:(c + 1) * D], in_=wo_sp[c * 128:(c + 1) * 128, :]),
            f"wol{c}"))
    p.wait("tensor", SA, p.n(SA))
    p.wait("tensor", SV, p.n(SV))
    p.wait("tensor", bo_ld)
    v_bob = None
    for rnd in range(2):
        t_bo2 = []
        for rr2 in range(2):
            rr = rnd * 2 + rr2
            t_bo2.append(p.op("tensor", lambda e, rr=rr, rr2=rr2: e.matmul(
                PS[6 + rr2][:, :], ones_f[0:1, :], bo_1row[0:1, rr * 512:(rr + 1) * 512],
                start=True, stop=True), inc=ST))
        for rr2 in range(2):
            rr = rnd * 2 + rr2
            p.wait("vector", ST, t_bo2[rr2])
            p.op("vector", lambda e, rr=rr, rr2=rr2: e.tensor_scalar(
                out=bo_b[:, rr * 512:(rr + 1) * 512], in0=PS[6 + rr2][:, :], scalar1=0.0,
                scalar2=None, op0=ALU.add))
        v_bob = p.sig("vector", SV)
        if rnd == 0:
            p.wait("tensor", SV, v_bob)
    v_bob = v_bob

    x2_spw, at_ev, vx2_t = [], {}, []
    for i in range(NT):
        if i >= 2:
            p.wait("gpsimd", SV, vx2_t[i - 2])
        xr_ld = p.dma("gpsimd", lambda e, i=i: e.dma_start(
            out=xr_f[i % 2][:], in_=x_sp[i * 128:(i + 1) * 128, :]), f"xrl{i % 2}")
        p.wait("tensor", SV, v_bob)
        for r in range(NQB):
            par = r % 2 + 4
            if i * NQB + r >= 2:
                p.wait("tensor", SV, at_ev[(i * NQB + r - 2)])
            t_mm = None
            for c in range(H):
                if i == 0 and r == 0:
                    p.wait("tensor", o_rld[c])
                    p.wait("tensor", wo_ld[c])
                t_mm = p.op("tensor", lambda e, par=par, c=c, i=i, r=r: e.matmul(
                    PS[par][:, :], o_full[:, c * S + i * 128: c * S + (i + 1) * 128],
                    wo_bf[:, c * D + r * 512: c * D + (r + 1) * 512],
                    start=(c == 0), stop=(c == H - 1)),
                    inc=ST if c == H - 1 else None)
            p.wait("vector", ST, t_mm)
            p.op("vector", lambda e, par=par, i=i, r=r: e.tensor_tensor(
                out=at_f[i % 2][:, r * 512:(r + 1) * 512], in0=PS[par][:, :],
                in1=bo_b[:, r * 512:(r + 1) * 512], op=ALU.add))
            at_ev[i * NQB + r] = p.sig("vector", SV)

        def vstats3(e, i=i):
            for gi in range(4):
                e.bn_stats(out=stats3[:, gi * 6:(gi + 1) * 6],
                           in_=at_f[i % 2][:, gi * 512:(gi + 1) * 512])
            e.drain()
            e.bn_aggr(out=mv3[:], in_=stats3[:])
            e.drain()
        p.op("vector", vstats3)
        v_st3 = p.sig("vector", SV)
        p.wait("scalar", SV, v_st3)

        def srs3(e):
            e.activation(out=rs3[:], in_=mv3[:, 1:2], func=ACT.Ln, bias=eps_t[:], scale=1.0)
            e.drain()
            e.activation(out=rs3[:], in_=rs3[:], func=ACT.Exp, bias=0.0, scale=-0.5)
            e.drain()
            e.activation(out=nb3[:], in_=mv3[:, 0:1], func=ACT.Identity, bias=0.0, scale=rs3[:])
            e.drain()
            e.activation(out=nb3[:], in_=nb3[:], func=ACT.Identity, bias=0.0, scale=-1.0)
            e.drain()
        p.op("scalar", srs3)
        a_rs3 = p.sig("scalar", SA)
        p.wait("vector", SA, a_rs3)
        p.wait("vector", xr_ld)
        if i >= 2:
            p.wait("vector", x2_spw[i - 2])

        def vx2(e, i=i):
            e.tensor_scalar(out=at_f[i % 2][:], in0=at_f[i % 2][:], scalar1=rs3[:],
                            scalar2=nb3[:], op0=ALU.mult, op1=ALU.add)
            e.drain()
            e.tensor_tensor(out=x2t[i % 2][:], in0=at_f[i % 2][:], in1=xr_f[i % 2][:],
                            op=ALU.add)
            e.drain()
        p.op("vector", vx2)
        vx2_t.append(p.sig("vector", SV))
        p.wait("gpsimd", SV, vx2_t[i])
        x2_spw.append(p.dma("gpsimd", lambda e, i=i: e.dma_start(
            out=x2_sp[i * 128:(i + 1) * 128, :], in_=x2t[i % 2][:]), f"x2s{i % 2}"))

    x2_all = x2_spw[-1]
    ph3.close()
    ph_y1.close()
    p.full_barrier()

    ph4 = contextlib.ExitStack()
    x2q = sbuf("x2q", [128, 4 * D], F32, ph4)
    qfm_ar = sbuf("qfm_ar", [128, IC * 512], BF16, ph4)
    q1fm = qfm_ar
    q2fm = qfm_ar
    vtmp = sbuf("vtmp", [128, D], F32, ph4)
    vtmp2 = sbuf("vtmp2", [128, D], F32, ph4)
    qtile = sbuf("qtile", [128, I], BF16, ph4)
    stats4 = sbuf("stats4", [128, 24], F32, ph4)
    mv4 = sbuf("mv4", [128, 2], F32, ph4)
    rs4 = sbuf("rs4", [128, 4], F32, ph4)
    sc4 = sbuf("sc4", [128, 4], F32, ph4)
    nb4 = sbuf("nb4", [128, 4], F32, ph4)
    m1c = sbuf("m1c", [128, 4], F32, ph4)
    a2f = sbuf("a2f", [128, 4], F32, ph4)
    m2a = sbuf("m2a", [128, 4], F32, ph4)
    f2 = sbuf("f2", [128, 4], F32, ph4)
    s2e = sbuf("s2e", [128, 1], F32, ph4)
    wsl_f = sbuf("wsl_f", [128, 3 * DC * 512], BF16, ph4)
    h2q = sbuf("h2q", [128, 4 * I], F32, ph4)
    out2q = h2q

    p.wait("tensor", SV, p.n(SV))
    p.wait("tensor", SA, p.n(SA))

    st_done = []
    last_q2tr = None
    tup_pre = None
    for qq in range(4):
        if qq == 0:
            p.wait("gpsimd", "x2s0", p.n("x2s0"))
            p.wait("gpsimd", "x2s1", p.n("x2s1"))
        else:
            for slot in range(4):
                p.wait("gpsimd", f"out{slot}", p.n(f"out{slot}"))
        x2_ld = []
        for t in range(4):
            x2_ld.append(p.dma("gpsimd", lambda e, qq=qq, t=t: e.dma_start(
                out=x2q[:, t * D:(t + 1) * D],
                in_=x2_sp[(qq * 4 + t) * 128:(qq * 4 + t + 1) * 128, :]), f"x2l{t}"))

        if qq == 0:
            p.wait("sync", "stq0", p.n("stq0"))
            p.wait("sync", "stq1", p.n("stq1"))
            tup_pre = []
            for j in range(3):
                lt = None
                for c in range(DC):
                    lt = p.dma("sync", lambda e, j=j, c=c: e.dma_start(
                        out=wsl_f[:, j * SLOT + c * 512:
                                  j * SLOT + (c + 1) * 512],
                        in_=tup_sp[j, c]), f"tup{j}")
                tup_pre.append(lt)

        q1_tr = []
        for t in range(4):
            p.wait("vector", x2_ld[t])
            if last_q2tr is not None and t == 0:
                p.wait("vector", last_q2tr)

            def vq1(e, t=t):
                xa = x2q[:, t * D:(t + 1) * D]
                for gi in range(4):
                    e.bn_stats(out=stats4[:, gi * 6:(gi + 1) * 6],
                               in_=xa[:, gi * 512:(gi + 1) * 512])
                e.drain()
                e.bn_aggr(out=mv4[:], in_=stats4[:])
                e.drain()
                e.tensor_reduce(out=m1c[:, t:t + 1], in_=xa, axis=AX, op=ALU.max)
                e.tensor_reduce(out=m2a[:, t:t + 1], in_=xa, axis=AX, op=ALU.min)
                e.drain()
            p.op("vector", vq1)
            v_q1s = p.sig("vector", SV)
            p.wait("scalar", SV, v_q1s)

            def sq1(e, t=t):
                e.activation(out=rs4[:, t:t + 1], in_=mv4[:, 1:2], func=ACT.Ln,
                             bias=eps_t[:], scale=1.0)
                e.drain()
                e.activation(out=rs4[:, t:t + 1], in_=rs4[:, t:t + 1], func=ACT.Exp,
                             bias=0.0, scale=-0.5)
                e.drain()
            p.op("scalar", sq1)
            a_q1 = p.sig("scalar", SA)
            p.wait("vector", SA, a_q1)
            if t > 0:
                p.wait("vector", q1_tr[t - 1])

            def vq1b(e, t=t):
                e.tensor_tensor(out=tinyA[:], in0=m1c[:, t:t + 1], in1=mv4[:, 0:1],
                                op=ALU.subtract)
                e.tensor_tensor(out=tinyB[:], in0=mv4[:, 0:1], in1=m2a[:, t:t + 1],
                                op=ALU.subtract)
                e.drain()
                e.tensor_tensor(out=tinyA[:], in0=tinyA[:], in1=tinyB[:], op=ALU.max)
                e.drain()
                e.tensor_scalar(out=tinyA[:], in0=tinyA[:], scalar1=rs4[:, t:t + 1],
                                scalar2=None, op0=ALU.mult)
                e.drain()
                e.tensor_scalar(out=m1c[:, t:t + 1], in0=tinyA[:], scalar1=EPS,
                                scalar2=None, op0=ALU.max)
                e.drain()
                e.reciprocal(out=tinyB[:], in_=m1c[:, t:t + 1])
                e.drain()
                e.tensor_scalar(out=tinyB[:], in0=tinyB[:], scalar1=127.0, scalar2=None,
                                op0=ALU.mult)
                e.drain()
                e.tensor_scalar(out=sc4[:, t:t + 1], in0=tinyB[:],
                                scalar1=rs4[:, t:t + 1], scalar2=None, op0=ALU.mult)
                e.drain()
                e.tensor_tensor(out=nb4[:, t:t + 1], in0=mv4[:, 0:1],
                                in1=sc4[:, t:t + 1], op=ALU.mult)
                e.drain()
                e.tensor_scalar(out=nb4[:, t:t + 1], in0=nb4[:, t:t + 1], scalar1=-1.0,
                                scalar2=None, op0=ALU.mult)
                e.drain()
                e.tensor_tensor(out=tinyB[:], in0=m1c[:, t:t + 1], in1=s_upt[:], op=ALU.mult)
                e.drain()
                e.tensor_scalar(out=tinyB[:], in0=tinyB[:], scalar1=INV127, scalar2=None,
                                op0=ALU.mult)
                e.drain()
                e.tensor_tensor(out=a2f[:, t:t + 1], in0=tinyB[:], in1=tinyB[:], op=ALU.mult)
                e.drain()
            p.op("vector", vq1b)
            v_ch1 = p.sig("vector", SV)
            p.wait("scalar", SV, v_ch1)

            def sq1b(e, t=t):
                xa = x2q[:, t * D:(t + 1) * D]
                e.activation(out=vtmp2[:, 0:D], in_=xa, func=ACT.Identity,
                             bias=nb4[:, t:t + 1], scale=sc4[:, t:t + 1])
                e.drain()
                e.activation(out=vtmp2[:, 0:D], in_=vtmp2[:, 0:D], func=ACT.Identity,
                             bias=m64b[:], scale=1.0)
                e.drain()
                e.activation(out=vtmp2[:, 0:D], in_=vtmp2[:, 0:D], func=ACT.Identity,
                             bias=nm64b[:], scale=1.0)
                e.drain()
            p.op("scalar", sq1b)
            a_q1b = p.sig("scalar", SA)

            def vq1r(e, t=t):
                xa = x2q[:, t * D:(t + 1) * D]
                e.tensor_scalar(out=vtmp[:, 0:D], in0=xa, scalar1=sc4[:, t:t + 1],
                                scalar2=nb4[:, t:t + 1], op0=ALU.mult, op1=ALU.add)
                e.drain()
                e.tensor_scalar(out=vtmp[:, 0:D], in0=vtmp[:, 0:D], scalar1=M32,
                                scalar2=M32, op0=ALU.add, op1=ALU.subtract)
                e.drain()
            p.op("vector", vq1r)
            p.wait("vector", SA, a_q1b)

            def vq1c(e, t=t):
                e.scalar_tensor_tensor(out=qtile[:, 0:D], in0=vtmp[:, 0:D], scalar=mu[:],
                                       in1=vtmp2[:, 0:D], op0=ALU.mult, op1=ALU.add)
                e.drain()
            p.op("vector", vq1c)
            v_q1 = p.sig("vector", SV)
            p.wait("sync", SV, v_q1)
            tr = p.dma("sync", lambda e, t=t: e.dma_start_transpose(
                out=q1fm[:].rearrange("p (c s) -> p c s", s=512)[:, 0:DC, t * 128:(t + 1) * 128],
                in_=qtile[:, 0:D].rearrange("p (c f) -> p c f", f=128)), "ftr")
            q1_tr.append(tr)

        ffn1_mm = []
        ffn1_last_t = {}
        ev_h = {}
        q2_tr = []
        tdn_ld = {}
        tdn_last = {}

        def tdn_half_load(k):
            r, h = k // 2, k % 2
            s = SOF[k % 3]
            lt = None
            for i in range(16):
                lt = p.dma("sync", lambda e, r=r, h=h, i=i, s=s: e.dma_start(
                    out=wsl_f[:, s * SLOT + i * 512: s * SLOT + (i + 1) * 512],
                    in_=tdn_sp[r, h * 16 + i]), f"tdn{s}")
            tdn_ld[k] = lt

        for j in range(I // 512):
            if j < 3:
                tl = tup_pre[j]
            else:
                p.wait("sync", ST, ffn1_mm[j - 3])
                lt = None
                for c in range(DC):
                    lt = p.dma("sync", lambda e, j=j, c=c: e.dma_start(
                        out=wsl_f[:, (j % 3) * SLOT + c * 512:
                                  (j % 3) * SLOT + (c + 1) * 512],
                        in_=tup_sp[j, c]), f"tup{j % 3}")
                tl = lt
            p.wait("tensor", tl)
            for t in range(4):
                par = 4 + (j * 4 + t) % 4
                p.wait("tensor", q1_tr[t])
                if j * 4 + t >= 4:
                    p.wait("tensor", SV, ev_h[j * 4 + t - 4])
                t_mm = None
                for c in range(DC):
                    t_mm = p.op("tensor", lambda e, par=par, c=c, t=t, j=j: e.matmul(
                        PS[par][:, :],
                        q1fm[:, c * 512 + t * 128: c * 512 + (t + 1) * 128],
                        wsl_f[:, (j % 3) * SLOT + c * 512:
                              (j % 3) * SLOT + (c + 1) * 512],
                        start=(c == 0), stop=(c == DC - 1)),
                        inc=ST if c == DC - 1 else None)
                if t == 3:
                    ffn1_mm.append(t_mm)
                p.wait("vector", ST, t_mm)

                def vh2(e, par=par, t=t, j=j):
                    dst = h2q[:, t * I + j * 512: t * I + (j + 1) * 512]
                    e.tensor_copy(out=dst, in_=PS[par][:, :])
                    e.drain()
                    e.scalar_tensor_tensor(out=dst, in0=dst, scalar=0.0,
                                           in1=dst, op0=ALU.max, op1=ALU.mult)
                    e.drain()
                    e.tensor_reduce(out=tinyA[:], in_=dst, axis=AX, op=ALU.max)
                    e.drain()
                    if j > 0:
                        e.tensor_tensor(out=m2a[:, t:t + 1], in0=m2a[:, t:t + 1],
                                        in1=tinyA[:], op=ALU.max)
                    else:
                        e.tensor_copy(out=m2a[:, t:t + 1], in_=tinyA[:])
                    e.drain()
                p.op("vector", vh2)
                ev_h[j * 4 + t] = p.sig("vector", SV)

                if j != I // 512 - 1:
                    continue
                ffn1_last_t[t] = t_mm
                if t == 0:
                    p.wait("sync", ST, ffn1_mm[5])
                    tdn_half_load(1)
                    p.wait("sync", ST, ffn1_mm[6])
                    tdn_half_load(0)
                    p.wait("vector", q1_tr[3])
                else:
                    p.wait("vector", q2_tr[t - 1])

                def vq2a(e, t=t):
                    e.tensor_tensor(out=tinyA[:], in0=a2f[:, t:t + 1],
                                    in1=m2a[:, t:t + 1], op=ALU.mult)
                    e.drain()
                    e.tensor_scalar(out=tinyA[:], in0=tinyA[:], scalar1=EPS,
                                    scalar2=None, op0=ALU.max)
                    e.drain()
                    e.tensor_tensor(out=f2[:, t:t + 1], in0=tinyA[:], in1=s_dnt[:],
                                    op=ALU.mult)
                    e.drain()
                    e.tensor_scalar(out=f2[:, t:t + 1], in0=f2[:, t:t + 1],
                                    scalar1=INV127, scalar2=None, op0=ALU.mult)
                    e.reciprocal(out=tinyB[:], in_=tinyA[:])
                    e.drain()
                    e.tensor_scalar(out=tinyB[:], in0=tinyB[:], scalar1=127.0,
                                    scalar2=None, op0=ALU.mult)
                    e.drain()
                    e.tensor_tensor(out=s2e[:], in0=tinyB[:], in1=a2f[:, t:t + 1],
                                    op=ALU.mult)
                    e.drain()
                p.op("vector", vq2a)
                v_gate = p.sig("vector", SV)
                for hf in range(2):
                    p.wait("scalar", SV, v_gate)

                    def s2b(e, t=t, hf=hf):
                        ha = h2q[:, t * I + hf * D: t * I + (hf + 1) * D]
                        e.activation(out=vtmp2[:], in_=ha, func=ACT.Identity,
                                     bias=m64b[:], scale=s2e[:])
                        e.drain()
                        e.activation(out=vtmp2[:], in_=vtmp2[:], func=ACT.Identity,
                                     bias=nm64b[:], scale=1.0)
                        e.drain()
                    p.op("scalar", s2b)
                    a_b = p.sig("scalar", SA)

                    def v2a(e, t=t, hf=hf):
                        ha = h2q[:, t * I + hf * D: t * I + (hf + 1) * D]
                        e.tensor_scalar(out=vtmp[:], in0=ha, scalar1=s2e[:],
                                        scalar2=M32, op0=ALU.mult, op1=ALU.add)
                        e.drain()
                        e.tensor_scalar(out=vtmp[:], in0=vtmp[:], scalar1=M32,
                                        scalar2=None, op0=ALU.subtract)
                        e.drain()
                    p.op("vector", v2a)
                    p.wait("vector", SA, a_b)

                    def v2c(e, t=t, hf=hf):
                        e.scalar_tensor_tensor(out=qtile[:, hf * D:(hf + 1) * D],
                                               in0=vtmp[:], scalar=mu[:],
                                               in1=vtmp2[:], op0=ALU.mult, op1=ALU.add)
                        e.drain()
                    p.op("vector", v2c)
                    v_gate = p.sig("vector", SV)
                v_q2 = v_gate
                if t == 3:
                    p.wait("sync", ST, ffn1_last_t[3])
                    tdn_half_load(2)
                p.wait("sync", SV, v_q2)
                p.wait("sync", ST, ffn1_last_t[t])
                tr = p.dma("sync", lambda e, t=t: e.dma_start_transpose(
                    out=q2fm[:].rearrange("p (c s) -> p c s", s=512)[:, :, t * 128:(t + 1) * 128],
                    in_=qtile[:].rearrange("p (c f) -> p c f", f=128)), "ftr")
                q2_tr.append(tr)
        last_q2tr = q2_tr[3]

        ffn2_mm = []
        ev_o2 = {}
        for r in range(NQB):
            for h in range(2):
                k = 2 * r + h
                if k not in tdn_ld:
                    p.wait("sync", ST, tdn_last[k - 3])
                    tdn_half_load(k)
            p.wait("tensor", tdn_ld[2 * r])
            for t in range(4):
                par = 4 + (r * 4 + t) % 4
                p.wait("tensor", q2_tr[t])
                if r * 4 + t >= 4:
                    p.wait("tensor", SV, ev_o2[r * 4 + t - 4])
                t_mm = None
                for c in range(IC):
                    if c == 16 and t == 0:
                        p.wait("tensor", tdn_ld[2 * r + 1])
                    s = SOF[(2 * r + c // 16) % 3]
                    t_mm = p.op("tensor", lambda e, par=par, c=c, t=t, s=s: e.matmul(
                        PS[par][:, :],
                        q2fm[:, c * 512 + t * 128: c * 512 + (t + 1) * 128],
                        wsl_f[:, s * SLOT + (c % 16) * 512:
                              s * SLOT + (c % 16 + 1) * 512],
                        start=(c == 0), stop=(c == IC - 1)),
                        inc=ST if (c == IC - 1 or (t == 3 and c == 15)) else None)
                    if t == 3 and c == 15:
                        tdn_last[2 * r] = t_mm
                if t == 3:
                    ffn2_mm.append(t_mm)
                    tdn_last[2 * r + 1] = t_mm
                p.wait("vector", ST, t_mm)
                p.op("vector", lambda e, par=par, t=t, r=r: e.tensor_scalar(
                    out=out2q[:, t * D + r * 512: t * D + (r + 1) * 512],
                    in0=PS[par][:, :], scalar1=f2[:, t:t + 1], scalar2=None, op0=ALU.mult))
                ev_o2[r * 4 + t] = p.sig("vector", SV)

        if qq < 3:
            tup_pre = []
            for j, kfree in ((0, 6), (1, 5), (2, 7)):
                p.wait("sync", ST, tdn_last[kfree])
                lt = None
                for c in range(DC):
                    lt = p.dma("sync", lambda e, j=j, c=c: e.dma_start(
                        out=wsl_f[:, j * SLOT + c * 512:
                                  j * SLOT + (c + 1) * 512],
                        in_=tup_sp[j, c]), f"tup{j}")
                tup_pre.append(lt)
        for t in range(4):
            p.wait("vector", SV, ev_o2[(NQB - 1) * 4 + t])

            def vst4(e, t=t):
                oa = out2q[:, t * D:(t + 1) * D]
                for gi in range(4):
                    e.bn_stats(out=stats4[:, gi * 6:(gi + 1) * 6],
                               in_=oa[:, gi * 512:(gi + 1) * 512])
                e.drain()
                e.bn_aggr(out=mv4[:], in_=stats4[:])
                e.drain()
            p.op("vector", vst4)
            v_s4 = p.sig("vector", SV)
            p.wait("scalar", SV, v_s4)

            def srs4(e, t=t):
                e.activation(out=rs4[:, t:t + 1], in_=mv4[:, 1:2], func=ACT.Ln,
                             bias=eps_t[:], scale=1.0)
                e.drain()
                e.activation(out=rs4[:, t:t + 1], in_=rs4[:, t:t + 1], func=ACT.Exp,
                             bias=0.0, scale=-0.5)
                e.drain()
                e.activation(out=nb4[:, t:t + 1], in_=mv4[:, 0:1], func=ACT.Identity,
                             bias=0.0, scale=rs4[:, t:t + 1])
                e.drain()
                e.activation(out=nb4[:, t:t + 1], in_=nb4[:, t:t + 1], func=ACT.Identity,
                             bias=0.0, scale=-1.0)
                e.drain()
            p.op("scalar", srs4)
            a_s4 = p.sig("scalar", SA)
            p.wait("vector", SA, a_s4)

            def vfin(e, t=t):
                oa = out2q[:, t * D:(t + 1) * D]
                e.tensor_scalar(out=oa, in0=oa, scalar1=rs4[:, t:t + 1],
                                scalar2=nb4[:, t:t + 1], op0=ALU.mult, op1=ALU.add)
                e.drain()
                e.tensor_tensor(out=x2q[:, t * D:(t + 1) * D], in0=oa,
                                in1=x2q[:, t * D:(t + 1) * D], op=ALU.add)
                e.drain()
            p.op("vector", vfin)
            v_fin = p.sig("vector", SV)
            p.wait("sync", SV, v_fin)
            st_done.append(p.dma("sync", lambda e, qq=qq, t=t: e.dma_start(
                out=out[(qq * 4 + t) * 128:(qq * 4 + t + 1) * 128, :],
                in_=x2q[:, t * D:(t + 1) * D]), f"out{t}"))

    for slot in range(4):
        p.wait("sync", f"out{slot}", p.n(f"out{slot}"))
    ph4.close()

    block = es.enter_context(nc.Block())
    for eng in Prog.ENGINES:
        ops = p.ops[eng]
        getattr(block, eng)(lambda e, ops=ops: [fn(e) for fn in ops])
    es.close()
    return nc


_NC_CACHE = {}


def kernel(L0_attn_X, L0_attn_Wq, L0_attn_bq, L0_attn_Wk, L0_attn_bk,
           L0_attn_Wv, L0_attn_bv, L0_attn_Wo, L0_attn_bo,
           L0_ffn_w_up, L0_ffn_w_down):
    X = np.ascontiguousarray(np.asarray(L0_attn_X, dtype=np.float32))
    wq_t = np.ascontiguousarray(np.asarray(L0_attn_Wq, np.float32).T)
    wk_t = np.ascontiguousarray(np.asarray(L0_attn_Wk, np.float32).T)
    wv_t = np.ascontiguousarray(np.asarray(L0_attn_Wv, np.float32).T)
    wo_t = np.ascontiguousarray(np.asarray(L0_attn_Wo, np.float32).T)
    wup_t = np.ascontiguousarray(np.asarray(L0_ffn_w_up, np.float32).T)
    wdn_t = np.ascontiguousarray(np.asarray(L0_ffn_w_down, np.float32).T)
    bqv = np.ascontiguousarray(np.asarray(L0_attn_bq, np.float32))
    bkv = np.ascontiguousarray(np.asarray(L0_attn_bk, np.float32))
    bvv = np.ascontiguousarray(np.asarray(L0_attn_bv, np.float32))
    bov = np.ascontiguousarray(np.asarray(L0_attn_bo, np.float32))

    if "nc" not in _NC_CACHE:
        _NC_CACHE["nc"] = build_core()
    nc = _NC_CACHE["nc"]
    in_maps = []
    for c in range(8):
        in_maps.append({
            "x": np.ascontiguousarray(X[c]),
            "wqt": wq_t, "wkt": wk_t, "wvt": wv_t, "wot": wo_t,
            "wupt": wup_t, "wdnt": wdn_t,
            "bq": bqv, "bk": bkv, "bv": bvv, "bo": bov,
        })
    res = run_bass_kernel_spmd(nc, in_maps, core_ids=list(range(8)))
    return np.stack([np.asarray(r["out"], np.float32) for r in res.results], axis=0)


# version 15
# speedup vs baseline: 1.0744x; 1.0744x over previous
import contextlib
import numpy as np
import concourse.bass as bass
import concourse.mybir as mybir
from concourse.bass_utils import run_bass_kernel_spmd

F32 = mybir.dt.float32
BF16 = mybir.dt.bfloat16
ALU = mybir.AluOpType
ACT = mybir.ActivationFunctionType
AX = mybir.AxisListType.X

S = 2048
D = 2048
H = 16
I = 4096
NT = S // 128
DC = D // 128
IC = I // 128
SLOT = DC * 512
SOF = (0, 2, 1)
NQB = 4
NG = 4
M32 = float(np.float32(12582912.0))
M64 = float(np.float32(1.5 * 2 ** 52))
_SCL64 = 1.0 / np.sqrt(np.sqrt(128.0))
SCL = float(np.float32(_SCL64))
SCL_C = float(np.float32(_SCL64 - SCL))
EPS = 1.0e-5
INV_WN = float(np.float32(1.0 / 8388608.0))
INV127 = float(np.float32(1.0 / 127.0))


class Prog:
    ENGINES = ("sync", "gpsimd", "vector", "scalar", "tensor")

    def __init__(self, es, nc):
        self.nc = nc
        self.ops = {e: [] for e in self.ENGINES}
        self.sems = {}
        self.es = es
        self.last_wait = {e: {} for e in self.ENGINES}

    def sem(self, name):
        if name not in self.sems:
            h = self.es.enter_context(self.nc.semaphore(name))
            self.sems[name] = [h, 0]
        return name

    def n(self, s):
        self.sem(s)
        return self.sems[s][1]

    def wait(self, eng, s, val=None):
        if isinstance(s, tuple):
            s, val = s
        h, n = self.sems[self.sem(s)]
        if val is None:
            val = n
        assert val <= n, f"wait on future: {s} {val} > {n}"
        if val <= 0:
            return
        if self.last_wait[eng].get(s, 0) >= val:
            return
        self.last_wait[eng][s] = val
        self.ops[eng].append(lambda e, h=h, v=val: e.wait_ge(h, v))

    def op(self, eng, fn, inc=None, k=1):
        if inc is None:
            self.ops[eng].append(fn)
            return None
        h, _ = self.sems[self.sem(inc)]
        self.sems[inc][1] += k
        tgt = self.sems[inc][1]
        self.ops[eng].append(lambda e, fn=fn, h=h, k=k: fn(e).then_inc(h, k))
        return tgt

    def dma(self, eng, fn, stream):
        return (stream, self.op(eng, fn, inc=stream, k=16))

    def full_barrier(self, exclude=()):
        for eng in self.ENGINES:
            for s in list(self.sems):
                if any(s.startswith(x) for x in exclude):
                    continue
                self.wait(eng, s)

    def sig(self, eng, inc, k=1):
        if eng in ("vector", "scalar", "gpsimd"):
            self.ops[eng].append(lambda e: e.drain())
        return self.op(eng, lambda e: e.nop(), inc=inc, k=k)


def build_core():
    es = contextlib.ExitStack()
    nc = bass.Bass("TRN2", target_bir_lowering=False)

    x_in = nc.dram_tensor("x", [S, D], F32, kind="ExternalInput").ap()
    wqt = nc.dram_tensor("wqt", [D, D], F32, kind="ExternalInput").ap()
    wkt = nc.dram_tensor("wkt", [D, D], F32, kind="ExternalInput").ap()
    wvt = nc.dram_tensor("wvt", [D, D], F32, kind="ExternalInput").ap()
    wot = nc.dram_tensor("wot", [D, D], F32, kind="ExternalInput").ap()
    wupt = nc.dram_tensor("wupt", [D, I], F32, kind="ExternalInput").ap()
    wdnt = nc.dram_tensor("wdnt", [I, D], F32, kind="ExternalInput").ap()
    b_in = {n: nc.dram_tensor(n, [D], F32, kind="ExternalInput").ap()
            for n in ("bq", "bk", "bv", "bo")}
    out = nc.dram_tensor("out", [S, D], F32, kind="ExternalOutput").ap()

    x_sp = nc.dram_tensor("x_sp", [S, D], BF16, kind="Internal").ap()
    x2_sp = nc.dram_tensor("x2_sp", [S, D], BF16, kind="Internal").ap()
    wo_sp = nc.dram_tensor("wo_sp", [D, D], BF16, kind="Internal").ap()
    o_sp = nc.dram_tensor("o_sp", [H, 128, S], BF16, kind="Internal").ap()
    tup_sp = nc.dram_tensor("tup_sp", [I // 512, DC, 128, 512], BF16, kind="Internal").ap()
    tdn_sp = nc.dram_tensor("tdn_sp", [D // 512, IC, 128, 512], BF16, kind="Internal").ap()

    def sbuf(name, shape, dt=F32, st=es):
        return st.enter_context(nc.sbuf_tensor(name, shape, dt))

    p = Prog(es, nc)
    ST, SV, SA = (p.sem(s) for s in ("s_t", "s_v", "s_a"))
    SG = p.sem("s_g")

    mu = sbuf("mu", [128, 1])
    sc_qk = sbuf("sc_qk", [128, 1])
    eps_t = sbuf("eps_t", [128, 1])
    e35_t = sbuf("e35_t", [128, 1])
    c60_t = sbuf("c60_t", [128, 1])
    m64b = sbuf("m64b", [128, 1])
    nm64b = sbuf("nm64b", [128, 1])
    ones_bf = sbuf("ones_bf", [128, 128], BF16)
    ones_f = sbuf("ones_f", [128, 128])
    strip = sbuf("strip", [128, 896], BF16)
    bq_s = sbuf("bq_s", [128, H])
    bk_s = sbuf("bk_s", [128, H])
    bv_s = sbuf("bv_s", [128, H])
    wst_up = sbuf("wst_up", [128, 128])
    wst_dn = sbuf("wst_dn", [128, 128])
    th_up = sbuf("th_up", [128, 1])
    th_dn = sbuf("th_dn", [128, 1])
    nth_up = sbuf("nth_up", [128, 1])
    nth_dn = sbuf("nth_dn", [128, 1])
    s_upt = sbuf("s_upt", [128, 1])
    s_dnt = sbuf("s_dnt", [128, 1])
    nk2 = sbuf("nk2", [128, H])
    smallA = sbuf("smallA", [128, 4])
    negC = sbuf("negC", [128, 1])
    tinyA = sbuf("tinyA", [128, 1])
    tinyB = sbuf("tinyB", [128, 1])

    PSA = es.enter_context(nc.psum_tensor("psA", [128, 2048], F32))
    PSB = es.enter_context(nc.psum_tensor("psB", [128, 2048], F32))
    PS = [None, None, None, None] + [PSB[:, i * 512:(i + 1) * 512] for i in range(4)]
    S2 = [PSA[:, 0:1024], PSA[:, 1024:2048]]
    S4 = [PSA[:, j * 512:(j + 1) * 512] for j in range(4)]

    for nm, t in (("bq", bq_s), ("bk", bk_s), ("bv", bv_s)):
        p.op("gpsimd", lambda e, nm=nm, t=t: e.dma_start(
            out=t[:], in_=b_in[nm].rearrange("(h e) -> e h", e=128),
            allow_slow_non_contiguous=True), inc="bia", k=16)
    w_bias = ("bia", p.n("bia"))

    ph_y1 = contextlib.ExitStack()
    y1_fm = sbuf("y1_fm", [128, DC * S], BF16, ph_y1)
    ph1 = contextlib.ExitStack()
    xt = [sbuf(f"xt{i}", [128, D], F32, ph1) for i in range(2)]
    y1t = [sbuf(f"y1t{i}", [128, D], BF16, ph1) for i in range(2)]
    stats = sbuf("stats", [128, 24], F32, ph1)
    mv = sbuf("mv", [128, 2], F32, ph1)
    rs_t = sbuf("rs_t", [128, 1], F32, ph1)
    nb_t = sbuf("nb_t", [128, 1], F32, ph1)
    wch = [sbuf(f"wch{i}", [128, I], F32, ph1) for i in range(2)]
    strip_f = sbuf("strip_f", [128, 896], F32, ph1)

    g_strip = p.op("gpsimd", lambda e: e.iota(
        strip_f[:], pattern=[[1, 896]], base=-384, channel_multiplier=-1,
        allow_small_or_imprecise_dtypes=True), inc=SG)

    p.wait("vector", w_bias)
    p.wait("vector", SG, g_strip)

    def vconst(e):
        e.memset(mu[:], 0.75)
        e.memset(eps_t[:], EPS)
        e.memset(e35_t[:], 1.0e-35)
        e.memset(c60_t[:], -60.0)
        e.memset(m64b[:], M64)
        e.memset(nm64b[:], -M64)
        e.memset(ones_f[:], 1.0)
        e.memset(ones_bf[:], 1.0)
        e.drain()
        e.tensor_scalar(out=mu[:], in0=mu[:], scalar1=M64, scalar2=M64,
                        op0=ALU.add, op1=ALU.subtract)
        e.drain()
        e.tensor_scalar(out=mu[:], in0=mu[:], scalar1=-1.0, scalar2=1.0,
                        op0=ALU.mult, op1=ALU.add)
        e.drain()
        e.tensor_scalar(out=sc_qk[:], in0=mu[:], scalar1=-SCL_C, scalar2=SCL + SCL_C,
                        op0=ALU.mult, op1=ALU.add)
        e.tensor_scalar(out=strip[:], in0=strip_f[:], scalar1=0.0, scalar2=None,
                        op0=ALU.is_ge)
        e.drain()
        e.tensor_scalar(out=bq_s[:], in0=bq_s[:], scalar1=sc_qk[:], scalar2=None,
                        op0=ALU.mult)
        e.tensor_scalar(out=bk_s[:], in0=bk_s[:], scalar1=sc_qk[:], scalar2=None,
                        op0=ALU.mult)
        e.drain()
    p.op("vector", vconst)
    v_const = p.sig("vector", SV)

    x_ld, a_apply, tr_done, sp_done = [], [], [], []
    for i in range(NT):
        if i >= 2:
            p.wait("sync", SA, a_apply[i - 2])
            p.wait("sync", sp_done[i - 2])
        x_ld.append(p.dma("sync", lambda e, i=i: e.dma_start(
            out=xt[i % 2][:], in_=x_in[i * 128:(i + 1) * 128, :]), f"xl{i % 2}"))

        p.wait("vector", x_ld[i])

        def vstats(e, i=i):
            for gi in range(4):
                e.bn_stats(out=stats[:, gi * 6:(gi + 1) * 6],
                           in_=xt[i % 2][:, gi * 512:(gi + 1) * 512])
            e.drain()
            e.bn_aggr(out=mv[:], in_=stats[:])
            e.drain()
        p.op("vector", vstats)
        v_st = p.sig("vector", SV)

        p.wait("scalar", SV, v_st)
        if i >= 2:
            p.wait("scalar", tr_done[i - 2])

        def sapply(e):
            e.activation(out=rs_t[:], in_=mv[:, 1:2], func=ACT.Ln, bias=eps_t[:], scale=1.0)
            e.drain()
            e.activation(out=rs_t[:], in_=rs_t[:], func=ACT.Exp, bias=0.0, scale=-0.5)
            e.drain()
            e.activation(out=nb_t[:], in_=mv[:, 0:1], func=ACT.Identity, bias=0.0,
                         scale=rs_t[:])
            e.drain()
            e.activation(out=nb_t[:], in_=nb_t[:], func=ACT.Identity, bias=0.0, scale=-1.0)
            e.drain()
        p.op("scalar", sapply)
        a_apply.append(p.op("scalar", lambda e, i=i: e.activation(
            out=y1t[i % 2][:], in_=xt[i % 2][:], func=ACT.Identity,
            bias=nb_t[:], scale=rs_t[:]), inc=SA))

        p.wait("sync", SA, a_apply[i])
        t = p.dma("sync", lambda e, i=i: e.dma_start_transpose(
            out=y1_fm[:].rearrange("p (c s) -> p c s", s=S)[:, :, i * 128:(i + 1) * 128],
            in_=y1t[i % 2][:].rearrange("p (c f) -> p c f", f=128)), f"p1tr{i % 2}")
        tr_done.append(t)

        p.wait("gpsimd", SA, a_apply[i])
        sp_done.append(p.dma("gpsimd", lambda e, i=i: e.dma_start(
            out=x_sp[i * 128:(i + 1) * 128, :], in_=xt[i % 2][:]), f"xsp{i % 2}"))

    NWCH = DC + IC
    wst_ld, wst_v = [], []
    for j in range(NWCH):
        if j >= 2:
            p.wait("gpsimd", SV, wst_v[j - 2])
        if j < DC:
            wst_ld.append(p.dma("gpsimd", lambda e, j=j: e.dma_start(
                out=wch[j % 2][:], in_=wupt[j * 128:(j + 1) * 128, :]), f"ws{j % 2}"))
        else:
            wst_ld.append(p.dma("gpsimd", lambda e, c=j - DC, j=j: e.dma_start(
                out=wch[j % 2][:, 0:D], in_=wdnt[c * 128:(c + 1) * 128, :]), f"ws{j % 2}"))
        p.wait("vector", wst_ld[j])

        def vred(e, j=j):
            if j < DC:
                for gi in range(8):
                    e.tensor_reduce(out=wst_up[:, j * 8 + gi: j * 8 + gi + 1],
                                    in_=wch[j % 2][:, gi * 512:(gi + 1) * 512],
                                    axis=AX, op=ALU.add, apply_absolute_value=True)
            else:
                c = j - DC
                for gi in range(4):
                    e.tensor_reduce(out=wst_dn[:, c * 4 + gi: c * 4 + gi + 1],
                                    in_=wch[j % 2][:, gi * 512:(gi + 1) * 512],
                                    axis=AX, op=ALU.add, apply_absolute_value=True)
        p.op("vector", vred)
        wst_v.append(p.sig("vector", SV))

    def vtot(e):
        e.drain()
        e.tensor_reduce(out=smallA[:, 0:1], in_=wst_up[:], axis=AX, op=ALU.add)
        e.tensor_reduce(out=smallA[:, 1:2], in_=wst_dn[:], axis=AX, op=ALU.add)
        e.memset(smallA[:, 2:4], 0.0)
        e.drain()
    p.op("vector", vtot)
    v_tot = p.sig("vector", SV)
    p.wait("tensor", SV, v_tot)
    t_bc1 = p.op("tensor", lambda e: e.matmul(
        PS[5][0:1, 0:4], ones_f[:, 0:1], smallA[:, 0:4], start=True, stop=True), inc=ST)
    p.wait("vector", ST, t_bc1)
    p.op("vector", lambda e: e.tensor_copy(out=smallA[0:1, 0:4], in_=PS[5][0:1, 0:4]))
    v_bc1 = p.sig("vector", SV)
    p.wait("tensor", SV, v_bc1)
    t_bc = p.op("tensor", lambda e: e.matmul(
        PS[4][:, 0:4], ones_f[0:1, :], smallA[0:1, 0:4], start=True, stop=True), inc=ST)
    p.wait("vector", ST, t_bc)

    def vth(e):
        e.tensor_scalar(out=s_upt[:], in0=PS[4][:, 0:1], scalar1=INV_WN, scalar2=EPS,
                        op0=ALU.mult, op1=ALU.max)
        e.tensor_scalar(out=s_dnt[:], in0=PS[4][:, 1:2], scalar1=INV_WN, scalar2=EPS,
                        op0=ALU.mult, op1=ALU.max)
        e.drain()
        e.tensor_scalar(out=th_up[:], in0=s_upt[:], scalar1=0.5, scalar2=None, op0=ALU.mult)
        e.tensor_scalar(out=th_dn[:], in0=s_dnt[:], scalar1=0.5, scalar2=None, op0=ALU.mult)
        e.drain()
        e.tensor_scalar(out=nth_up[:], in0=th_up[:], scalar1=-1.0, scalar2=None, op0=ALU.mult)
        e.tensor_scalar(out=nth_dn[:], in0=th_dn[:], scalar1=-1.0, scalar2=None, op0=ALU.mult)
        e.drain()
    p.op("vector", vth)
    v_th = p.sig("vector", SV)

    ph1.close()
    p.full_barrier()

    for c in range(DC):
        p.dma("gpsimd", lambda e, c=c: e.dma_start(
            out=wo_sp[c * 128:(c + 1) * 128, :],
            in_=wot[c * 128:(c + 1) * 128, :]), "woc")

    ph2 = contextlib.ExitStack()
    kg = sbuf("kg", [128, 4 * S], BF16, ph2)
    vg = sbuf("vg", [128, NT * 512], BF16, ph2)
    wsl2 = [sbuf(f"wsl{i}", [128, DC * 512], BF16, ph2) for i in range(2)]
    qh_t = [sbuf(f"qh{i}", [128, 4 * 512], BF16, ph2) for i in range(2)]
    sq2_t = sbuf("sq2_t", [128, 512], F32, ph2)
    p_t = [sbuf(f"pt{i}", [128, 1024], BF16, ph2) for i in range(2)]
    rcp_t = [sbuf(f"rcp{i}", [128, 512], F32, ph2) for i in range(2)]
    sq_t = sbuf("sq_t", [128, 512], F32, ph2)
    ot_b = [sbuf(f"otb{i}", [128, 512], BF16, ph2) for i in range(2)]
    bv_b = sbuf("bv_b", [128, 512], F32, ph2)
    bv_row = sbuf("bv_row", [1, 512], F32, ph2)
    wqch = [sbuf(f"wqch{i}", [128, D], F32, ph2) for i in range(2)]
    wq_p = sbuf("wq_p", [128, D], F32, ph2)
    wq_n = sbuf("wq_n", [128, D], F32, ph2)
    tq_bf = sbuf("tq_bf", [128, D], BF16, ph2)

    wq_units = ([("up", c, hf) for c in range(DC) for hf in range(2)]
                + [("dn", c, 0) for c in range(IC)])
    wq_ld, wq_vdone, wq_spdone = {}, {}, {}
    wq_state = {"ld": 0, "q": 0}

    def wq_emit_load(n=1):
        for _ in range(n):
            u = wq_state["ld"]
            if u >= len(wq_units) or u >= wq_state["q"] + 2:
                return
            kind, c, hf = wq_units[u]
            if u >= 2:
                p.wait("gpsimd", SV, wq_vdone[u - 2])
            if kind == "up":
                wq_ld[u] = p.dma("gpsimd", lambda e, c=c, hf=hf, u=u: e.dma_start(
                    out=wqch[u % 2][:],
                    in_=wupt[c * 128:(c + 1) * 128, hf * 2048:(hf + 1) * 2048]),
                    f"wqs{u % 2}")
            else:
                wq_ld[u] = p.dma("gpsimd", lambda e, c=c, u=u: e.dma_start(
                    out=wqch[u % 2][:], in_=wdnt[c * 128:(c + 1) * 128, :]), f"wqs{u % 2}")
            wq_state["ld"] += 1

    def wq_emit_quant(n=1):
        for _ in range(n):
            u = wq_state["q"]
            if u >= len(wq_units) or u >= wq_state["ld"]:
                return
            kind, c, hf = wq_units[u]
            th, nth = (th_up, nth_up) if kind == "up" else (th_dn, nth_dn)
            p.wait("vector", SV, v_th)
            p.wait("vector", wq_ld[u])
            if u >= 1:
                p.wait("vector", wq_spdone[u - 1])

            def vq(e, u=u, th=th, nth=nth):
                e.tensor_scalar(out=wq_p[:], in0=wqch[u % 2][:], scalar1=th[:],
                                scalar2=None, op0=ALU.is_ge)
                e.tensor_scalar(out=wq_n[:], in0=wqch[u % 2][:], scalar1=nth[:],
                                scalar2=None, op0=ALU.is_le)
                e.drain()
                e.tensor_tensor(out=tq_bf[:], in0=wq_p[:], in1=wq_n[:], op=ALU.subtract)
            p.op("vector", vq)
            wq_vdone[u] = p.sig("vector", SV)
            p.wait("gpsimd", SV, wq_vdone[u])
            last = None
            for j in range(4):
                if kind == "up":
                    last = p.dma("gpsimd", lambda e, c=c, hf=hf, j=j: e.dma_start(
                        out=tup_sp[hf * 4 + j, c], in_=tq_bf[:, j * 512:(j + 1) * 512]),
                        f"stq{u % 2}")
                else:
                    last = p.dma("gpsimd", lambda e, c=c, j=j: e.dma_start(
                        out=tdn_sp[j, c], in_=tq_bf[:, j * 512:(j + 1) * 512]),
                        f"stq{u % 2}")
            wq_spdone[u] = last
            wq_state["q"] += 1

    ev_k, ev_v = {}, {}
    o_evac, o_spw, q_evac, a_rec = [], [], [], []
    exp_done, mask_done, pv_mm = {}, {}, {}
    t_sqr = [None]
    last_qev = [None]
    last_qproj = [None]
    lk_pf = {}
    q_proj_last = {}
    hq = 0
    for grp in range(NG):
        p.wait("tensor", SA, p.n(SA))
        p.wait("tensor", SV, p.n(SV))
        wsl = wsl2[grp % 2]
        if grp == 0:
            lk = None
            for c in range(DC):
                lk = p.dma("gpsimd", lambda e, c=c, grp=grp, wsl=wsl: e.dma_start(
                    out=wsl[:, c * 512:(c + 1) * 512],
                    in_=wkt[c * 128:(c + 1) * 128, grp * 512:(grp + 1) * 512]), "kvw")
        else:
            lk = lk_pf[grp]
        p.wait("tensor", lk)
        for comb in range(16):
            hl, r = divmod(comb, NQB)
            h = grp * 4 + hl
            par = 4 + comb % 2
            if comb >= 2:
                p.wait("tensor", SA, ev_k[(grp, comb - 2)])
            p.wait("tensor", tr_done[4 * r + 3])
            t_mm = None
            for c in range(DC):
                t_mm = p.op("tensor", lambda e, par=par, c=c, hl=hl, r=r, wsl=wsl: e.matmul(
                    PS[par][:, :], wsl[:, c * 512 + hl * 128: c * 512 + (hl + 1) * 128],
                    y1_fm[:, c * S + r * 512: c * S + (r + 1) * 512],
                    start=(c == 0), stop=(c == DC - 1)),
                    inc=ST if c == DC - 1 else None)
            p.wait("scalar", ST, t_mm)
            ev_k[(grp, comb)] = p.op("scalar", lambda e, par=par, hl=hl, h=h, r=r: e.activation(
                out=kg[:, hl * S + r * 512: hl * S + (r + 1) * 512],
                in_=PS[par][:, :], func=ACT.Identity,
                bias=bk_s[:, h:h + 1], scale=sc_qk[:]), inc=SA)
            k_last = t_mm

        lv = None
        p.wait("gpsimd", ST, k_last)
        for c in range(DC):
            lv = p.dma("gpsimd", lambda e, c=c, grp=grp, wsl=wsl: e.dma_start(
                out=wsl[:, c * 512:(c + 1) * 512],
                in_=wvt[c * 128:(c + 1) * 128, grp * 512:(grp + 1) * 512]), "kvw")
        bvr_ld = p.dma("gpsimd", lambda e, grp=grp: e.dma_start(
            out=bv_row[:], in_=b_in["bv"].rearrange("(o d) -> o d", o=4)[grp:grp + 1, :]),
            "bvr")
        p.wait("tensor", SV, v_const)
        p.wait("tensor", bvr_ld)
        tb = p.op("tensor", lambda e: e.matmul(
            PS[7][:, :], ones_f[0:1, :], bv_row[0:1, :], start=True, stop=True), inc=ST)
        p.wait("vector", ST, tb)
        p.op("vector", lambda e: e.tensor_copy(out=bv_b[:], in_=PS[7][:, :]))
        v_bvb = p.sig("vector", SV)

        p.wait("tensor", lv)
        for t in range(NT):
            par = 4 + t % 2
            if t < 2:
                p.wait("tensor", SA, ev_k[(grp, 14 + t)])
            else:
                p.wait("tensor", SV, ev_v[(grp, t - 2)])
            t_mm = None
            for c in range(DC):
                t_mm = p.op("tensor", lambda e, par=par, c=c, t=t, wsl=wsl: e.matmul(
                    PS[par][:, :], y1_fm[:, c * S + t * 128: c * S + (t + 1) * 128],
                    wsl[:, c * 512:(c + 1) * 512],
                    start=(c == 0), stop=(c == DC - 1)),
                    inc=ST if c == DC - 1 else None)
            v_last = t_mm
            p.wait("vector", ST, t_mm)
            p.wait("vector", SV, v_bvb)
            p.op("vector", lambda e, par=par, t=t: e.tensor_tensor(
                out=vg[:, t * 512:(t + 1) * 512], in0=PS[par][:, :], in1=bv_b[:],
                op=ALU.add))
            ev_v[(grp, t)] = p.sig("vector", SV)

        vnk_prev = None
        for hl in range(4):
            h = grp * 4 + hl

            def vnkinit(e, h=h):
                e.memset(nk2[:, h:h + 1], 0.0)
                e.drain()
            p.op("vector", vnkinit)
            for r in range(NQB):
                p.wait("vector", SA, ev_k[(grp, hl * NQB + r)])
                if t_sqr[0] is not None:
                    p.wait("vector", ST, t_sqr[0])
                p.op("vector", lambda e, hl=hl, r=r: e.tensor_tensor(
                    out=sq_t[:], in0=kg[:, hl * S + r * 512: hl * S + (r + 1) * 512],
                    in1=kg[:, hl * S + r * 512: hl * S + (r + 1) * 512], op=ALU.mult))
                v_sq = p.sig("vector", SV)
                p.wait("tensor", SV, v_sq)
                if vnk_prev is not None:
                    p.wait("tensor", SV, vnk_prev)
                t_nk = p.op("tensor", lambda e: e.matmul(
                    PS[6][:, :], ones_f[:, :], sq_t[:], start=True, stop=True), inc=ST)
                t_sqr[0] = t_nk
                p.wait("vector", ST, t_nk)

                def vnkacc(e, h=h):
                    e.tensor_reduce(out=tinyA[:], in_=PS[6][:, :], axis=AX, op=ALU.max)
                    e.drain()
                    e.tensor_tensor(out=nk2[:, h:h + 1], in0=nk2[:, h:h + 1],
                                    in1=tinyA[:], op=ALU.max)
                    e.drain()
                p.op("vector", vnkacc)
                vnk_prev = p.sig("vector", SV)

        p.wait("gpsimd", ST, v_last)
        lq = None
        for c in range(DC):
            lq = p.dma("gpsimd", lambda e, c=c, grp=grp, wsl=wsl: e.dma_start(
                out=wsl[:, c * 512:(c + 1) * 512],
                in_=wqt[c * 128:(c + 1) * 128, grp * 512:(grp + 1) * 512]), "kvw")
        if grp + 1 < NG:
            if grp >= 1:
                p.wait("gpsimd", ST, q_proj_last[grp - 1])
            t_pf = None
            for c in range(DC):
                t_pf = p.dma("gpsimd", lambda e, c=c, grp=grp: e.dma_start(
                    out=wsl2[(grp + 1) % 2][:, c * 512:(c + 1) * 512],
                    in_=wkt[c * 128:(c + 1) * 128, (grp + 1) * 512:(grp + 2) * 512]),
                    "kvwP")
            lk_pf[grp + 1] = t_pf

        for _ in range(3):
            wq_emit_load(1)
            wq_emit_quant(1)

        p.wait("tensor", SV, p.n(SV))
        p.wait("tensor", SA, p.n(SA))
        p.wait("tensor", lq)
        for hl in range(4):
            h = grp * 4 + hl
            hh = grp * 4 + hl
            hpar = hh % 2
            qev = []
            for r4 in range(4):
                if r4 > 0:
                    p.wait("tensor", SV, qev[r4 - 1])
                elif last_qev[0] is not None:
                    p.wait("tensor", SV, last_qev[0])
                if hh >= 2 and r4 == 0:
                    last_hq = (hh - 2) * NQB + NQB - 1
                    ns2 = 2 * NQB
                    p.wait("tensor", ST, pv_mm[(last_hq, ns2 - 1)])
                t_qmm = None
                for c in range(DC):
                    t_qmm = p.op("tensor", lambda e, c=c, hl=hl, r4=r4, wsl=wsl: e.matmul(
                        PS[4][:, :],
                        wsl[:, c * 512 + hl * 128: c * 512 + (hl + 1) * 128],
                        y1_fm[:, c * S + r4 * 512: c * S + (r4 + 1) * 512],
                        start=(c == 0), stop=(c == DC - 1)),
                        inc=ST if c == DC - 1 else None)
                p.wait("vector", ST, t_qmm)
                if t_sqr[0] is not None:
                    p.wait("vector", ST, t_sqr[0])

                def vqev(e, hpar=hpar, h=h, r4=r4):
                    e.tensor_scalar(out=qh_t[hpar][:, r4 * 512:(r4 + 1) * 512],
                                    in0=PS[4][:, :], scalar1=sc_qk[:],
                                    scalar2=bq_s[:, h:h + 1], op0=ALU.mult, op1=ALU.add)
                    e.drain()
                    e.tensor_tensor(out=(sq_t if r4 % 2 == 0 else sq2_t)[:],
                                    in0=qh_t[hpar][:, r4 * 512:(r4 + 1) * 512],
                                    in1=qh_t[hpar][:, r4 * 512:(r4 + 1) * 512],
                                    op=ALU.mult)
                p.op("vector", vqev)
                qev.append(p.sig("vector", SV))
                p.wait("tensor", SV, qev[r4])
                if r4 == 0 and hh >= 1:
                    p.wait("tensor", SA, a_rec[hh * NQB - 1])
                if r4 == 0 and hh >= 2:
                    p.wait("tensor", SV, o_evac[hh * NQB - 2])
                t_nq = p.op("tensor", lambda e, r4=r4: e.matmul(
                    PS[5][:, :], ones_f[:, :], (sq_t if r4 % 2 == 0 else sq2_t)[:],
                    start=(r4 == 0), stop=(r4 == 3)), inc=ST)
                t_sqr[0] = t_nq
                last_qproj[0] = t_qmm
            last_qev[0] = qev[3]
            p.wait("vector", ST, t_nq)

            def vcap(e, h=h):
                e.tensor_reduce(out=tinyA[:], in_=PS[5][:, :], axis=AX, op=ALU.max)
                e.drain()
                e.tensor_tensor(out=tinyB[:], in0=tinyA[:], in1=nk2[:, h:h + 1],
                                op=ALU.mult)
                e.drain()
            p.op("vector", vcap)
            cap_v = p.sig("vector", SV)
            p.wait("scalar", SV, cap_v)

            def scap(e):
                e.activation(out=negC[:], in_=tinyB[:], func=ACT.Ln, bias=0.0, scale=1.0)
                e.drain()
                e.activation(out=negC[:], in_=negC[:], func=ACT.Exp, bias=0.0, scale=0.5)
                e.drain()
                e.activation(out=negC[:], in_=negC[:], func=ACT.Relu, bias=c60_t[:],
                             scale=1.0)
                e.drain()
                e.activation(out=negC[:], in_=negC[:], func=ACT.Identity, bias=0.0,
                             scale=-1.0)
                e.drain()
            p.op("scalar", scap)

            for qb in range(NQB):
                nk = 4 * (qb + 1)
                ns = nk // 2
                qpar = hq % 2
                ns_prev = 2 * ((hq - 1) % NQB + 1) if hq >= 1 else 0
                qsl = qh_t[hpar][:, qb * 512:(qb + 1) * 512]
                for u in range(ns):
                    sb = u % 2
                    if u >= 2:
                        p.wait("tensor", SA, exp_done[(hq, u - 2)])
                    elif hq >= 1 and ns_prev - 2 + sb >= 0:
                        p.wait("tensor", SA, exp_done[(hq - 1, ns_prev - 2 + sb)])
                    if u == 0 and hq >= 1:
                        p.wait("tensor", SA, a_rec[hq - 1])
                    if u == 0 and hq >= 2:
                        p.wait("tensor", SV, o_evac[hq - 2])
                    p.op("tensor", lambda e, sb=sb, hl=hl, u=u, qsl=qsl: e.matmul(
                        S2[sb][:, 0:512],
                        kg[:, hl * S + (2 * u) * 128: hl * S + (2 * u + 1) * 128],
                        qsl, start=True, stop=True))
                    t_smm = p.op("tensor", lambda e, sb=sb, hl=hl, u=u, qsl=qsl: e.matmul(
                        S2[sb][:, 512:1024],
                        kg[:, hl * S + (2 * u + 1) * 128: hl * S + (2 * u + 2) * 128],
                        qsl, start=True, stop=True), inc=ST)
                    p.wait("scalar", ST, t_smm)
                    if u >= 2:
                        p.wait("scalar", ST, pv_mm[(hq, u - 2)])
                    elif hq >= 1 and (hq - 1, ns_prev - 2 + sb) in pv_mm:
                        p.wait("scalar", ST, pv_mm[(hq - 1, ns_prev - 2 + sb)])
                    exp_done[(hq, u)] = p.op("scalar", lambda e, sb=sb: e.activation(
                        out=p_t[sb][:], in_=S2[sb][:, 0:1024], func=ACT.Exp,
                        bias=negC[:], scale=1.0), inc=SA)
                    diag = (2 * u + 1) >= nk - 4
                    if diag:
                        p.wait("vector", SA, exp_done[(hq, u)])
                        for half in range(2):
                            kc = 2 * u + half
                            if kc >= nk - 4:
                                m = kc - 4 * qb
                                p.op("vector", lambda e, sb=sb, m=m, half=half: e.tensor_tensor(
                                    out=p_t[sb][:, half * 512:(half + 1) * 512],
                                    in0=p_t[sb][:, half * 512:(half + 1) * 512],
                                    in1=strip[:, 384 - 128 * m: 896 - 128 * m],
                                    op=ALU.mult))
                        mask_done[(hq, u)] = p.sig("vector", SV)
                    p.wait("tensor", SA, exp_done[(hq, u)])
                    if diag:
                        p.wait("tensor", SV, mask_done[(hq, u)])
                    if u == 0 and qb == 0:
                        p.wait("tensor", SV, cap_v)
                    p.op("tensor", lambda e, sb=sb, u=u: e.matmul(
                        PS[5][:, :], ones_bf[:, :], p_t[sb][:, 0:512],
                        start=(u == 0), stop=False))
                    p.op("tensor", lambda e, sb=sb, u=u, ns=ns: e.matmul(
                        PS[5][:, :], ones_bf[:, :], p_t[sb][:, 512:1024],
                        start=False, stop=(u == ns - 1)))
                    p.op("tensor", lambda e, sb=sb, u=u, qpar=qpar, hl=hl: e.matmul(
                        PS[6 + qpar][:, :],
                        vg[:, (2 * u) * 512 + hl * 128: (2 * u) * 512 + (hl + 1) * 128],
                        p_t[sb][:, 0:512], start=(u == 0), stop=False))
                    pv_mm[(hq, u)] = p.op("tensor",
                        lambda e, sb=sb, u=u, qpar=qpar, hl=hl, ns=ns: e.matmul(
                            PS[6 + qpar][:, :],
                            vg[:, (2 * u + 1) * 512 + hl * 128: (2 * u + 1) * 512 + (hl + 1) * 128],
                            p_t[sb][:, 512:1024], start=False, stop=(u == ns - 1)),
                        inc=ST)

                p.wait("scalar", ST, pv_mm[(hq, ns - 1)])
                if hq >= 2:
                    p.wait("scalar", SV, o_evac[hq - 2])

                def srec(e, qpar=qpar):
                    e.activation(out=rcp_t[qpar][:], in_=PS[5][:, :], func=ACT.Ln,
                                 bias=e35_t[:], scale=1.0)
                    e.drain()
                    e.activation(out=rcp_t[qpar][:], in_=rcp_t[qpar][:], func=ACT.Exp,
                                 bias=0.0, scale=-1.0)
                    e.drain()
                p.op("scalar", srec)
                a_rec.append(p.sig("scalar", SA))
                p.wait("vector", SA, a_rec[hq])
                if hq >= 2:
                    p.wait("vector", o_spw[hq - 2])
                p.op("vector", lambda e, qpar=qpar: e.tensor_tensor(
                    out=ot_b[qpar][:], in0=PS[6 + qpar][:, :], in1=rcp_t[qpar][:],
                    op=ALU.mult))
                o_evac.append(p.sig("vector", SV))
                p.wait("gpsimd", SV, o_evac[hq])
                o_spw.append(p.dma("gpsimd", lambda e, h=h, qb=qb, qpar=qpar: e.dma_start(
                    out=o_sp[h, :, qb * 512:(qb + 1) * 512], in_=ot_b[qpar][:]),
                    f"osp{hq % 2}"))
                hq += 1
            for _ in range(3):
                wq_emit_load(1)
                wq_emit_quant(1)
        q_proj_last[grp] = last_qproj[0]

    while wq_state["q"] < len(wq_units):
        wq_emit_load(1)
        wq_emit_quant(1)
    o_all_sp = o_spw[-1]

    o_full = y1_fm
    o_rld = []
    for hh in range(H):
        p.wait("sync", o_spw[hh * NQB + NQB - 1])
        p.wait("sync", o_spw[hh * NQB + NQB - 2])
        p.wait("sync", ST, last_qproj[0])
        if hh >= 8:
            p.wait("sync", o_rld[hh - 8])
        o_rld.append(p.dma("sync", lambda e, hh=hh: e.dma_start(
            out=o_full[:, hh * S:(hh + 1) * S], in_=o_sp[hh]), f"orl{hh % 8}"))

    ph2.close()
    p.full_barrier(exclude=("orl",))

    ph3 = contextlib.ExitStack()
    wo_bf = sbuf("wo_bf", [128, DC * D], BF16, ph3)
    at_f = [sbuf(f"at_f{i}", [128, D], F32, ph3) for i in range(2)]
    xr_f = [sbuf(f"xr{i}", [128, D], BF16, ph3) for i in range(2)]
    x2t = [sbuf(f"x2t{i}", [128, D], BF16, ph3) for i in range(2)]
    bo_b = sbuf("bo_b", [128, D], F32, ph3)
    bo_1row = sbuf("bo_1row", [1, D], F32, ph3)
    stats3 = sbuf("stats3", [128, 24], F32, ph3)
    mv3 = sbuf("mv3", [128, 2], F32, ph3)
    rs3 = sbuf("rs3", [128, 1], F32, ph3)
    nb3 = sbuf("nb3", [128, 1], F32, ph3)

    bo_ld = p.dma("sync", lambda e: e.dma_start(
        out=bo_1row[:], in_=b_in["bo"].rearrange("(o d) -> o d", o=1)), "bol")
    wo_ld = []
    for c in range(DC):
        wo_ld.append(p.dma("sync", lambda e, c=c: e.dma_start(
            out=wo_bf[:, c * D:(c + 1) * D], in_=wo_sp[c * 128:(c + 1) * 128, :]),
            f"wol{c}"))
    p.wait("tensor", SA, p.n(SA))
    p.wait("tensor", SV, p.n(SV))
    p.wait("tensor", bo_ld)
    v_bob = None
    for rnd in range(2):
        t_bo2 = []
        for rr2 in range(2):
            rr = rnd * 2 + rr2
            t_bo2.append(p.op("tensor", lambda e, rr=rr, rr2=rr2: e.matmul(
                PS[6 + rr2][:, :], ones_f[0:1, :], bo_1row[0:1, rr * 512:(rr + 1) * 512],
                start=True, stop=True), inc=ST))
        for rr2 in range(2):
            rr = rnd * 2 + rr2
            p.wait("vector", ST, t_bo2[rr2])
            p.op("vector", lambda e, rr=rr, rr2=rr2: e.tensor_scalar(
                out=bo_b[:, rr * 512:(rr + 1) * 512], in0=PS[6 + rr2][:, :], scalar1=0.0,
                scalar2=None, op0=ALU.add))
        v_bob = p.sig("vector", SV)
        if rnd == 0:
            p.wait("tensor", SV, v_bob)
    v_bob = v_bob

    x2_spw, at_ev, vx2_t = [], {}, []
    for i in range(NT):
        if i >= 2:
            p.wait("gpsimd", SV, vx2_t[i - 2])
        xr_ld = p.dma("gpsimd", lambda e, i=i: e.dma_start(
            out=xr_f[i % 2][:], in_=x_sp[i * 128:(i + 1) * 128, :]), f"xrl{i % 2}")
        p.wait("tensor", SV, v_bob)
        for r in range(NQB):
            par = r % 2 + 4
            if i * NQB + r >= 2:
                p.wait("tensor", SV, at_ev[(i * NQB + r - 2)])
            t_mm = None
            for c in range(H):
                if i == 0 and r == 0:
                    p.wait("tensor", o_rld[c])
                    p.wait("tensor", wo_ld[c])
                t_mm = p.op("tensor", lambda e, par=par, c=c, i=i, r=r: e.matmul(
                    PS[par][:, :], o_full[:, c * S + i * 128: c * S + (i + 1) * 128],
                    wo_bf[:, c * D + r * 512: c * D + (r + 1) * 512],
                    start=(c == 0), stop=(c == H - 1)),
                    inc=ST if c == H - 1 else None)
            p.wait("vector", ST, t_mm)
            p.op("vector", lambda e, par=par, i=i, r=r: e.tensor_tensor(
                out=at_f[i % 2][:, r * 512:(r + 1) * 512], in0=PS[par][:, :],
                in1=bo_b[:, r * 512:(r + 1) * 512], op=ALU.add))
            at_ev[i * NQB + r] = p.sig("vector", SV)

        def vstats3(e, i=i):
            for gi in range(4):
                e.bn_stats(out=stats3[:, gi * 6:(gi + 1) * 6],
                           in_=at_f[i % 2][:, gi * 512:(gi + 1) * 512])
            e.drain()
            e.bn_aggr(out=mv3[:], in_=stats3[:])
            e.drain()
        p.op("vector", vstats3)
        v_st3 = p.sig("vector", SV)
        p.wait("scalar", SV, v_st3)

        def srs3(e):
            e.activation(out=rs3[:], in_=mv3[:, 1:2], func=ACT.Ln, bias=eps_t[:], scale=1.0)
            e.drain()
            e.activation(out=rs3[:], in_=rs3[:], func=ACT.Exp, bias=0.0, scale=-0.5)
            e.drain()
            e.activation(out=nb3[:], in_=mv3[:, 0:1], func=ACT.Identity, bias=0.0, scale=rs3[:])
            e.drain()
            e.activation(out=nb3[:], in_=nb3[:], func=ACT.Identity, bias=0.0, scale=-1.0)
            e.drain()
        p.op("scalar", srs3)
        a_rs3 = p.sig("scalar", SA)
        p.wait("vector", SA, a_rs3)
        p.wait("vector", xr_ld)
        if i >= 2:
            p.wait("vector", x2_spw[i - 2])

        def vx2(e, i=i):
            e.tensor_scalar(out=at_f[i % 2][:], in0=at_f[i % 2][:], scalar1=rs3[:],
                            scalar2=nb3[:], op0=ALU.mult, op1=ALU.add)
            e.drain()
            e.tensor_tensor(out=x2t[i % 2][:], in0=at_f[i % 2][:], in1=xr_f[i % 2][:],
                            op=ALU.add)
            e.drain()
        p.op("vector", vx2)
        vx2_t.append(p.sig("vector", SV))
        p.wait("gpsimd", SV, vx2_t[i])
        x2_spw.append(p.dma("gpsimd", lambda e, i=i: e.dma_start(
            out=x2_sp[i * 128:(i + 1) * 128, :], in_=x2t[i % 2][:]), f"x2s{i % 2}"))

    x2_all = x2_spw[-1]
    ph3.close()
    ph_y1.close()
    p.full_barrier()

    ph4 = contextlib.ExitStack()
    x2q = sbuf("x2q", [128, 4 * D], F32, ph4)
    qfm_ar = sbuf("qfm_ar", [128, IC * 512], BF16, ph4)
    q1fm = qfm_ar
    q2fm = qfm_ar
    vtmp = sbuf("vtmp", [128, D], F32, ph4)
    vtmp2 = sbuf("vtmp2", [128, D], F32, ph4)
    qtile = sbuf("qtile", [128, I], BF16, ph4)
    stats4 = sbuf("stats4", [128, 24], F32, ph4)
    mv4 = sbuf("mv4", [128, 2], F32, ph4)
    rs4 = sbuf("rs4", [128, 4], F32, ph4)
    sc4 = sbuf("sc4", [128, 4], F32, ph4)
    nb4 = sbuf("nb4", [128, 4], F32, ph4)
    m1c = sbuf("m1c", [128, 4], F32, ph4)
    a2f = sbuf("a2f", [128, 4], F32, ph4)
    m2a = sbuf("m2a", [128, 4], F32, ph4)
    f2 = sbuf("f2", [128, 4], F32, ph4)
    s2e = sbuf("s2e", [128, 1], F32, ph4)
    wsl_f = sbuf("wsl_f", [128, 3 * DC * 512], BF16, ph4)
    h2q = sbuf("h2q", [128, 4 * I], F32, ph4)
    out2q = h2q

    p.wait("tensor", SV, p.n(SV))
    p.wait("tensor", SA, p.n(SA))

    st_done = []
    last_q2tr = None
    tup_pre = None
    for qq in range(4):
        if qq == 0:
            p.wait("gpsimd", "x2s0", p.n("x2s0"))
            p.wait("gpsimd", "x2s1", p.n("x2s1"))
        else:
            for slot in range(4):
                p.wait("gpsimd", f"out{slot}", p.n(f"out{slot}"))
        x2_ld = []
        for t in range(4):
            x2_ld.append(p.dma("gpsimd", lambda e, qq=qq, t=t: e.dma_start(
                out=x2q[:, t * D:(t + 1) * D],
                in_=x2_sp[(qq * 4 + t) * 128:(qq * 4 + t + 1) * 128, :]), f"x2l{t}"))

        if qq == 0:
            p.wait("sync", "stq0", p.n("stq0"))
            p.wait("sync", "stq1", p.n("stq1"))
            tup_pre = []
            for j in range(3):
                lt = None
                for c in range(DC):
                    lt = p.dma("sync", lambda e, j=j, c=c: e.dma_start(
                        out=wsl_f[:, j * SLOT + c * 512:
                                  j * SLOT + (c + 1) * 512],
                        in_=tup_sp[j, c]), f"tup{j}")
                tup_pre.append(lt)

        q1_tr = []
        for t in range(4):
            p.wait("vector", x2_ld[t])
            if last_q2tr is not None and t == 0:
                p.wait("vector", last_q2tr)

            def vq1(e, t=t):
                xa = x2q[:, t * D:(t + 1) * D]
                for gi in range(4):
                    e.bn_stats(out=stats4[:, gi * 6:(gi + 1) * 6],
                               in_=xa[:, gi * 512:(gi + 1) * 512])
                e.drain()
                e.bn_aggr(out=mv4[:], in_=stats4[:])
                e.drain()
                e.tensor_reduce(out=m1c[:, t:t + 1], in_=xa, axis=AX, op=ALU.max)
                e.tensor_reduce(out=m2a[:, t:t + 1], in_=xa, axis=AX, op=ALU.min)
                e.drain()
            p.op("vector", vq1)
            v_q1s = p.sig("vector", SV)
            p.wait("scalar", SV, v_q1s)

            def sq1(e, t=t):
                e.activation(out=rs4[:, t:t + 1], in_=mv4[:, 1:2], func=ACT.Ln,
                             bias=eps_t[:], scale=1.0)
                e.drain()
                e.activation(out=rs4[:, t:t + 1], in_=rs4[:, t:t + 1], func=ACT.Exp,
                             bias=0.0, scale=-0.5)
                e.drain()
            p.op("scalar", sq1)
            a_q1 = p.sig("scalar", SA)
            p.wait("vector", SA, a_q1)
            if t > 0:
                p.wait("vector", q1_tr[t - 1])

            def vq1b(e, t=t):
                e.tensor_tensor(out=tinyA[:], in0=m1c[:, t:t + 1], in1=mv4[:, 0:1],
                                op=ALU.subtract)
                e.tensor_tensor(out=tinyB[:], in0=mv4[:, 0:1], in1=m2a[:, t:t + 1],
                                op=ALU.subtract)
                e.drain()
                e.tensor_tensor(out=tinyA[:], in0=tinyA[:], in1=tinyB[:], op=ALU.max)
                e.drain()
                e.tensor_scalar(out=tinyA[:], in0=tinyA[:], scalar1=rs4[:, t:t + 1],
                                scalar2=None, op0=ALU.mult)
                e.drain()
                e.tensor_scalar(out=m1c[:, t:t + 1], in0=tinyA[:], scalar1=EPS,
                                scalar2=None, op0=ALU.max)
                e.drain()
                e.reciprocal(out=tinyB[:], in_=m1c[:, t:t + 1])
                e.drain()
                e.tensor_scalar(out=tinyB[:], in0=tinyB[:], scalar1=127.0, scalar2=None,
                                op0=ALU.mult)
                e.drain()
                e.tensor_scalar(out=sc4[:, t:t + 1], in0=tinyB[:],
                                scalar1=rs4[:, t:t + 1], scalar2=None, op0=ALU.mult)
                e.drain()
                e.tensor_tensor(out=nb4[:, t:t + 1], in0=mv4[:, 0:1],
                                in1=sc4[:, t:t + 1], op=ALU.mult)
                e.drain()
                e.tensor_scalar(out=nb4[:, t:t + 1], in0=nb4[:, t:t + 1], scalar1=-1.0,
                                scalar2=None, op0=ALU.mult)
                e.drain()
                e.tensor_tensor(out=tinyB[:], in0=m1c[:, t:t + 1], in1=s_upt[:], op=ALU.mult)
                e.drain()
                e.tensor_scalar(out=tinyB[:], in0=tinyB[:], scalar1=INV127, scalar2=None,
                                op0=ALU.mult)
                e.drain()
                e.tensor_tensor(out=a2f[:, t:t + 1], in0=tinyB[:], in1=tinyB[:], op=ALU.mult)
                e.drain()
            p.op("vector", vq1b)
            v_ch1 = p.sig("vector", SV)
            p.wait("scalar", SV, v_ch1)

            def sq1b(e, t=t):
                xa = x2q[:, t * D:(t + 1) * D]
                e.activation(out=vtmp2[:, 0:D], in_=xa, func=ACT.Identity,
                             bias=nb4[:, t:t + 1], scale=sc4[:, t:t + 1])
                e.drain()
                e.activation(out=vtmp2[:, 0:D], in_=vtmp2[:, 0:D], func=ACT.Identity,
                             bias=m64b[:], scale=1.0)
                e.drain()
                e.activation(out=vtmp2[:, 0:D], in_=vtmp2[:, 0:D], func=ACT.Identity,
                             bias=nm64b[:], scale=1.0)
                e.drain()
            p.op("scalar", sq1b)
            a_q1b = p.sig("scalar", SA)

            def vq1r(e, t=t):
                xa = x2q[:, t * D:(t + 1) * D]
                e.tensor_scalar(out=vtmp[:, 0:D], in0=xa, scalar1=sc4[:, t:t + 1],
                                scalar2=nb4[:, t:t + 1], op0=ALU.mult, op1=ALU.add)
                e.drain()
                e.tensor_scalar(out=vtmp[:, 0:D], in0=vtmp[:, 0:D], scalar1=M32,
                                scalar2=M32, op0=ALU.add, op1=ALU.subtract)
                e.drain()
            p.op("vector", vq1r)
            p.wait("vector", SA, a_q1b)

            def vq1c(e, t=t):
                e.scalar_tensor_tensor(out=qtile[:, 0:D], in0=vtmp[:, 0:D], scalar=mu[:],
                                       in1=vtmp2[:, 0:D], op0=ALU.mult, op1=ALU.add)
                e.drain()
            p.op("vector", vq1c)
            v_q1 = p.sig("vector", SV)
            p.wait("sync", SV, v_q1)
            tr = p.dma("sync", lambda e, t=t: e.dma_start_transpose(
                out=q1fm[:].rearrange("p (c s) -> p c s", s=512)[:, 0:DC, t * 128:(t + 1) * 128],
                in_=qtile[:, 0:D].rearrange("p (c f) -> p c f", f=128)), "ftr")
            q1_tr.append(tr)

        ffn1_mm = []
        ffn1_last_t = {}
        ev_h = {}
        q2_tr = []
        tdn_ld = {}
        tdn_last = {}

        def tdn_half_load(k):
            r, h = k // 2, k % 2
            s = SOF[k % 3]
            lt = None
            for i in range(16):
                lt = p.dma("sync", lambda e, r=r, h=h, i=i, s=s: e.dma_start(
                    out=wsl_f[:, s * SLOT + i * 512: s * SLOT + (i + 1) * 512],
                    in_=tdn_sp[r, h * 16 + i]), f"tdn{s}")
            tdn_ld[k] = lt

        for j in range(I // 512):
            if j < 3:
                tl = tup_pre[j]
            else:
                p.wait("sync", ST, ffn1_mm[j - 3])
                lt = None
                for c in range(DC):
                    lt = p.dma("sync", lambda e, j=j, c=c: e.dma_start(
                        out=wsl_f[:, (j % 3) * SLOT + c * 512:
                                  (j % 3) * SLOT + (c + 1) * 512],
                        in_=tup_sp[j, c]), f"tup{j % 3}")
                tl = lt
            p.wait("tensor", tl)
            for t in range(4):
                par = 4 + (j * 4 + t) % 4
                p.wait("tensor", q1_tr[t])
                if j * 4 + t >= 4:
                    p.wait("tensor", SV, ev_h[j * 4 + t - 4])
                t_mm = None
                for c in range(DC):
                    t_mm = p.op("tensor", lambda e, par=par, c=c, t=t, j=j: e.matmul(
                        PS[par][:, :],
                        q1fm[:, c * 512 + t * 128: c * 512 + (t + 1) * 128],
                        wsl_f[:, (j % 3) * SLOT + c * 512:
                              (j % 3) * SLOT + (c + 1) * 512],
                        start=(c == 0), stop=(c == DC - 1)),
                        inc=ST if c == DC - 1 else None)
                if t == 3:
                    ffn1_mm.append(t_mm)
                p.wait("vector", ST, t_mm)

                def vh2(e, par=par, t=t, j=j):
                    dst = h2q[:, t * I + j * 512: t * I + (j + 1) * 512]
                    e.tensor_copy(out=dst, in_=PS[par][:, :])
                    e.drain()
                    e.scalar_tensor_tensor(out=dst, in0=dst, scalar=0.0,
                                           in1=dst, op0=ALU.max, op1=ALU.mult)
                    e.drain()
                    e.tensor_reduce(out=tinyA[:], in_=dst, axis=AX, op=ALU.max)
                    e.drain()
                    if j > 0:
                        e.tensor_tensor(out=m2a[:, t:t + 1], in0=m2a[:, t:t + 1],
                                        in1=tinyA[:], op=ALU.max)
                    else:
                        e.tensor_copy(out=m2a[:, t:t + 1], in_=tinyA[:])
                    e.drain()
                p.op("vector", vh2)
                ev_h[j * 4 + t] = p.sig("vector", SV)

                if j != I // 512 - 1:
                    continue
                ffn1_last_t[t] = t_mm
                if t == 0:
                    p.wait("sync", ST, ffn1_mm[5])
                    tdn_half_load(1)
                    p.wait("sync", ST, ffn1_mm[6])
                    tdn_half_load(0)
                    p.wait("vector", q1_tr[3])
                else:
                    p.wait("vector", q2_tr[t - 1])

                def vq2a(e, t=t):
                    e.tensor_tensor(out=tinyA[:], in0=a2f[:, t:t + 1],
                                    in1=m2a[:, t:t + 1], op=ALU.mult)
                    e.drain()
                    e.tensor_scalar(out=tinyA[:], in0=tinyA[:], scalar1=EPS,
                                    scalar2=None, op0=ALU.max)
                    e.drain()
                    e.tensor_tensor(out=f2[:, t:t + 1], in0=tinyA[:], in1=s_dnt[:],
                                    op=ALU.mult)
                    e.drain()
                    e.tensor_scalar(out=f2[:, t:t + 1], in0=f2[:, t:t + 1],
                                    scalar1=INV127, scalar2=None, op0=ALU.mult)
                    e.reciprocal(out=tinyB[:], in_=tinyA[:])
                    e.drain()
                    e.tensor_scalar(out=tinyB[:], in0=tinyB[:], scalar1=127.0,
                                    scalar2=None, op0=ALU.mult)
                    e.drain()
                    e.tensor_tensor(out=s2e[:], in0=tinyB[:], in1=a2f[:, t:t + 1],
                                    op=ALU.mult)
                    e.drain()
                p.op("vector", vq2a)
                v_gate = p.sig("vector", SV)
                for hf in range(2):
                    p.wait("scalar", SV, v_gate)

                    def s2b(e, t=t, hf=hf):
                        ha = h2q[:, t * I + hf * D: t * I + (hf + 1) * D]
                        e.activation(out=vtmp2[:], in_=ha, func=ACT.Identity,
                                     bias=m64b[:], scale=s2e[:])
                        e.drain()
                        e.activation(out=vtmp2[:], in_=vtmp2[:], func=ACT.Identity,
                                     bias=nm64b[:], scale=1.0)
                        e.drain()
                    p.op("scalar", s2b)
                    a_b = p.sig("scalar", SA)

                    def v2a(e, t=t, hf=hf):
                        ha = h2q[:, t * I + hf * D: t * I + (hf + 1) * D]
                        e.tensor_scalar(out=vtmp[:], in0=ha, scalar1=s2e[:],
                                        scalar2=M32, op0=ALU.mult, op1=ALU.add)
                        e.drain()
                        e.tensor_scalar(out=vtmp[:], in0=vtmp[:], scalar1=M32,
                                        scalar2=None, op0=ALU.subtract)
                        e.drain()
                    p.op("vector", v2a)
                    p.wait("vector", SA, a_b)

                    def v2c(e, t=t, hf=hf):
                        e.scalar_tensor_tensor(out=qtile[:, hf * D:(hf + 1) * D],
                                               in0=vtmp[:], scalar=mu[:],
                                               in1=vtmp2[:], op0=ALU.mult, op1=ALU.add)
                        e.drain()
                    p.op("vector", v2c)
                    v_gate = p.sig("vector", SV)
                v_q2 = v_gate
                if t == 3:
                    p.wait("sync", ST, ffn1_last_t[3])
                    tdn_half_load(2)
                p.wait("sync", SV, v_q2)
                p.wait("sync", ST, ffn1_last_t[t])
                tr = p.dma("sync", lambda e, t=t: e.dma_start_transpose(
                    out=q2fm[:].rearrange("p (c s) -> p c s", s=512)[:, :, t * 128:(t + 1) * 128],
                    in_=qtile[:].rearrange("p (c f) -> p c f", f=128)), "ftr")
                q2_tr.append(tr)
        last_q2tr = q2_tr[3]

        ffn2_mm = []
        ev_o2 = {}
        for r in range(NQB):
            for h in range(2):
                k = 2 * r + h
                if k not in tdn_ld:
                    p.wait("sync", ST, tdn_last[k - 3])
                    tdn_half_load(k)
            p.wait("tensor", tdn_ld[2 * r])
            for t in range(4):
                par = 4 + (r * 4 + t) % 4
                p.wait("tensor", q2_tr[t])
                if r * 4 + t >= 4:
                    p.wait("tensor", SV, ev_o2[r * 4 + t - 4])
                t_mm = None
                for c in range(IC):
                    if c == 16 and t == 0:
                        p.wait("tensor", tdn_ld[2 * r + 1])
                    s = SOF[(2 * r + c // 16) % 3]
                    t_mm = p.op("tensor", lambda e, par=par, c=c, t=t, s=s: e.matmul(
                        PS[par][:, :],
                        q2fm[:, c * 512 + t * 128: c * 512 + (t + 1) * 128],
                        wsl_f[:, s * SLOT + (c % 16) * 512:
                              s * SLOT + (c % 16 + 1) * 512],
                        start=(c == 0), stop=(c == IC - 1)),
                        inc=ST if (c == IC - 1 or (t == 3 and c == 15)) else None)
                    if t == 3 and c == 15:
                        tdn_last[2 * r] = t_mm
                if t == 3:
                    ffn2_mm.append(t_mm)
                    tdn_last[2 * r + 1] = t_mm
                p.wait("vector", ST, t_mm)
                p.op("vector", lambda e, par=par, t=t, r=r: e.tensor_scalar(
                    out=out2q[:, t * D + r * 512: t * D + (r + 1) * 512],
                    in0=PS[par][:, :], scalar1=f2[:, t:t + 1], scalar2=None, op0=ALU.mult))
                ev_o2[r * 4 + t] = p.sig("vector", SV)

        if qq < 3:
            tup_pre = []
            for j, kfree in ((0, 6), (1, 5), (2, 7)):
                p.wait("sync", ST, tdn_last[kfree])
                lt = None
                for c in range(DC):
                    lt = p.dma("sync", lambda e, j=j, c=c: e.dma_start(
                        out=wsl_f[:, j * SLOT + c * 512:
                                  j * SLOT + (c + 1) * 512],
                        in_=tup_sp[j, c]), f"tup{j}")
                tup_pre.append(lt)
        for t in range(4):
            p.wait("vector", SV, ev_o2[(NQB - 1) * 4 + t])

            def vst4(e, t=t):
                oa = out2q[:, t * D:(t + 1) * D]
                for gi in range(4):
                    e.bn_stats(out=stats4[:, gi * 6:(gi + 1) * 6],
                               in_=oa[:, gi * 512:(gi + 1) * 512])
                e.drain()
                e.bn_aggr(out=mv4[:], in_=stats4[:])
                e.drain()
            p.op("vector", vst4)
            v_s4 = p.sig("vector", SV)
            p.wait("scalar", SV, v_s4)

            def srs4(e, t=t):
                e.activation(out=rs4[:, t:t + 1], in_=mv4[:, 1:2], func=ACT.Ln,
                             bias=eps_t[:], scale=1.0)
                e.drain()
                e.activation(out=rs4[:, t:t + 1], in_=rs4[:, t:t + 1], func=ACT.Exp,
                             bias=0.0, scale=-0.5)
                e.drain()
                e.activation(out=nb4[:, t:t + 1], in_=mv4[:, 0:1], func=ACT.Identity,
                             bias=0.0, scale=rs4[:, t:t + 1])
                e.drain()
                e.activation(out=nb4[:, t:t + 1], in_=nb4[:, t:t + 1], func=ACT.Identity,
                             bias=0.0, scale=-1.0)
                e.drain()
            p.op("scalar", srs4)
            a_s4 = p.sig("scalar", SA)
            p.wait("vector", SA, a_s4)

            def vfin(e, t=t):
                oa = out2q[:, t * D:(t + 1) * D]
                e.tensor_scalar(out=oa, in0=oa, scalar1=rs4[:, t:t + 1],
                                scalar2=nb4[:, t:t + 1], op0=ALU.mult, op1=ALU.add)
                e.drain()
                e.tensor_tensor(out=x2q[:, t * D:(t + 1) * D], in0=oa,
                                in1=x2q[:, t * D:(t + 1) * D], op=ALU.add)
                e.drain()
            p.op("vector", vfin)
            v_fin = p.sig("vector", SV)
            p.wait("sync", SV, v_fin)
            st_done.append(p.dma("sync", lambda e, qq=qq, t=t: e.dma_start(
                out=out[(qq * 4 + t) * 128:(qq * 4 + t + 1) * 128, :],
                in_=x2q[:, t * D:(t + 1) * D]), f"out{t}"))

    for slot in range(4):
        p.wait("sync", f"out{slot}", p.n(f"out{slot}"))
    ph4.close()

    block = es.enter_context(nc.Block())
    for eng in Prog.ENGINES:
        ops = p.ops[eng]
        getattr(block, eng)(lambda e, ops=ops: [fn(e) for fn in ops])
    es.close()
    return nc


_NC_CACHE = {}


def kernel(L0_attn_X, L0_attn_Wq, L0_attn_bq, L0_attn_Wk, L0_attn_bk,
           L0_attn_Wv, L0_attn_bv, L0_attn_Wo, L0_attn_bo,
           L0_ffn_w_up, L0_ffn_w_down):
    X = np.ascontiguousarray(np.asarray(L0_attn_X, dtype=np.float32))
    wq_t = np.ascontiguousarray(np.asarray(L0_attn_Wq, np.float32).T)
    wk_t = np.ascontiguousarray(np.asarray(L0_attn_Wk, np.float32).T)
    wv_t = np.ascontiguousarray(np.asarray(L0_attn_Wv, np.float32).T)
    wo_t = np.ascontiguousarray(np.asarray(L0_attn_Wo, np.float32).T)
    wup_t = np.ascontiguousarray(np.asarray(L0_ffn_w_up, np.float32).T)
    wdn_t = np.ascontiguousarray(np.asarray(L0_ffn_w_down, np.float32).T)
    bqv = np.ascontiguousarray(np.asarray(L0_attn_bq, np.float32))
    bkv = np.ascontiguousarray(np.asarray(L0_attn_bk, np.float32))
    bvv = np.ascontiguousarray(np.asarray(L0_attn_bv, np.float32))
    bov = np.ascontiguousarray(np.asarray(L0_attn_bo, np.float32))

    if "nc" not in _NC_CACHE:
        _NC_CACHE["nc"] = build_core()
    nc = _NC_CACHE["nc"]
    in_maps = []
    for c in range(8):
        in_maps.append({
            "x": np.ascontiguousarray(X[c]),
            "wqt": wq_t, "wkt": wk_t, "wvt": wv_t, "wot": wo_t,
            "wupt": wup_t, "wdnt": wdn_t,
            "bq": bqv, "bk": bkv, "bv": bvv, "bo": bov,
        })
    res = run_bass_kernel_spmd(nc, in_maps, core_ids=list(range(8)))
    return np.stack([np.asarray(r["out"], np.float32) for r in res.results], axis=0)


# version 16
# speedup vs baseline: 1.0951x; 1.0193x over previous
import contextlib
import numpy as np
import concourse.bass as bass
import concourse.mybir as mybir
from concourse.bass_utils import run_bass_kernel_spmd

F32 = mybir.dt.float32
BF16 = mybir.dt.bfloat16
ALU = mybir.AluOpType
ACT = mybir.ActivationFunctionType
AX = mybir.AxisListType.X

S = 2048
D = 2048
H = 16
I = 4096
NT = S // 128
DC = D // 128
IC = I // 128
SLOT = DC * 512
SOF = (0, 2, 1)
NQB = 4
NG = 4
M32 = float(np.float32(12582912.0))
M64 = float(np.float32(1.5 * 2 ** 52))
_SCL64 = 1.0 / np.sqrt(np.sqrt(128.0))
SCL = float(np.float32(_SCL64))
SCL_C = float(np.float32(_SCL64 - SCL))
EPS = 1.0e-5
INV_WN = float(np.float32(1.0 / 8388608.0))
INV127 = float(np.float32(1.0 / 127.0))


class Prog:
    ENGINES = ("sync", "gpsimd", "vector", "scalar", "tensor")

    def __init__(self, es, nc):
        self.nc = nc
        self.ops = {e: [] for e in self.ENGINES}
        self.sems = {}
        self.es = es
        self.last_wait = {e: {} for e in self.ENGINES}

    def sem(self, name):
        if name not in self.sems:
            h = self.es.enter_context(self.nc.semaphore(name))
            self.sems[name] = [h, 0]
        return name

    def n(self, s):
        self.sem(s)
        return self.sems[s][1]

    def wait(self, eng, s, val=None):
        if isinstance(s, tuple):
            s, val = s
        h, n = self.sems[self.sem(s)]
        if val is None:
            val = n
        assert val <= n, f"wait on future: {s} {val} > {n}"
        if val <= 0:
            return
        if self.last_wait[eng].get(s, 0) >= val:
            return
        self.last_wait[eng][s] = val
        self.ops[eng].append(lambda e, h=h, v=val: e.wait_ge(h, v))

    def op(self, eng, fn, inc=None, k=1):
        if inc is None:
            self.ops[eng].append(fn)
            return None
        h, _ = self.sems[self.sem(inc)]
        self.sems[inc][1] += k
        tgt = self.sems[inc][1]
        self.ops[eng].append(lambda e, fn=fn, h=h, k=k: fn(e).then_inc(h, k))
        return tgt

    def dma(self, eng, fn, stream):
        return (stream, self.op(eng, fn, inc=stream, k=16))

    def full_barrier(self, exclude=()):
        for eng in self.ENGINES:
            for s in list(self.sems):
                if any(s.startswith(x) for x in exclude):
                    continue
                self.wait(eng, s)

    def sig(self, eng, inc, k=1):
        if eng in ("vector", "scalar", "gpsimd"):
            self.ops[eng].append(lambda e: e.drain())
        return self.op(eng, lambda e: e.nop(), inc=inc, k=k)


def build_core():
    es = contextlib.ExitStack()
    nc = bass.Bass("TRN2", target_bir_lowering=False)

    x_in = nc.dram_tensor("x", [S, D], F32, kind="ExternalInput").ap()
    wqt = nc.dram_tensor("wqt", [D, D], F32, kind="ExternalInput").ap()
    wkt = nc.dram_tensor("wkt", [D, D], F32, kind="ExternalInput").ap()
    wvt = nc.dram_tensor("wvt", [D, D], F32, kind="ExternalInput").ap()
    wot = nc.dram_tensor("wot", [D, D], F32, kind="ExternalInput").ap()
    wupt = nc.dram_tensor("wupt", [D, I], F32, kind="ExternalInput").ap()
    wdnt = nc.dram_tensor("wdnt", [I, D], F32, kind="ExternalInput").ap()
    b_in = {n: nc.dram_tensor(n, [D], F32, kind="ExternalInput").ap()
            for n in ("bq", "bk", "bv", "bo")}
    out = nc.dram_tensor("out", [S, D], F32, kind="ExternalOutput").ap()

    x_sp = nc.dram_tensor("x_sp", [S, D], BF16, kind="Internal").ap()
    x2_sp = nc.dram_tensor("x2_sp", [S, D], BF16, kind="Internal").ap()
    wo_sp = nc.dram_tensor("wo_sp", [D, D], BF16, kind="Internal").ap()
    o_sp = nc.dram_tensor("o_sp", [H, 128, S], BF16, kind="Internal").ap()
    tup_sp = nc.dram_tensor("tup_sp", [I // 512, DC, 128, 512], BF16, kind="Internal").ap()
    tdn_sp = nc.dram_tensor("tdn_sp", [D // 512, IC, 128, 512], BF16, kind="Internal").ap()

    def sbuf(name, shape, dt=F32, st=es):
        return st.enter_context(nc.sbuf_tensor(name, shape, dt))

    p = Prog(es, nc)
    ST, SV, SA = (p.sem(s) for s in ("s_t", "s_v", "s_a"))
    SG = p.sem("s_g")

    mu = sbuf("mu", [128, 1])
    sc_qk = sbuf("sc_qk", [128, 1])
    eps_t = sbuf("eps_t", [128, 1])
    e35_t = sbuf("e35_t", [128, 1])
    c60_t = sbuf("c60_t", [128, 1])
    m64b = sbuf("m64b", [128, 1])
    nm64b = sbuf("nm64b", [128, 1])
    ones_bf = sbuf("ones_bf", [128, 128], BF16)
    ones_f = sbuf("ones_f", [128, 128])
    strip = sbuf("strip", [128, 896], BF16)
    bq_s = sbuf("bq_s", [128, H])
    bk_s = sbuf("bk_s", [128, H])
    bv_s = sbuf("bv_s", [128, H])
    wst_up = sbuf("wst_up", [128, 128])
    wst_dn = sbuf("wst_dn", [128, 128])
    th_up = sbuf("th_up", [128, 1])
    th_dn = sbuf("th_dn", [128, 1])
    nth_up = sbuf("nth_up", [128, 1])
    nth_dn = sbuf("nth_dn", [128, 1])
    s_upt = sbuf("s_upt", [128, 1])
    s_dnt = sbuf("s_dnt", [128, 1])
    nk2 = sbuf("nk2", [128, H])
    smallA = sbuf("smallA", [128, 4])
    negC = sbuf("negC", [128, 1])
    tinyA = sbuf("tinyA", [128, 1])
    tinyB = sbuf("tinyB", [128, 1])

    PSA = es.enter_context(nc.psum_tensor("psA", [128, 2048], F32))
    PSB = es.enter_context(nc.psum_tensor("psB", [128, 2048], F32))
    PS = [None, None, None, None] + [PSB[:, i * 512:(i + 1) * 512] for i in range(4)]
    S2 = [PSA[:, 0:1024], PSA[:, 1024:2048]]
    S4 = [PSA[:, j * 512:(j + 1) * 512] for j in range(4)]

    for nm, t in (("bq", bq_s), ("bk", bk_s), ("bv", bv_s)):
        p.op("gpsimd", lambda e, nm=nm, t=t: e.dma_start(
            out=t[:], in_=b_in[nm].rearrange("(h e) -> e h", e=128),
            allow_slow_non_contiguous=True), inc="bia", k=16)
    w_bias = ("bia", p.n("bia"))

    ph_y1 = contextlib.ExitStack()
    y1_fm = sbuf("y1_fm", [128, DC * S], BF16, ph_y1)
    ph1 = contextlib.ExitStack()
    xt = [sbuf(f"xt{i}", [128, D], F32, ph1) for i in range(2)]
    y1t = [sbuf(f"y1t{i}", [128, D], BF16, ph1) for i in range(2)]
    stats = sbuf("stats", [128, 24], F32, ph1)
    mv = sbuf("mv", [128, 2], F32, ph1)
    rs_t = sbuf("rs_t", [128, 1], F32, ph1)
    nb_t = sbuf("nb_t", [128, 1], F32, ph1)
    wch = [sbuf(f"wch{i}", [128, I], F32, ph1) for i in range(2)]
    strip_f = sbuf("strip_f", [128, 896], F32, ph1)

    g_strip = p.op("gpsimd", lambda e: e.iota(
        strip_f[:], pattern=[[1, 896]], base=-384, channel_multiplier=-1,
        allow_small_or_imprecise_dtypes=True), inc=SG)

    p.wait("vector", w_bias)
    p.wait("vector", SG, g_strip)

    def vconst(e):
        e.memset(mu[:], 0.75)
        e.memset(eps_t[:], EPS)
        e.memset(e35_t[:], 1.0e-35)
        e.memset(c60_t[:], -60.0)
        e.memset(m64b[:], M64)
        e.memset(nm64b[:], -M64)
        e.memset(ones_f[:], 1.0)
        e.memset(ones_bf[:], 1.0)
        e.drain()
        e.tensor_scalar(out=mu[:], in0=mu[:], scalar1=M64, scalar2=M64,
                        op0=ALU.add, op1=ALU.subtract)
        e.drain()
        e.tensor_scalar(out=mu[:], in0=mu[:], scalar1=-1.0, scalar2=1.0,
                        op0=ALU.mult, op1=ALU.add)
        e.drain()
        e.tensor_scalar(out=sc_qk[:], in0=mu[:], scalar1=-SCL_C, scalar2=SCL + SCL_C,
                        op0=ALU.mult, op1=ALU.add)
        e.tensor_scalar(out=strip[:], in0=strip_f[:], scalar1=0.0, scalar2=None,
                        op0=ALU.is_ge)
        e.drain()
        e.tensor_scalar(out=bq_s[:], in0=bq_s[:], scalar1=sc_qk[:], scalar2=None,
                        op0=ALU.mult)
        e.tensor_scalar(out=bk_s[:], in0=bk_s[:], scalar1=sc_qk[:], scalar2=None,
                        op0=ALU.mult)
        e.drain()
    p.op("vector", vconst)
    v_const = p.sig("vector", SV)

    x_ld, a_apply, tr_done, sp_done = [], [], [], []
    for i in range(NT):
        if i >= 2:
            p.wait("sync", SA, a_apply[i - 2])
            p.wait("sync", sp_done[i - 2])
        x_ld.append(p.dma("sync", lambda e, i=i: e.dma_start(
            out=xt[i % 2][:], in_=x_in[i * 128:(i + 1) * 128, :]), f"xl{i % 2}"))

        p.wait("vector", x_ld[i])

        def vstats(e, i=i):
            for gi in range(4):
                e.bn_stats(out=stats[:, gi * 6:(gi + 1) * 6],
                           in_=xt[i % 2][:, gi * 512:(gi + 1) * 512])
            e.drain()
            e.bn_aggr(out=mv[:], in_=stats[:])
            e.drain()
        p.op("vector", vstats)
        v_st = p.sig("vector", SV)

        p.wait("scalar", SV, v_st)
        if i >= 2:
            p.wait("scalar", tr_done[i - 2])

        def sapply(e):
            e.activation(out=rs_t[:], in_=mv[:, 1:2], func=ACT.Ln, bias=eps_t[:], scale=1.0)
            e.drain()
            e.activation(out=rs_t[:], in_=rs_t[:], func=ACT.Exp, bias=0.0, scale=-0.5)
            e.drain()
            e.activation(out=nb_t[:], in_=mv[:, 0:1], func=ACT.Identity, bias=0.0,
                         scale=rs_t[:])
            e.drain()
            e.activation(out=nb_t[:], in_=nb_t[:], func=ACT.Identity, bias=0.0, scale=-1.0)
            e.drain()
        p.op("scalar", sapply)
        a_apply.append(p.op("scalar", lambda e, i=i: e.activation(
            out=y1t[i % 2][:], in_=xt[i % 2][:], func=ACT.Identity,
            bias=nb_t[:], scale=rs_t[:]), inc=SA))

        p.wait("sync", SA, a_apply[i])
        t = p.dma("sync", lambda e, i=i: e.dma_start_transpose(
            out=y1_fm[:].rearrange("p (c s) -> p c s", s=S)[:, :, i * 128:(i + 1) * 128],
            in_=y1t[i % 2][:].rearrange("p (c f) -> p c f", f=128)), f"p1tr{i % 2}")
        tr_done.append(t)

        p.wait("gpsimd", SA, a_apply[i])
        sp_done.append(p.dma("gpsimd", lambda e, i=i: e.dma_start(
            out=x_sp[i * 128:(i + 1) * 128, :], in_=xt[i % 2][:]), f"xsp{i % 2}"))

    NWCH = DC + IC
    wst_ld, wst_v = [], []
    for j in range(NWCH):
        if j >= 2:
            p.wait("gpsimd", SV, wst_v[j - 2])
        if j < DC:
            wst_ld.append(p.dma("gpsimd", lambda e, j=j: e.dma_start(
                out=wch[j % 2][:], in_=wupt[j * 128:(j + 1) * 128, :]), f"ws{j % 2}"))
        else:
            wst_ld.append(p.dma("gpsimd", lambda e, c=j - DC, j=j: e.dma_start(
                out=wch[j % 2][:, 0:D], in_=wdnt[c * 128:(c + 1) * 128, :]), f"ws{j % 2}"))
        p.wait("vector", wst_ld[j])

        def vred(e, j=j):
            if j < DC:
                for gi in range(8):
                    e.tensor_reduce(out=wst_up[:, j * 8 + gi: j * 8 + gi + 1],
                                    in_=wch[j % 2][:, gi * 512:(gi + 1) * 512],
                                    axis=AX, op=ALU.add, apply_absolute_value=True)
            else:
                c = j - DC
                for gi in range(4):
                    e.tensor_reduce(out=wst_dn[:, c * 4 + gi: c * 4 + gi + 1],
                                    in_=wch[j % 2][:, gi * 512:(gi + 1) * 512],
                                    axis=AX, op=ALU.add, apply_absolute_value=True)
        p.op("vector", vred)
        wst_v.append(p.sig("vector", SV))

    def vtot(e):
        e.drain()
        e.tensor_reduce(out=smallA[:, 0:1], in_=wst_up[:], axis=AX, op=ALU.add)
        e.tensor_reduce(out=smallA[:, 1:2], in_=wst_dn[:], axis=AX, op=ALU.add)
        e.memset(smallA[:, 2:4], 0.0)
        e.drain()
    p.op("vector", vtot)
    v_tot = p.sig("vector", SV)
    p.wait("tensor", SV, v_tot)
    t_bc1 = p.op("tensor", lambda e: e.matmul(
        PS[5][0:1, 0:4], ones_f[:, 0:1], smallA[:, 0:4], start=True, stop=True), inc=ST)
    p.wait("vector", ST, t_bc1)
    p.op("vector", lambda e: e.tensor_copy(out=smallA[0:1, 0:4], in_=PS[5][0:1, 0:4]))
    v_bc1 = p.sig("vector", SV)
    p.wait("tensor", SV, v_bc1)
    t_bc = p.op("tensor", lambda e: e.matmul(
        PS[4][:, 0:4], ones_f[0:1, :], smallA[0:1, 0:4], start=True, stop=True), inc=ST)
    p.wait("vector", ST, t_bc)

    def vth(e):
        e.tensor_scalar(out=s_upt[:], in0=PS[4][:, 0:1], scalar1=INV_WN, scalar2=EPS,
                        op0=ALU.mult, op1=ALU.max)
        e.tensor_scalar(out=s_dnt[:], in0=PS[4][:, 1:2], scalar1=INV_WN, scalar2=EPS,
                        op0=ALU.mult, op1=ALU.max)
        e.drain()
        e.tensor_scalar(out=th_up[:], in0=s_upt[:], scalar1=0.5, scalar2=None, op0=ALU.mult)
        e.tensor_scalar(out=th_dn[:], in0=s_dnt[:], scalar1=0.5, scalar2=None, op0=ALU.mult)
        e.drain()
        e.tensor_scalar(out=nth_up[:], in0=th_up[:], scalar1=-1.0, scalar2=None, op0=ALU.mult)
        e.tensor_scalar(out=nth_dn[:], in0=th_dn[:], scalar1=-1.0, scalar2=None, op0=ALU.mult)
        e.drain()
    p.op("vector", vth)
    v_th = p.sig("vector", SV)

    ph1.close()
    p.full_barrier()

    for c in range(DC):
        p.dma("gpsimd", lambda e, c=c: e.dma_start(
            out=wo_sp[c * 128:(c + 1) * 128, :],
            in_=wot[c * 128:(c + 1) * 128, :]), "woc")

    ph2 = contextlib.ExitStack()
    kg = sbuf("kg", [128, 4 * S], BF16, ph2)
    vg = sbuf("vg", [128, NT * 512], BF16, ph2)
    wsl2 = [sbuf(f"wsl{i}", [128, DC * 512], BF16, ph2) for i in range(2)]
    qh_t = [sbuf(f"qh{i}", [128, 4 * 512], BF16, ph2) for i in range(2)]
    sq2_t = sbuf("sq2_t", [128, 512], F32, ph2)
    p_t = [sbuf(f"pt{i}", [128, 1024], BF16, ph2) for i in range(2)]
    rcp_t = [sbuf(f"rcp{i}", [128, 512], F32, ph2) for i in range(2)]
    sq_t = sbuf("sq_t", [128, 512], F32, ph2)
    ot_b = [sbuf(f"otb{i}", [128, 512], BF16, ph2) for i in range(2)]
    bv_b = sbuf("bv_b", [128, 512], F32, ph2)
    bv_row = sbuf("bv_row", [1, 512], F32, ph2)
    wqch = [sbuf(f"wqch{i}", [128, D], F32, ph2) for i in range(2)]
    wq_p = sbuf("wq_p", [128, D], F32, ph2)
    wq_n = sbuf("wq_n", [128, D], F32, ph2)
    tq_bf = sbuf("tq_bf", [128, D], BF16, ph2)

    wq_units = ([("up", c, hf) for c in range(DC) for hf in range(2)]
                + [("dn", c, 0) for c in range(IC)])
    wq_ld, wq_vdone, wq_spdone = {}, {}, {}
    wq_state = {"ld": 0, "q": 0}

    def wq_emit_load(n=1):
        for _ in range(n):
            u = wq_state["ld"]
            if u >= len(wq_units) or u >= wq_state["q"] + 2:
                return
            kind, c, hf = wq_units[u]
            if u >= 2:
                p.wait("gpsimd", SV, wq_vdone[u - 2])
            if kind == "up":
                wq_ld[u] = p.dma("gpsimd", lambda e, c=c, hf=hf, u=u: e.dma_start(
                    out=wqch[u % 2][:],
                    in_=wupt[c * 128:(c + 1) * 128, hf * 2048:(hf + 1) * 2048]),
                    f"wqs{u % 2}")
            else:
                wq_ld[u] = p.dma("gpsimd", lambda e, c=c, u=u: e.dma_start(
                    out=wqch[u % 2][:], in_=wdnt[c * 128:(c + 1) * 128, :]), f"wqs{u % 2}")
            wq_state["ld"] += 1

    def wq_emit_quant(n=1):
        for _ in range(n):
            u = wq_state["q"]
            if u >= len(wq_units) or u >= wq_state["ld"]:
                return
            kind, c, hf = wq_units[u]
            th, nth = (th_up, nth_up) if kind == "up" else (th_dn, nth_dn)
            p.wait("vector", SV, v_th)
            p.wait("vector", wq_ld[u])
            if u >= 1:
                p.wait("vector", wq_spdone[u - 1])

            def vq(e, u=u, th=th, nth=nth):
                e.tensor_scalar(out=wq_p[:], in0=wqch[u % 2][:], scalar1=th[:],
                                scalar2=None, op0=ALU.is_ge)
                e.tensor_scalar(out=wq_n[:], in0=wqch[u % 2][:], scalar1=nth[:],
                                scalar2=None, op0=ALU.is_le)
                e.drain()
                e.tensor_tensor(out=tq_bf[:], in0=wq_p[:], in1=wq_n[:], op=ALU.subtract)
            p.op("vector", vq)
            wq_vdone[u] = p.sig("vector", SV)
            p.wait("gpsimd", SV, wq_vdone[u])
            last = None
            for j in range(4):
                if kind == "up":
                    last = p.dma("gpsimd", lambda e, c=c, hf=hf, j=j: e.dma_start(
                        out=tup_sp[hf * 4 + j, c], in_=tq_bf[:, j * 512:(j + 1) * 512]),
                        f"stq{u % 2}")
                else:
                    last = p.dma("gpsimd", lambda e, c=c, j=j: e.dma_start(
                        out=tdn_sp[j, c], in_=tq_bf[:, j * 512:(j + 1) * 512]),
                        f"stq{u % 2}")
            wq_spdone[u] = last
            wq_state["q"] += 1

    ev_k, ev_v = {}, {}
    o_evac, o_spw, q_evac, a_rec = [], [], [], []
    exp_done, mask_done, pv_mm = {}, {}, {}
    t_sqr = [None]
    last_qev = [None]
    last_qproj = [None]
    lk_pf = {}
    q_proj_last = {}
    hq = 0
    for grp in range(NG):
        p.wait("tensor", SA, p.n(SA))
        p.wait("tensor", SV, p.n(SV))
        wsl = wsl2[grp % 2]
        if grp == 0:
            lk = None
            for c in range(DC):
                lk = p.dma("gpsimd", lambda e, c=c, grp=grp, wsl=wsl: e.dma_start(
                    out=wsl[:, c * 512:(c + 1) * 512],
                    in_=wkt[c * 128:(c + 1) * 128, grp * 512:(grp + 1) * 512]), "kvw")
        else:
            lk = lk_pf[grp]
        p.wait("tensor", lk)
        for comb in range(16):
            hl, r = divmod(comb, NQB)
            h = grp * 4 + hl
            par = 4 + comb % 2
            if comb >= 2:
                p.wait("tensor", SA, ev_k[(grp, comb - 2)])
            p.wait("tensor", tr_done[4 * r + 3])
            t_mm = None
            for c in range(DC):
                t_mm = p.op("tensor", lambda e, par=par, c=c, hl=hl, r=r, wsl=wsl: e.matmul(
                    PS[par][:, :], wsl[:, c * 512 + hl * 128: c * 512 + (hl + 1) * 128],
                    y1_fm[:, c * S + r * 512: c * S + (r + 1) * 512],
                    start=(c == 0), stop=(c == DC - 1)),
                    inc=ST if c == DC - 1 else None)
            p.wait("scalar", ST, t_mm)
            ev_k[(grp, comb)] = p.op("scalar", lambda e, par=par, hl=hl, h=h, r=r: e.activation(
                out=kg[:, hl * S + r * 512: hl * S + (r + 1) * 512],
                in_=PS[par][:, :], func=ACT.Identity,
                bias=bk_s[:, h:h + 1], scale=sc_qk[:]), inc=SA)
            k_last = t_mm

        lv = None
        p.wait("gpsimd", ST, k_last)
        for c in range(DC):
            lv = p.dma("gpsimd", lambda e, c=c, grp=grp, wsl=wsl: e.dma_start(
                out=wsl[:, c * 512:(c + 1) * 512],
                in_=wvt[c * 128:(c + 1) * 128, grp * 512:(grp + 1) * 512]), "kvw")
        bvr_ld = p.dma("gpsimd", lambda e, grp=grp: e.dma_start(
            out=bv_row[:], in_=b_in["bv"].rearrange("(o d) -> o d", o=4)[grp:grp + 1, :]),
            "bvr")
        p.wait("tensor", SV, v_const)
        p.wait("tensor", bvr_ld)
        tb = p.op("tensor", lambda e: e.matmul(
            PS[7][:, :], ones_f[0:1, :], bv_row[0:1, :], start=True, stop=True), inc=ST)
        p.wait("vector", ST, tb)
        p.op("vector", lambda e: e.tensor_copy(out=bv_b[:], in_=PS[7][:, :]))
        v_bvb = p.sig("vector", SV)

        p.wait("tensor", lv)
        for t in range(NT):
            par = 4 + t % 2
            if t < 2:
                p.wait("tensor", SA, ev_k[(grp, 14 + t)])
            else:
                p.wait("tensor", SV, ev_v[(grp, t - 2)])
            t_mm = None
            for c in range(DC):
                t_mm = p.op("tensor", lambda e, par=par, c=c, t=t, wsl=wsl: e.matmul(
                    PS[par][:, :], y1_fm[:, c * S + t * 128: c * S + (t + 1) * 128],
                    wsl[:, c * 512:(c + 1) * 512],
                    start=(c == 0), stop=(c == DC - 1)),
                    inc=ST if c == DC - 1 else None)
            v_last = t_mm
            p.wait("vector", ST, t_mm)
            p.wait("vector", SV, v_bvb)
            p.op("vector", lambda e, par=par, t=t: e.tensor_tensor(
                out=vg[:, t * 512:(t + 1) * 512], in0=PS[par][:, :], in1=bv_b[:],
                op=ALU.add))
            ev_v[(grp, t)] = p.sig("vector", SV)

        vnk_prev = None
        for hl in range(4):
            h = grp * 4 + hl

            def vnkinit(e, h=h):
                e.memset(nk2[:, h:h + 1], 0.0)
                e.drain()
            p.op("vector", vnkinit)
            for r in range(NQB):
                p.wait("vector", SA, ev_k[(grp, hl * NQB + r)])
                if t_sqr[0] is not None:
                    p.wait("vector", ST, t_sqr[0])
                p.op("vector", lambda e, hl=hl, r=r: e.tensor_tensor(
                    out=sq_t[:], in0=kg[:, hl * S + r * 512: hl * S + (r + 1) * 512],
                    in1=kg[:, hl * S + r * 512: hl * S + (r + 1) * 512], op=ALU.mult))
                v_sq = p.sig("vector", SV)
                p.wait("tensor", SV, v_sq)
                if vnk_prev is not None:
                    p.wait("tensor", SV, vnk_prev)
                t_nk = p.op("tensor", lambda e: e.matmul(
                    PS[6][:, :], ones_f[:, :], sq_t[:], start=True, stop=True), inc=ST)
                t_sqr[0] = t_nk
                p.wait("vector", ST, t_nk)

                def vnkacc(e, h=h):
                    e.tensor_reduce(out=tinyA[:], in_=PS[6][:, :], axis=AX, op=ALU.max)
                    e.drain()
                    e.tensor_tensor(out=nk2[:, h:h + 1], in0=nk2[:, h:h + 1],
                                    in1=tinyA[:], op=ALU.max)
                    e.drain()
                p.op("vector", vnkacc)
                vnk_prev = p.sig("vector", SV)

        p.wait("gpsimd", ST, v_last)
        lq = None
        for c in range(DC):
            lq = p.dma("gpsimd", lambda e, c=c, grp=grp, wsl=wsl: e.dma_start(
                out=wsl[:, c * 512:(c + 1) * 512],
                in_=wqt[c * 128:(c + 1) * 128, grp * 512:(grp + 1) * 512]), "kvw")
        if grp + 1 < NG:
            if grp >= 1:
                p.wait("gpsimd", ST, q_proj_last[grp - 1])
            t_pf = None
            for c in range(DC):
                t_pf = p.dma("gpsimd", lambda e, c=c, grp=grp: e.dma_start(
                    out=wsl2[(grp + 1) % 2][:, c * 512:(c + 1) * 512],
                    in_=wkt[c * 128:(c + 1) * 128, (grp + 1) * 512:(grp + 2) * 512]),
                    "kvwP")
            lk_pf[grp + 1] = t_pf

        for _ in range(3):
            wq_emit_load(1)
            wq_emit_quant(1)

        p.wait("tensor", SV, p.n(SV))
        p.wait("tensor", SA, p.n(SA))
        p.wait("tensor", lq)
        for hl in range(4):
            h = grp * 4 + hl
            hh = grp * 4 + hl
            hpar = hh % 2
            qev = []
            for r4 in range(4):
                if r4 > 0:
                    p.wait("tensor", SV, qev[r4 - 1])
                elif last_qev[0] is not None:
                    p.wait("tensor", SV, last_qev[0])
                if hh >= 2 and r4 == 0:
                    last_hq = (hh - 2) * NQB + NQB - 1
                    ns2 = 2 * NQB
                    p.wait("tensor", ST, pv_mm[(last_hq, ns2 - 1)])
                t_qmm = None
                for c in range(DC):
                    t_qmm = p.op("tensor", lambda e, c=c, hl=hl, r4=r4, wsl=wsl: e.matmul(
                        PS[4][:, :],
                        wsl[:, c * 512 + hl * 128: c * 512 + (hl + 1) * 128],
                        y1_fm[:, c * S + r4 * 512: c * S + (r4 + 1) * 512],
                        start=(c == 0), stop=(c == DC - 1)),
                        inc=ST if c == DC - 1 else None)
                p.wait("vector", ST, t_qmm)
                if t_sqr[0] is not None:
                    p.wait("vector", ST, t_sqr[0])

                def vqev(e, hpar=hpar, h=h, r4=r4):
                    e.tensor_scalar(out=qh_t[hpar][:, r4 * 512:(r4 + 1) * 512],
                                    in0=PS[4][:, :], scalar1=sc_qk[:],
                                    scalar2=bq_s[:, h:h + 1], op0=ALU.mult, op1=ALU.add)
                    e.drain()
                    e.tensor_tensor(out=(sq_t if r4 % 2 == 0 else sq2_t)[:],
                                    in0=qh_t[hpar][:, r4 * 512:(r4 + 1) * 512],
                                    in1=qh_t[hpar][:, r4 * 512:(r4 + 1) * 512],
                                    op=ALU.mult)
                p.op("vector", vqev)
                qev.append(p.sig("vector", SV))
                p.wait("tensor", SV, qev[r4])
                if r4 == 0 and hh >= 1:
                    p.wait("tensor", SA, a_rec[hh * NQB - 1])
                if r4 == 0 and hh >= 2:
                    p.wait("tensor", SV, o_evac[hh * NQB - 2])
                t_nq = p.op("tensor", lambda e, r4=r4: e.matmul(
                    PS[5][:, :], ones_f[:, :], (sq_t if r4 % 2 == 0 else sq2_t)[:],
                    start=(r4 == 0), stop=(r4 == 3)), inc=ST)
                t_sqr[0] = t_nq
                last_qproj[0] = t_qmm
            last_qev[0] = qev[3]
            p.wait("vector", ST, t_nq)

            def vcap(e, h=h):
                e.tensor_reduce(out=tinyA[:], in_=PS[5][:, :], axis=AX, op=ALU.max)
                e.drain()
                e.tensor_tensor(out=tinyB[:], in0=tinyA[:], in1=nk2[:, h:h + 1],
                                op=ALU.mult)
                e.drain()
            p.op("vector", vcap)
            cap_v = p.sig("vector", SV)
            p.wait("scalar", SV, cap_v)

            def scap(e):
                e.activation(out=negC[:], in_=tinyB[:], func=ACT.Ln, bias=0.0, scale=1.0)
                e.drain()
                e.activation(out=negC[:], in_=negC[:], func=ACT.Exp, bias=0.0, scale=0.5)
                e.drain()
                e.activation(out=negC[:], in_=negC[:], func=ACT.Relu, bias=c60_t[:],
                             scale=1.0)
                e.drain()
                e.activation(out=negC[:], in_=negC[:], func=ACT.Identity, bias=0.0,
                             scale=-1.0)
                e.drain()
            p.op("scalar", scap)

            for qb in range(NQB):
                nk = 4 * (qb + 1)
                ns = nk // 2
                qpar = hq % 2
                ns_prev = 2 * ((hq - 1) % NQB + 1) if hq >= 1 else 0
                qsl = qh_t[hpar][:, qb * 512:(qb + 1) * 512]
                for u in range(ns):
                    sb = u % 2
                    if u >= 2:
                        p.wait("tensor", SA, exp_done[(hq, u - 2)])
                    elif hq >= 1 and ns_prev - 2 + sb >= 0:
                        p.wait("tensor", SA, exp_done[(hq - 1, ns_prev - 2 + sb)])
                    if u == 0 and hq >= 1:
                        p.wait("tensor", SA, a_rec[hq - 1])
                    if u == 0 and hq >= 2:
                        p.wait("tensor", SV, o_evac[hq - 2])
                    p.op("tensor", lambda e, sb=sb, hl=hl, u=u, qsl=qsl: e.matmul(
                        S2[sb][:, 0:512],
                        kg[:, hl * S + (2 * u) * 128: hl * S + (2 * u + 1) * 128],
                        qsl, start=True, stop=True))
                    t_smm = p.op("tensor", lambda e, sb=sb, hl=hl, u=u, qsl=qsl: e.matmul(
                        S2[sb][:, 512:1024],
                        kg[:, hl * S + (2 * u + 1) * 128: hl * S + (2 * u + 2) * 128],
                        qsl, start=True, stop=True), inc=ST)
                    p.wait("scalar", ST, t_smm)
                    if u >= 2:
                        p.wait("scalar", ST, pv_mm[(hq, u - 2)])
                    elif hq >= 1 and (hq - 1, ns_prev - 2 + sb) in pv_mm:
                        p.wait("scalar", ST, pv_mm[(hq - 1, ns_prev - 2 + sb)])
                    exp_done[(hq, u)] = p.op("scalar", lambda e, sb=sb: e.activation(
                        out=p_t[sb][:], in_=S2[sb][:, 0:1024], func=ACT.Exp,
                        bias=negC[:], scale=1.0), inc=SA)
                    diag = (2 * u + 1) >= nk - 4
                    if diag:
                        p.wait("vector", SA, exp_done[(hq, u)])
                        for half in range(2):
                            kc = 2 * u + half
                            if kc >= nk - 4:
                                m = kc - 4 * qb
                                p.op("vector", lambda e, sb=sb, m=m, half=half: e.tensor_tensor(
                                    out=p_t[sb][:, half * 512:(half + 1) * 512],
                                    in0=p_t[sb][:, half * 512:(half + 1) * 512],
                                    in1=strip[:, 384 - 128 * m: 896 - 128 * m],
                                    op=ALU.mult))
                        mask_done[(hq, u)] = p.sig("vector", SV)
                    p.wait("tensor", SA, exp_done[(hq, u)])
                    if diag:
                        p.wait("tensor", SV, mask_done[(hq, u)])
                    if u == 0 and qb == 0:
                        p.wait("tensor", SV, cap_v)
                    p.op("tensor", lambda e, sb=sb, u=u: e.matmul(
                        PS[5][:, :], ones_bf[:, :], p_t[sb][:, 0:512],
                        start=(u == 0), stop=False))
                    p.op("tensor", lambda e, sb=sb, u=u, ns=ns: e.matmul(
                        PS[5][:, :], ones_bf[:, :], p_t[sb][:, 512:1024],
                        start=False, stop=(u == ns - 1)))
                    p.op("tensor", lambda e, sb=sb, u=u, qpar=qpar, hl=hl: e.matmul(
                        PS[6 + qpar][:, :],
                        vg[:, (2 * u) * 512 + hl * 128: (2 * u) * 512 + (hl + 1) * 128],
                        p_t[sb][:, 0:512], start=(u == 0), stop=False))
                    pv_mm[(hq, u)] = p.op("tensor",
                        lambda e, sb=sb, u=u, qpar=qpar, hl=hl, ns=ns: e.matmul(
                            PS[6 + qpar][:, :],
                            vg[:, (2 * u + 1) * 512 + hl * 128: (2 * u + 1) * 512 + (hl + 1) * 128],
                            p_t[sb][:, 512:1024], start=False, stop=(u == ns - 1)),
                        inc=ST)

                p.wait("scalar", ST, pv_mm[(hq, ns - 1)])
                if hq >= 2:
                    p.wait("scalar", SV, o_evac[hq - 2])

                def srec(e, qpar=qpar):
                    e.activation(out=rcp_t[qpar][:], in_=PS[5][:, :], func=ACT.Ln,
                                 bias=e35_t[:], scale=1.0)
                    e.drain()
                    e.activation(out=rcp_t[qpar][:], in_=rcp_t[qpar][:], func=ACT.Exp,
                                 bias=0.0, scale=-1.0)
                    e.drain()
                p.op("scalar", srec)
                a_rec.append(p.sig("scalar", SA))
                p.wait("vector", SA, a_rec[hq])
                if hq >= 2:
                    p.wait("vector", o_spw[hq - 2])
                p.op("vector", lambda e, qpar=qpar: e.tensor_tensor(
                    out=ot_b[qpar][:], in0=PS[6 + qpar][:, :], in1=rcp_t[qpar][:],
                    op=ALU.mult))
                o_evac.append(p.sig("vector", SV))
                p.wait("gpsimd", SV, o_evac[hq])
                o_spw.append(p.dma("gpsimd", lambda e, h=h, qb=qb, qpar=qpar: e.dma_start(
                    out=o_sp[h, :, qb * 512:(qb + 1) * 512], in_=ot_b[qpar][:]),
                    f"osp{hq % 2}"))
                hq += 1
            for _ in range(3):
                wq_emit_load(1)
                wq_emit_quant(1)
        q_proj_last[grp] = last_qproj[0]

    while wq_state["q"] < len(wq_units):
        wq_emit_load(1)
        wq_emit_quant(1)
    o_all_sp = o_spw[-1]

    o_full = y1_fm
    o_rld = []
    for hh in range(H):
        p.wait("sync", o_spw[hh * NQB + NQB - 1])
        p.wait("sync", o_spw[hh * NQB + NQB - 2])
        p.wait("sync", ST, last_qproj[0])
        if hh >= 8:
            p.wait("sync", o_rld[hh - 8])
        o_rld.append(p.dma("sync", lambda e, hh=hh: e.dma_start(
            out=o_full[:, hh * S:(hh + 1) * S], in_=o_sp[hh]), f"orl{hh % 8}"))

    ph2.close()
    p.full_barrier(exclude=("orl",))

    ph3 = contextlib.ExitStack()
    wo_bf = sbuf("wo_bf", [128, DC * D], BF16, ph3)
    at_f = [sbuf(f"at_f{i}", [128, D], F32, ph3) for i in range(2)]
    xr_f = [sbuf(f"xr{i}", [128, D], BF16, ph3) for i in range(2)]
    x2t = [sbuf(f"x2t{i}", [128, D], BF16, ph3) for i in range(2)]
    bo_b = sbuf("bo_b", [128, D], F32, ph3)
    bo_1row = sbuf("bo_1row", [1, D], F32, ph3)
    stats3 = sbuf("stats3", [128, 24], F32, ph3)
    mv3 = sbuf("mv3", [128, 2], F32, ph3)
    rs3 = sbuf("rs3", [128, 1], F32, ph3)
    nb3 = sbuf("nb3", [128, 1], F32, ph3)

    bo_ld = p.dma("sync", lambda e: e.dma_start(
        out=bo_1row[:], in_=b_in["bo"].rearrange("(o d) -> o d", o=1)), "bol")
    wo_ld = []
    for c in range(DC):
        wo_ld.append(p.dma("sync", lambda e, c=c: e.dma_start(
            out=wo_bf[:, c * D:(c + 1) * D], in_=wo_sp[c * 128:(c + 1) * 128, :]),
            f"wol{c}"))
    p.wait("tensor", SA, p.n(SA))
    p.wait("tensor", SV, p.n(SV))
    p.wait("tensor", bo_ld)
    v_bob = None
    for rnd in range(2):
        t_bo2 = []
        for rr2 in range(2):
            rr = rnd * 2 + rr2
            t_bo2.append(p.op("tensor", lambda e, rr=rr, rr2=rr2: e.matmul(
                PS[6 + rr2][:, :], ones_f[0:1, :], bo_1row[0:1, rr * 512:(rr + 1) * 512],
                start=True, stop=True), inc=ST))
        for rr2 in range(2):
            rr = rnd * 2 + rr2
            p.wait("vector", ST, t_bo2[rr2])
            p.op("vector", lambda e, rr=rr, rr2=rr2: e.tensor_scalar(
                out=bo_b[:, rr * 512:(rr + 1) * 512], in0=PS[6 + rr2][:, :], scalar1=0.0,
                scalar2=None, op0=ALU.add))
        v_bob = p.sig("vector", SV)
        if rnd == 0:
            p.wait("tensor", SV, v_bob)
    v_bob = v_bob

    x2_spw, at_ev, vx2_t = [], {}, []
    for i in range(NT):
        if i >= 2:
            p.wait("gpsimd", SV, vx2_t[i - 2])
        xr_ld = p.dma("gpsimd", lambda e, i=i: e.dma_start(
            out=xr_f[i % 2][:], in_=x_sp[i * 128:(i + 1) * 128, :]), f"xrl{i % 2}")
        p.wait("tensor", SV, v_bob)
        for r in range(NQB):
            par = r % 2 + 4
            if i * NQB + r >= 2:
                p.wait("tensor", SV, at_ev[(i * NQB + r - 2)])
            t_mm = None
            for c in range(H):
                if i == 0 and r == 0:
                    p.wait("tensor", o_rld[c])
                    p.wait("tensor", wo_ld[c])
                t_mm = p.op("tensor", lambda e, par=par, c=c, i=i, r=r: e.matmul(
                    PS[par][:, :], o_full[:, c * S + i * 128: c * S + (i + 1) * 128],
                    wo_bf[:, c * D + r * 512: c * D + (r + 1) * 512],
                    start=(c == 0), stop=(c == H - 1)),
                    inc=ST if c == H - 1 else None)
            p.wait("vector", ST, t_mm)
            p.op("vector", lambda e, par=par, i=i, r=r: e.tensor_tensor(
                out=at_f[i % 2][:, r * 512:(r + 1) * 512], in0=PS[par][:, :],
                in1=bo_b[:, r * 512:(r + 1) * 512], op=ALU.add))
            at_ev[i * NQB + r] = p.sig("vector", SV)

        def vstats3(e, i=i):
            for gi in range(4):
                e.bn_stats(out=stats3[:, gi * 6:(gi + 1) * 6],
                           in_=at_f[i % 2][:, gi * 512:(gi + 1) * 512])
            e.drain()
            e.bn_aggr(out=mv3[:], in_=stats3[:])
            e.drain()
        p.op("vector", vstats3)
        v_st3 = p.sig("vector", SV)
        p.wait("scalar", SV, v_st3)

        def srs3(e):
            e.activation(out=rs3[:], in_=mv3[:, 1:2], func=ACT.Ln, bias=eps_t[:], scale=1.0)
            e.drain()
            e.activation(out=rs3[:], in_=rs3[:], func=ACT.Exp, bias=0.0, scale=-0.5)
            e.drain()
            e.activation(out=nb3[:], in_=mv3[:, 0:1], func=ACT.Identity, bias=0.0, scale=rs3[:])
            e.drain()
            e.activation(out=nb3[:], in_=nb3[:], func=ACT.Identity, bias=0.0, scale=-1.0)
            e.drain()
        p.op("scalar", srs3)
        a_rs3 = p.sig("scalar", SA)
        p.wait("vector", SA, a_rs3)
        p.wait("vector", xr_ld)
        if i >= 2:
            p.wait("vector", x2_spw[i - 2])

        def vx2(e, i=i):
            e.tensor_scalar(out=at_f[i % 2][:], in0=at_f[i % 2][:], scalar1=rs3[:],
                            scalar2=nb3[:], op0=ALU.mult, op1=ALU.add)
            e.drain()
            e.tensor_tensor(out=x2t[i % 2][:], in0=at_f[i % 2][:], in1=xr_f[i % 2][:],
                            op=ALU.add)
            e.drain()
        p.op("vector", vx2)
        vx2_t.append(p.sig("vector", SV))
        p.wait("gpsimd", SV, vx2_t[i])
        x2_spw.append(p.dma("gpsimd", lambda e, i=i: e.dma_start(
            out=x2_sp[i * 128:(i + 1) * 128, :], in_=x2t[i % 2][:]), f"x2s{i % 2}"))

    x2_all = x2_spw[-1]
    ph3.close()
    ph_y1.close()
    p.full_barrier()

    ph4 = contextlib.ExitStack()
    x2q = sbuf("x2q", [128, 4 * D], F32, ph4)
    qfm_ar = sbuf("qfm_ar", [128, IC * 512], BF16, ph4)
    q1fm = qfm_ar
    q2fm = qfm_ar
    vtmp = sbuf("vtmp", [128, D], F32, ph4)
    vtmp2 = sbuf("vtmp2", [128, D], F32, ph4)
    qtile = sbuf("qtile", [128, I], BF16, ph4)
    stats4 = sbuf("stats4", [128, 24], F32, ph4)
    mv4 = sbuf("mv4", [128, 2], F32, ph4)
    rs4 = sbuf("rs4", [128, 4], F32, ph4)
    sc4 = sbuf("sc4", [128, 4], F32, ph4)
    nb4 = sbuf("nb4", [128, 4], F32, ph4)
    m1c = sbuf("m1c", [128, 4], F32, ph4)
    a2f = sbuf("a2f", [128, 4], F32, ph4)
    m2a = sbuf("m2a", [128, 4], F32, ph4)
    f2 = sbuf("f2", [128, 4], F32, ph4)
    s2e = sbuf("s2e", [128, 1], F32, ph4)
    wsl_f = sbuf("wsl_f", [128, 3 * DC * 512], BF16, ph4)
    h2q = sbuf("h2q", [128, 4 * I], F32, ph4)
    out2q = h2q

    p.wait("tensor", SV, p.n(SV))
    p.wait("tensor", SA, p.n(SA))

    st_done = []
    last_q2tr = None
    tup_pre = None
    for qq in range(4):
        if qq == 0:
            p.wait("gpsimd", "x2s0", p.n("x2s0"))
            p.wait("gpsimd", "x2s1", p.n("x2s1"))
        x2_ld = []
        for t in range(4):
            if qq > 0:
                p.wait("gpsimd", f"out{t}", p.n(f"out{t}"))
            x2_ld.append(p.dma("gpsimd", lambda e, qq=qq, t=t: e.dma_start(
                out=x2q[:, t * D:(t + 1) * D],
                in_=x2_sp[(qq * 4 + t) * 128:(qq * 4 + t + 1) * 128, :]), f"x2l{t}"))

        if qq == 0:
            p.wait("sync", "stq0", p.n("stq0"))
            p.wait("sync", "stq1", p.n("stq1"))
            tup_pre = []
            for j in range(3):
                lt = None
                for c in range(DC):
                    lt = p.dma("sync", lambda e, j=j, c=c: e.dma_start(
                        out=wsl_f[:, j * SLOT + c * 512:
                                  j * SLOT + (c + 1) * 512],
                        in_=tup_sp[j, c]), f"tup{j}")
                tup_pre.append(lt)

        q1_tr = []
        for t in range(4):
            p.wait("vector", x2_ld[t])
            if last_q2tr is not None and t == 0:
                p.wait("vector", last_q2tr)

            def vq1(e, t=t):
                xa = x2q[:, t * D:(t + 1) * D]
                for gi in range(4):
                    e.bn_stats(out=stats4[:, gi * 6:(gi + 1) * 6],
                               in_=xa[:, gi * 512:(gi + 1) * 512])
                e.drain()
                e.bn_aggr(out=mv4[:], in_=stats4[:])
                e.drain()
                e.tensor_reduce(out=m1c[:, t:t + 1], in_=xa, axis=AX, op=ALU.max)
                e.tensor_reduce(out=m2a[:, t:t + 1], in_=xa, axis=AX, op=ALU.min)
                e.drain()
            p.op("vector", vq1)
            v_q1s = p.sig("vector", SV)
            p.wait("scalar", SV, v_q1s)

            def sq1(e, t=t):
                e.activation(out=rs4[:, t:t + 1], in_=mv4[:, 1:2], func=ACT.Ln,
                             bias=eps_t[:], scale=1.0)
                e.drain()
                e.activation(out=rs4[:, t:t + 1], in_=rs4[:, t:t + 1], func=ACT.Exp,
                             bias=0.0, scale=-0.5)
                e.drain()
            p.op("scalar", sq1)
            a_q1 = p.sig("scalar", SA)
            p.wait("vector", SA, a_q1)
            if t > 0:
                p.wait("vector", q1_tr[t - 1])

            def vq1b(e, t=t):
                e.tensor_tensor(out=tinyA[:], in0=m1c[:, t:t + 1], in1=mv4[:, 0:1],
                                op=ALU.subtract)
                e.tensor_tensor(out=tinyB[:], in0=mv4[:, 0:1], in1=m2a[:, t:t + 1],
                                op=ALU.subtract)
                e.drain()
                e.tensor_tensor(out=tinyA[:], in0=tinyA[:], in1=tinyB[:], op=ALU.max)
                e.drain()
                e.tensor_scalar(out=tinyA[:], in0=tinyA[:], scalar1=rs4[:, t:t + 1],
                                scalar2=None, op0=ALU.mult)
                e.drain()
                e.tensor_scalar(out=m1c[:, t:t + 1], in0=tinyA[:], scalar1=EPS,
                                scalar2=None, op0=ALU.max)
                e.drain()
                e.reciprocal(out=tinyB[:], in_=m1c[:, t:t + 1])
                e.drain()
                e.tensor_scalar(out=tinyB[:], in0=tinyB[:], scalar1=127.0, scalar2=None,
                                op0=ALU.mult)
                e.drain()
                e.tensor_scalar(out=sc4[:, t:t + 1], in0=tinyB[:],
                                scalar1=rs4[:, t:t + 1], scalar2=None, op0=ALU.mult)
                e.drain()
                e.tensor_tensor(out=nb4[:, t:t + 1], in0=mv4[:, 0:1],
                                in1=sc4[:, t:t + 1], op=ALU.mult)
                e.drain()
                e.tensor_scalar(out=nb4[:, t:t + 1], in0=nb4[:, t:t + 1], scalar1=-1.0,
                                scalar2=None, op0=ALU.mult)
                e.drain()
                e.tensor_tensor(out=tinyB[:], in0=m1c[:, t:t + 1], in1=s_upt[:], op=ALU.mult)
                e.drain()
                e.tensor_scalar(out=tinyB[:], in0=tinyB[:], scalar1=INV127, scalar2=None,
                                op0=ALU.mult)
                e.drain()
                e.tensor_tensor(out=a2f[:, t:t + 1], in0=tinyB[:], in1=tinyB[:], op=ALU.mult)
                e.drain()
            p.op("vector", vq1b)
            v_ch1 = p.sig("vector", SV)
            p.wait("scalar", SV, v_ch1)

            def sq1b(e, t=t):
                xa = x2q[:, t * D:(t + 1) * D]
                e.activation(out=vtmp2[:, 0:D], in_=xa, func=ACT.Identity,
                             bias=nb4[:, t:t + 1], scale=sc4[:, t:t + 1])
                e.drain()
                e.activation(out=vtmp2[:, 0:D], in_=vtmp2[:, 0:D], func=ACT.Identity,
                             bias=m64b[:], scale=1.0)
                e.drain()
                e.activation(out=vtmp2[:, 0:D], in_=vtmp2[:, 0:D], func=ACT.Identity,
                             bias=nm64b[:], scale=1.0)
                e.drain()
            p.op("scalar", sq1b)
            a_q1b = p.sig("scalar", SA)

            def vq1r(e, t=t):
                xa = x2q[:, t * D:(t + 1) * D]
                e.tensor_scalar(out=vtmp[:, 0:D], in0=xa, scalar1=sc4[:, t:t + 1],
                                scalar2=nb4[:, t:t + 1], op0=ALU.mult, op1=ALU.add)
                e.drain()
                e.tensor_scalar(out=vtmp[:, 0:D], in0=vtmp[:, 0:D], scalar1=M32,
                                scalar2=M32, op0=ALU.add, op1=ALU.subtract)
                e.drain()
            p.op("vector", vq1r)
            p.wait("vector", SA, a_q1b)

            def vq1c(e, t=t):
                e.scalar_tensor_tensor(out=qtile[:, 0:D], in0=vtmp[:, 0:D], scalar=mu[:],
                                       in1=vtmp2[:, 0:D], op0=ALU.mult, op1=ALU.add)
                e.drain()
            p.op("vector", vq1c)
            v_q1 = p.sig("vector", SV)
            p.wait("sync", SV, v_q1)
            tr = p.dma("sync", lambda e, t=t: e.dma_start_transpose(
                out=q1fm[:].rearrange("p (c s) -> p c s", s=512)[:, 0:DC, t * 128:(t + 1) * 128],
                in_=qtile[:, 0:D].rearrange("p (c f) -> p c f", f=128)), "ftr")
            q1_tr.append(tr)

        ffn1_mm = []
        ffn1_last_t = {}
        ev_h = {}
        q2_tr = []
        tdn_ld = {}
        tdn_last = {}

        def tdn_half_load(k):
            r, h = k // 2, k % 2
            s = SOF[k % 3]
            lt = None
            for i in range(16):
                lt = p.dma("sync", lambda e, r=r, h=h, i=i, s=s: e.dma_start(
                    out=wsl_f[:, s * SLOT + i * 512: s * SLOT + (i + 1) * 512],
                    in_=tdn_sp[r, h * 16 + i]), f"tdn{s}")
            tdn_ld[k] = lt

        for j in range(I // 512):
            if j < 3:
                tl = tup_pre[j]
            else:
                p.wait("sync", ST, ffn1_mm[j - 3])
                lt = None
                for c in range(DC):
                    lt = p.dma("sync", lambda e, j=j, c=c: e.dma_start(
                        out=wsl_f[:, (j % 3) * SLOT + c * 512:
                                  (j % 3) * SLOT + (c + 1) * 512],
                        in_=tup_sp[j, c]), f"tup{j % 3}")
                tl = lt
            p.wait("tensor", tl)
            for t in range(4):
                par = 4 + (j * 4 + t) % 4
                p.wait("tensor", q1_tr[t])
                if j * 4 + t >= 4:
                    p.wait("tensor", SV, ev_h[j * 4 + t - 4])
                t_mm = None
                for c in range(DC):
                    t_mm = p.op("tensor", lambda e, par=par, c=c, t=t, j=j: e.matmul(
                        PS[par][:, :],
                        q1fm[:, c * 512 + t * 128: c * 512 + (t + 1) * 128],
                        wsl_f[:, (j % 3) * SLOT + c * 512:
                              (j % 3) * SLOT + (c + 1) * 512],
                        start=(c == 0), stop=(c == DC - 1)),
                        inc=ST if c == DC - 1 else None)
                if t == 3:
                    ffn1_mm.append(t_mm)
                p.wait("vector", ST, t_mm)

                def vh2(e, par=par, t=t, j=j):
                    dst = h2q[:, t * I + j * 512: t * I + (j + 1) * 512]
                    e.tensor_copy(out=dst, in_=PS[par][:, :])
                    e.drain()
                    e.scalar_tensor_tensor(out=dst, in0=dst, scalar=0.0,
                                           in1=dst, op0=ALU.max, op1=ALU.mult)
                    e.drain()
                    e.tensor_reduce(out=tinyA[:], in_=dst, axis=AX, op=ALU.max)
                    e.drain()
                    if j > 0:
                        e.tensor_tensor(out=m2a[:, t:t + 1], in0=m2a[:, t:t + 1],
                                        in1=tinyA[:], op=ALU.max)
                    else:
                        e.tensor_copy(out=m2a[:, t:t + 1], in_=tinyA[:])
                    e.drain()
                p.op("vector", vh2)
                ev_h[j * 4 + t] = p.sig("vector", SV)

                if j != I // 512 - 1:
                    continue
                ffn1_last_t[t] = t_mm
                if t == 0:
                    p.wait("sync", ST, ffn1_mm[5])
                    tdn_half_load(1)
                    p.wait("sync", ST, ffn1_mm[6])
                    tdn_half_load(0)
                    p.wait("vector", q1_tr[3])
                else:
                    p.wait("vector", q2_tr[t - 1])

                def vq2a(e, t=t):
                    e.tensor_tensor(out=tinyA[:], in0=a2f[:, t:t + 1],
                                    in1=m2a[:, t:t + 1], op=ALU.mult)
                    e.drain()
                    e.tensor_scalar(out=tinyA[:], in0=tinyA[:], scalar1=EPS,
                                    scalar2=None, op0=ALU.max)
                    e.drain()
                    e.tensor_tensor(out=f2[:, t:t + 1], in0=tinyA[:], in1=s_dnt[:],
                                    op=ALU.mult)
                    e.drain()
                    e.tensor_scalar(out=f2[:, t:t + 1], in0=f2[:, t:t + 1],
                                    scalar1=INV127, scalar2=None, op0=ALU.mult)
                    e.reciprocal(out=tinyB[:], in_=tinyA[:])
                    e.drain()
                    e.tensor_scalar(out=tinyB[:], in0=tinyB[:], scalar1=127.0,
                                    scalar2=None, op0=ALU.mult)
                    e.drain()
                    e.tensor_tensor(out=s2e[:], in0=tinyB[:], in1=a2f[:, t:t + 1],
                                    op=ALU.mult)
                    e.drain()
                p.op("vector", vq2a)
                v_gate = p.sig("vector", SV)
                for hf in range(2):
                    p.wait("scalar", SV, v_gate)

                    def s2b(e, t=t, hf=hf):
                        ha = h2q[:, t * I + hf * D: t * I + (hf + 1) * D]
                        e.activation(out=vtmp2[:], in_=ha, func=ACT.Identity,
                                     bias=m64b[:], scale=s2e[:])
                        e.drain()
                        e.activation(out=vtmp2[:], in_=vtmp2[:], func=ACT.Identity,
                                     bias=nm64b[:], scale=1.0)
                        e.drain()
                    p.op("scalar", s2b)
                    a_b = p.sig("scalar", SA)

                    def v2a(e, t=t, hf=hf):
                        ha = h2q[:, t * I + hf * D: t * I + (hf + 1) * D]
                        e.tensor_scalar(out=vtmp[:], in0=ha, scalar1=s2e[:],
                                        scalar2=M32, op0=ALU.mult, op1=ALU.add)
                        e.drain()
                        e.tensor_scalar(out=vtmp[:], in0=vtmp[:], scalar1=M32,
                                        scalar2=None, op0=ALU.subtract)
                        e.drain()
                    p.op("vector", v2a)
                    p.wait("vector", SA, a_b)

                    def v2c(e, t=t, hf=hf):
                        e.scalar_tensor_tensor(out=qtile[:, hf * D:(hf + 1) * D],
                                               in0=vtmp[:], scalar=mu[:],
                                               in1=vtmp2[:], op0=ALU.mult, op1=ALU.add)
                        e.drain()
                    p.op("vector", v2c)
                    v_gate = p.sig("vector", SV)
                v_q2 = v_gate
                if t == 3:
                    p.wait("sync", ST, ffn1_last_t[3])
                    tdn_half_load(2)
                p.wait("sync", SV, v_q2)
                p.wait("sync", ST, ffn1_last_t[t])
                tr = p.dma("sync", lambda e, t=t: e.dma_start_transpose(
                    out=q2fm[:].rearrange("p (c s) -> p c s", s=512)[:, :, t * 128:(t + 1) * 128],
                    in_=qtile[:].rearrange("p (c f) -> p c f", f=128)), "ftr")
                q2_tr.append(tr)
        last_q2tr = q2_tr[3]

        ffn2_mm = []
        ev_o2 = {}
        for r in range(NQB):
            for h in range(2):
                k = 2 * r + h
                if k not in tdn_ld:
                    p.wait("sync", ST, tdn_last[k - 3])
                    tdn_half_load(k)
            p.wait("tensor", tdn_ld[2 * r])
            for t in range(4):
                par = 4 + (r * 4 + t) % 4
                p.wait("tensor", q2_tr[t])
                if r * 4 + t >= 4:
                    p.wait("tensor", SV, ev_o2[r * 4 + t - 4])
                t_mm = None
                for c in range(IC):
                    if c == 16 and t == 0:
                        p.wait("tensor", tdn_ld[2 * r + 1])
                    s = SOF[(2 * r + c // 16) % 3]
                    t_mm = p.op("tensor", lambda e, par=par, c=c, t=t, s=s: e.matmul(
                        PS[par][:, :],
                        q2fm[:, c * 512 + t * 128: c * 512 + (t + 1) * 128],
                        wsl_f[:, s * SLOT + (c % 16) * 512:
                              s * SLOT + (c % 16 + 1) * 512],
                        start=(c == 0), stop=(c == IC - 1)),
                        inc=ST if (c == IC - 1 or (t == 3 and c == 15)) else None)
                    if t == 3 and c == 15:
                        tdn_last[2 * r] = t_mm
                if t == 3:
                    ffn2_mm.append(t_mm)
                    tdn_last[2 * r + 1] = t_mm
                p.wait("vector", ST, t_mm)
                p.op("vector", lambda e, par=par, t=t, r=r: e.tensor_scalar(
                    out=out2q[:, t * D + r * 512: t * D + (r + 1) * 512],
                    in0=PS[par][:, :], scalar1=f2[:, t:t + 1], scalar2=None, op0=ALU.mult))
                ev_o2[r * 4 + t] = p.sig("vector", SV)
                if r == NQB - 1:
                    def vst4(e, t=t):
                        oa = out2q[:, t * D:(t + 1) * D]
                        for gi in range(4):
                            e.bn_stats(out=stats4[:, gi * 6:(gi + 1) * 6],
                                       in_=oa[:, gi * 512:(gi + 1) * 512])
                        e.drain()
                        e.bn_aggr(out=mv4[:], in_=stats4[:])
                        e.drain()
                    p.op("vector", vst4)
                    v_s4 = p.sig("vector", SV)
                    p.wait("scalar", SV, v_s4)

                    def srs4(e, t=t):
                        e.activation(out=rs4[:, t:t + 1], in_=mv4[:, 1:2], func=ACT.Ln,
                                     bias=eps_t[:], scale=1.0)
                        e.drain()
                        e.activation(out=rs4[:, t:t + 1], in_=rs4[:, t:t + 1],
                                     func=ACT.Exp, bias=0.0, scale=-0.5)
                        e.drain()
                        e.activation(out=nb4[:, t:t + 1], in_=mv4[:, 0:1],
                                     func=ACT.Identity, bias=0.0, scale=rs4[:, t:t + 1])
                        e.drain()
                        e.activation(out=nb4[:, t:t + 1], in_=nb4[:, t:t + 1],
                                     func=ACT.Identity, bias=0.0, scale=-1.0)
                        e.drain()
                    p.op("scalar", srs4)
                    a_s4 = p.sig("scalar", SA)
                    p.wait("vector", SA, a_s4)

                    def vfin(e, t=t):
                        oa = out2q[:, t * D:(t + 1) * D]
                        e.tensor_scalar(out=oa, in0=oa, scalar1=rs4[:, t:t + 1],
                                        scalar2=nb4[:, t:t + 1], op0=ALU.mult, op1=ALU.add)
                        e.drain()
                        e.tensor_tensor(out=x2q[:, t * D:(t + 1) * D], in0=oa,
                                        in1=x2q[:, t * D:(t + 1) * D], op=ALU.add)
                        e.drain()
                    p.op("vector", vfin)
                    v_fin = p.sig("vector", SV)
                    p.wait("sync", SV, v_fin)
                    st_done.append(p.dma("sync", lambda e, qq=qq, t=t: e.dma_start(
                        out=out[(qq * 4 + t) * 128:(qq * 4 + t + 1) * 128, :],
                        in_=x2q[:, t * D:(t + 1) * D]), f"out{t}"))

        if qq < 3:
            tup_pre = []
            for j, kfree in ((0, 6), (1, 5), (2, 7)):
                p.wait("sync", ST, tdn_last[kfree])
                lt = None
                for c in range(DC):
                    lt = p.dma("sync", lambda e, j=j, c=c: e.dma_start(
                        out=wsl_f[:, j * SLOT + c * 512:
                                  j * SLOT + (c + 1) * 512],
                        in_=tup_sp[j, c]), f"tup{j}")
                tup_pre.append(lt)

    for slot in range(4):
        p.wait("sync", f"out{slot}", p.n(f"out{slot}"))
    ph4.close()

    block = es.enter_context(nc.Block())
    for eng in Prog.ENGINES:
        ops = p.ops[eng]
        getattr(block, eng)(lambda e, ops=ops: [fn(e) for fn in ops])
    es.close()
    return nc


_NC_CACHE = {}


def kernel(L0_attn_X, L0_attn_Wq, L0_attn_bq, L0_attn_Wk, L0_attn_bk,
           L0_attn_Wv, L0_attn_bv, L0_attn_Wo, L0_attn_bo,
           L0_ffn_w_up, L0_ffn_w_down):
    X = np.ascontiguousarray(np.asarray(L0_attn_X, dtype=np.float32))
    wq_t = np.ascontiguousarray(np.asarray(L0_attn_Wq, np.float32).T)
    wk_t = np.ascontiguousarray(np.asarray(L0_attn_Wk, np.float32).T)
    wv_t = np.ascontiguousarray(np.asarray(L0_attn_Wv, np.float32).T)
    wo_t = np.ascontiguousarray(np.asarray(L0_attn_Wo, np.float32).T)
    wup_t = np.ascontiguousarray(np.asarray(L0_ffn_w_up, np.float32).T)
    wdn_t = np.ascontiguousarray(np.asarray(L0_ffn_w_down, np.float32).T)
    bqv = np.ascontiguousarray(np.asarray(L0_attn_bq, np.float32))
    bkv = np.ascontiguousarray(np.asarray(L0_attn_bk, np.float32))
    bvv = np.ascontiguousarray(np.asarray(L0_attn_bv, np.float32))
    bov = np.ascontiguousarray(np.asarray(L0_attn_bo, np.float32))

    if "nc" not in _NC_CACHE:
        _NC_CACHE["nc"] = build_core()
    nc = _NC_CACHE["nc"]
    in_maps = []
    for c in range(8):
        in_maps.append({
            "x": np.ascontiguousarray(X[c]),
            "wqt": wq_t, "wkt": wk_t, "wvt": wv_t, "wot": wo_t,
            "wupt": wup_t, "wdnt": wdn_t,
            "bq": bqv, "bk": bkv, "bv": bvv, "bo": bov,
        })
    res = run_bass_kernel_spmd(nc, in_maps, core_ids=list(range(8)))
    return np.stack([np.asarray(r["out"], np.float32) for r in res.results], axis=0)


# version 17
# speedup vs baseline: 1.1841x; 1.0813x over previous
import contextlib
import numpy as np
import concourse.bass as bass
import concourse.mybir as mybir
from concourse.bass_utils import run_bass_kernel_spmd

F32 = mybir.dt.float32
BF16 = mybir.dt.bfloat16
ALU = mybir.AluOpType
ACT = mybir.ActivationFunctionType
AX = mybir.AxisListType.X

S = 2048
D = 2048
H = 16
I = 4096
NT = S // 128
DC = D // 128
IC = I // 128
SLOT = DC * 512
SOF = (0, 2, 1)
NQB = 4
NG = 4
M32 = float(np.float32(12582912.0))
M64 = float(np.float32(1.5 * 2 ** 52))
_SCL64 = 1.0 / np.sqrt(np.sqrt(128.0))
SCL = float(np.float32(_SCL64))
SCL_C = float(np.float32(_SCL64 - SCL))
EPS = 1.0e-5
INV_WN = float(np.float32(1.0 / 8388608.0))
INV127 = float(np.float32(1.0 / 127.0))


class Prog:
    ENGINES = ("sync", "gpsimd", "vector", "scalar", "tensor")

    def __init__(self, es, nc):
        self.nc = nc
        self.ops = {e: [] for e in self.ENGINES}
        self.sems = {}
        self.es = es
        self.last_wait = {e: {} for e in self.ENGINES}

    def sem(self, name):
        if name not in self.sems:
            h = self.es.enter_context(self.nc.semaphore(name))
            self.sems[name] = [h, 0]
        return name

    def n(self, s):
        self.sem(s)
        return self.sems[s][1]

    def wait(self, eng, s, val=None):
        if isinstance(s, tuple):
            s, val = s
        h, n = self.sems[self.sem(s)]
        if val is None:
            val = n
        assert val <= n, f"wait on future: {s} {val} > {n}"
        if val <= 0:
            return
        if self.last_wait[eng].get(s, 0) >= val:
            return
        self.last_wait[eng][s] = val
        self.ops[eng].append(lambda e, h=h, v=val: e.wait_ge(h, v))

    def op(self, eng, fn, inc=None, k=1):
        if inc is None:
            self.ops[eng].append(fn)
            return None
        h, _ = self.sems[self.sem(inc)]
        self.sems[inc][1] += k
        tgt = self.sems[inc][1]
        self.ops[eng].append(lambda e, fn=fn, h=h, k=k: fn(e).then_inc(h, k))
        return tgt

    def dma(self, eng, fn, stream):
        return (stream, self.op(eng, fn, inc=stream, k=16))

    def full_barrier(self, exclude=()):
        for eng in self.ENGINES:
            for s in list(self.sems):
                if any(s.startswith(x) for x in exclude):
                    continue
                self.wait(eng, s)

    def sig(self, eng, inc, k=1):
        if eng in ("vector", "scalar", "gpsimd"):
            self.ops[eng].append(lambda e: e.drain())
        return self.op(eng, lambda e: e.nop(), inc=inc, k=k)


def build_core():
    es = contextlib.ExitStack()
    nc = bass.Bass("TRN2", target_bir_lowering=False)

    x_in = nc.dram_tensor("x", [S, D], F32, kind="ExternalInput").ap()
    wqt = nc.dram_tensor("wqt", [D, D], F32, kind="ExternalInput").ap()
    wkt = nc.dram_tensor("wkt", [D, D], F32, kind="ExternalInput").ap()
    wvt = nc.dram_tensor("wvt", [D, D], F32, kind="ExternalInput").ap()
    wot = nc.dram_tensor("wot", [D, D], F32, kind="ExternalInput").ap()
    wupt = nc.dram_tensor("wupt", [D, I], F32, kind="ExternalInput").ap()
    wdnt = nc.dram_tensor("wdnt", [I, D], F32, kind="ExternalInput").ap()
    b_in = {n: nc.dram_tensor(n, [D], F32, kind="ExternalInput").ap()
            for n in ("bq", "bk", "bv", "bo")}
    out = nc.dram_tensor("out", [S, D], F32, kind="ExternalOutput").ap()

    x_sp = nc.dram_tensor("x_sp", [S, D], BF16, kind="Internal").ap()
    x2_sp = nc.dram_tensor("x2_sp", [S, D], BF16, kind="Internal").ap()
    wo_sp = nc.dram_tensor("wo_sp", [D, D], BF16, kind="Internal").ap()
    o_sp = nc.dram_tensor("o_sp", [H, 128, S], BF16, kind="Internal").ap()
    tup_sp = nc.dram_tensor("tup_sp", [I // 512, DC, 128, 512], BF16, kind="Internal").ap()
    tdn_sp = nc.dram_tensor("tdn_sp", [D // 512, IC, 128, 512], BF16, kind="Internal").ap()

    def sbuf(name, shape, dt=F32, st=es):
        return st.enter_context(nc.sbuf_tensor(name, shape, dt))

    p = Prog(es, nc)
    ST, SV, SA = (p.sem(s) for s in ("s_t", "s_v", "s_a"))
    SG = p.sem("s_g")

    mu = sbuf("mu", [128, 1])
    sc_qk = sbuf("sc_qk", [128, 1])
    eps_t = sbuf("eps_t", [128, 1])
    e35_t = sbuf("e35_t", [128, 1])
    c60_t = sbuf("c60_t", [128, 1])
    m64b = sbuf("m64b", [128, 1])
    nm64b = sbuf("nm64b", [128, 1])
    ones_bf = sbuf("ones_bf", [128, 128], BF16)
    ones_f = sbuf("ones_f", [128, 128])
    strip = sbuf("strip", [128, 896], BF16)
    bq_s = sbuf("bq_s", [128, H])
    bk_s = sbuf("bk_s", [128, H])
    bv_s = sbuf("bv_s", [128, H])
    wst_up = sbuf("wst_up", [128, 128])
    wst_dn = sbuf("wst_dn", [128, 128])
    th_up = sbuf("th_up", [128, 1])
    th_dn = sbuf("th_dn", [128, 1])
    nth_up = sbuf("nth_up", [128, 1])
    nth_dn = sbuf("nth_dn", [128, 1])
    s_upt = sbuf("s_upt", [128, 1])
    s_dnt = sbuf("s_dnt", [128, 1])
    nk2 = sbuf("nk2", [128, H])
    smallA = sbuf("smallA", [128, 4])
    negC = sbuf("negC", [128, 1])
    tinyA = sbuf("tinyA", [128, 1])
    tinyB = sbuf("tinyB", [128, 1])

    PSA = es.enter_context(nc.psum_tensor("psA", [128, 2048], F32))
    PSB = es.enter_context(nc.psum_tensor("psB", [128, 2048], F32))
    PS = [None, None, None, None] + [PSB[:, i * 512:(i + 1) * 512] for i in range(4)]
    S2 = [PSA[:, 0:1024], PSA[:, 1024:2048]]
    S4 = [PSA[:, j * 512:(j + 1) * 512] for j in range(4)]

    for nm, t in (("bq", bq_s), ("bk", bk_s), ("bv", bv_s)):
        p.op("gpsimd", lambda e, nm=nm, t=t: e.dma_start(
            out=t[:], in_=b_in[nm].rearrange("(h e) -> e h", e=128),
            allow_slow_non_contiguous=True), inc="bia", k=16)
    w_bias = ("bia", p.n("bia"))

    ph_y1 = contextlib.ExitStack()
    y1_fm = sbuf("y1_fm", [128, DC * S], BF16, ph_y1)
    ph1 = contextlib.ExitStack()
    xt = [sbuf(f"xt{i}", [128, D], F32, ph1) for i in range(2)]
    y1t = [sbuf(f"y1t{i}", [128, D], BF16, ph1) for i in range(2)]
    stats = sbuf("stats", [128, 24], F32, ph1)
    mv = sbuf("mv", [128, 2], F32, ph1)
    rs_t = sbuf("rs_t", [128, 1], F32, ph1)
    nb_t = sbuf("nb_t", [128, 1], F32, ph1)
    wch = [sbuf(f"wch{i}", [128, I], F32, ph1) for i in range(2)]
    strip_f = sbuf("strip_f", [128, 896], F32, ph1)

    g_strip = p.op("gpsimd", lambda e: e.iota(
        strip_f[:], pattern=[[1, 896]], base=-384, channel_multiplier=-1,
        allow_small_or_imprecise_dtypes=True), inc=SG)

    p.wait("vector", w_bias)
    p.wait("vector", SG, g_strip)

    def vconst(e):
        e.memset(mu[:], 0.75)
        e.memset(eps_t[:], EPS)
        e.memset(e35_t[:], 1.0e-35)
        e.memset(c60_t[:], -60.0)
        e.memset(m64b[:], M64)
        e.memset(nm64b[:], -M64)
        e.memset(ones_f[:], 1.0)
        e.memset(ones_bf[:], 1.0)
        e.drain()
        e.tensor_scalar(out=mu[:], in0=mu[:], scalar1=M64, scalar2=M64,
                        op0=ALU.add, op1=ALU.subtract)
        e.drain()
        e.tensor_scalar(out=mu[:], in0=mu[:], scalar1=-1.0, scalar2=1.0,
                        op0=ALU.mult, op1=ALU.add)
        e.drain()
        e.tensor_scalar(out=sc_qk[:], in0=mu[:], scalar1=-SCL_C, scalar2=SCL + SCL_C,
                        op0=ALU.mult, op1=ALU.add)
        e.tensor_scalar(out=strip[:], in0=strip_f[:], scalar1=0.0, scalar2=None,
                        op0=ALU.is_ge)
        e.drain()
        e.tensor_scalar(out=bq_s[:], in0=bq_s[:], scalar1=sc_qk[:], scalar2=None,
                        op0=ALU.mult)
        e.tensor_scalar(out=bk_s[:], in0=bk_s[:], scalar1=sc_qk[:], scalar2=None,
                        op0=ALU.mult)
        e.drain()
    p.op("vector", vconst)
    v_const = p.sig("vector", SV)

    x_ld, a_apply, tr_done, sp_done = [], [], [], []
    for i in range(NT):
        if i >= 2:
            p.wait("sync", SA, a_apply[i - 2])
            p.wait("sync", sp_done[i - 2])
        x_ld.append(p.dma("sync", lambda e, i=i: e.dma_start(
            out=xt[i % 2][:], in_=x_in[i * 128:(i + 1) * 128, :]), f"xl{i % 2}"))

        p.wait("vector", x_ld[i])

        def vstats(e, i=i):
            for gi in range(4):
                e.bn_stats(out=stats[:, gi * 6:(gi + 1) * 6],
                           in_=xt[i % 2][:, gi * 512:(gi + 1) * 512])
            e.drain()
            e.bn_aggr(out=mv[:], in_=stats[:])
            e.drain()
        p.op("vector", vstats)
        v_st = p.sig("vector", SV)

        p.wait("scalar", SV, v_st)
        if i >= 2:
            p.wait("scalar", tr_done[i - 2])

        def sapply(e):
            e.activation(out=rs_t[:], in_=mv[:, 1:2], func=ACT.Ln, bias=eps_t[:], scale=1.0)
            e.drain()
            e.activation(out=rs_t[:], in_=rs_t[:], func=ACT.Exp, bias=0.0, scale=-0.5)
            e.drain()
            e.activation(out=nb_t[:], in_=mv[:, 0:1], func=ACT.Identity, bias=0.0,
                         scale=rs_t[:])
            e.drain()
            e.activation(out=nb_t[:], in_=nb_t[:], func=ACT.Identity, bias=0.0, scale=-1.0)
            e.drain()
        p.op("scalar", sapply)
        a_apply.append(p.op("scalar", lambda e, i=i: e.activation(
            out=y1t[i % 2][:], in_=xt[i % 2][:], func=ACT.Identity,
            bias=nb_t[:], scale=rs_t[:]), inc=SA))

        p.wait("sync", SA, a_apply[i])
        t = p.dma("sync", lambda e, i=i: e.dma_start_transpose(
            out=y1_fm[:].rearrange("p (c s) -> p c s", s=S)[:, :, i * 128:(i + 1) * 128],
            in_=y1t[i % 2][:].rearrange("p (c f) -> p c f", f=128)), f"p1tr{i % 2}")
        tr_done.append(t)

        p.wait("gpsimd", SA, a_apply[i])
        sp_done.append(p.dma("gpsimd", lambda e, i=i: e.dma_start(
            out=x_sp[i * 128:(i + 1) * 128, :], in_=xt[i % 2][:]), f"xsp{i % 2}"))

    NWCH = DC + IC
    wst_ld, wst_v = [], []
    for j in range(NWCH):
        if j >= 2:
            p.wait("gpsimd", SV, wst_v[j - 2])
        if j < DC:
            wst_ld.append(p.dma("gpsimd", lambda e, j=j: e.dma_start(
                out=wch[j % 2][:], in_=wupt[j * 128:(j + 1) * 128, :]), f"ws{j % 2}"))
        else:
            wst_ld.append(p.dma("gpsimd", lambda e, c=j - DC, j=j: e.dma_start(
                out=wch[j % 2][:, 0:D], in_=wdnt[c * 128:(c + 1) * 128, :]), f"ws{j % 2}"))
        p.wait("vector", wst_ld[j])

        def vred(e, j=j):
            if j < DC:
                for gi in range(8):
                    e.tensor_reduce(out=wst_up[:, j * 8 + gi: j * 8 + gi + 1],
                                    in_=wch[j % 2][:, gi * 512:(gi + 1) * 512],
                                    axis=AX, op=ALU.add, apply_absolute_value=True)
            else:
                c = j - DC
                for gi in range(4):
                    e.tensor_reduce(out=wst_dn[:, c * 4 + gi: c * 4 + gi + 1],
                                    in_=wch[j % 2][:, gi * 512:(gi + 1) * 512],
                                    axis=AX, op=ALU.add, apply_absolute_value=True)
        p.op("vector", vred)
        wst_v.append(p.sig("vector", SV))

    def vtot(e):
        e.drain()
        e.tensor_reduce(out=smallA[:, 0:1], in_=wst_up[:], axis=AX, op=ALU.add)
        e.tensor_reduce(out=smallA[:, 1:2], in_=wst_dn[:], axis=AX, op=ALU.add)
        e.memset(smallA[:, 2:4], 0.0)
        e.drain()
    p.op("vector", vtot)
    v_tot = p.sig("vector", SV)
    p.wait("tensor", SV, v_tot)
    t_bc1 = p.op("tensor", lambda e: e.matmul(
        PS[5][0:1, 0:4], ones_f[:, 0:1], smallA[:, 0:4], start=True, stop=True), inc=ST)
    p.wait("vector", ST, t_bc1)
    p.op("vector", lambda e: e.tensor_copy(out=smallA[0:1, 0:4], in_=PS[5][0:1, 0:4]))
    v_bc1 = p.sig("vector", SV)
    p.wait("tensor", SV, v_bc1)
    t_bc = p.op("tensor", lambda e: e.matmul(
        PS[4][:, 0:4], ones_f[0:1, :], smallA[0:1, 0:4], start=True, stop=True), inc=ST)
    p.wait("vector", ST, t_bc)

    def vth(e):
        e.tensor_scalar(out=s_upt[:], in0=PS[4][:, 0:1], scalar1=INV_WN, scalar2=EPS,
                        op0=ALU.mult, op1=ALU.max)
        e.tensor_scalar(out=s_dnt[:], in0=PS[4][:, 1:2], scalar1=INV_WN, scalar2=EPS,
                        op0=ALU.mult, op1=ALU.max)
        e.drain()
        e.tensor_scalar(out=th_up[:], in0=s_upt[:], scalar1=0.5, scalar2=None, op0=ALU.mult)
        e.tensor_scalar(out=th_dn[:], in0=s_dnt[:], scalar1=0.5, scalar2=None, op0=ALU.mult)
        e.drain()
        e.tensor_scalar(out=nth_up[:], in0=th_up[:], scalar1=-1.0, scalar2=None, op0=ALU.mult)
        e.tensor_scalar(out=nth_dn[:], in0=th_dn[:], scalar1=-1.0, scalar2=None, op0=ALU.mult)
        e.drain()
    p.op("vector", vth)
    v_th = p.sig("vector", SV)

    ph1.close()
    p.full_barrier()

    for c in range(DC):
        p.dma("gpsimd", lambda e, c=c: e.dma_start(
            out=wo_sp[c * 128:(c + 1) * 128, :],
            in_=wot[c * 128:(c + 1) * 128, :]), "woc")

    ph2 = contextlib.ExitStack()
    kg = sbuf("kg", [128, 4 * S], BF16, ph2)
    vg = sbuf("vg", [128, NT * 512], BF16, ph2)
    wsl2 = [sbuf(f"wsl{i}", [128, DC * 512], BF16, ph2) for i in range(2)]
    qh_t = [sbuf(f"qh{i}", [128, 4 * 512], BF16, ph2) for i in range(2)]
    sq2_t = sbuf("sq2_t", [128, 512], F32, ph2)
    p_t = [sbuf(f"pt{i}", [128, 1024], BF16, ph2) for i in range(2)]
    rcp_t = [sbuf(f"rcp{i}", [128, 512], F32, ph2) for i in range(2)]
    sq_t = sbuf("sq_t", [128, 512], F32, ph2)
    ot_b = [sbuf(f"otb{i}", [128, 512], BF16, ph2) for i in range(2)]
    bv_b = sbuf("bv_b", [128, 512], F32, ph2)
    bv_row = sbuf("bv_row", [1, 512], F32, ph2)
    wqch = [sbuf(f"wqch{i}", [128, D], F32, ph2) for i in range(2)]
    wq_p = sbuf("wq_p", [128, D], F32, ph2)
    wq_n = sbuf("wq_n", [128, D], F32, ph2)
    tq_bf = sbuf("tq_bf", [128, D], BF16, ph2)

    wq_units = ([("up", c, hf) for c in range(DC) for hf in range(2)]
                + [("dn", c, 0) for c in range(IC)])
    wq_ld, wq_vdone, wq_spdone = {}, {}, {}
    wq_state = {"ld": 0, "q": 0}

    def wq_emit_load(n=1):
        for _ in range(n):
            u = wq_state["ld"]
            if u >= len(wq_units) or u >= wq_state["q"] + 2:
                return
            kind, c, hf = wq_units[u]
            if u >= 2:
                p.wait("gpsimd", SV, wq_vdone[u - 2])
            if kind == "up":
                wq_ld[u] = p.dma("gpsimd", lambda e, c=c, hf=hf, u=u: e.dma_start(
                    out=wqch[u % 2][:],
                    in_=wupt[c * 128:(c + 1) * 128, hf * 2048:(hf + 1) * 2048]),
                    f"wqs{u % 2}")
            else:
                wq_ld[u] = p.dma("gpsimd", lambda e, c=c, u=u: e.dma_start(
                    out=wqch[u % 2][:], in_=wdnt[c * 128:(c + 1) * 128, :]), f"wqs{u % 2}")
            wq_state["ld"] += 1

    def wq_emit_quant(n=1):
        for _ in range(n):
            u = wq_state["q"]
            if u >= len(wq_units) or u >= wq_state["ld"]:
                return
            kind, c, hf = wq_units[u]
            th, nth = (th_up, nth_up) if kind == "up" else (th_dn, nth_dn)
            p.wait("vector", SV, v_th)
            p.wait("vector", wq_ld[u])
            if u >= 1:
                p.wait("vector", wq_spdone[u - 1])

            def vq(e, u=u, th=th, nth=nth):
                e.tensor_scalar(out=wq_p[:], in0=wqch[u % 2][:], scalar1=th[:],
                                scalar2=None, op0=ALU.is_ge)
                e.tensor_scalar(out=wq_n[:], in0=wqch[u % 2][:], scalar1=nth[:],
                                scalar2=None, op0=ALU.is_le)
                e.drain()
                e.tensor_tensor(out=tq_bf[:], in0=wq_p[:], in1=wq_n[:], op=ALU.subtract)
            p.op("vector", vq)
            wq_vdone[u] = p.sig("vector", SV)
            p.wait("gpsimd", SV, wq_vdone[u])
            last = None
            for j in range(4):
                if kind == "up":
                    last = p.dma("gpsimd", lambda e, c=c, hf=hf, j=j: e.dma_start(
                        out=tup_sp[hf * 4 + j, c], in_=tq_bf[:, j * 512:(j + 1) * 512]),
                        f"stq{u % 2}")
                else:
                    last = p.dma("gpsimd", lambda e, c=c, j=j: e.dma_start(
                        out=tdn_sp[j, c], in_=tq_bf[:, j * 512:(j + 1) * 512]),
                        f"stq{u % 2}")
            wq_spdone[u] = last
            wq_state["q"] += 1

    ev_k, ev_v = {}, {}
    o_evac, o_spw, q_evac, a_rec = [], [], [], []
    exp_done, mask_done, pv_mm = {}, {}, {}
    t_sqr = [None]
    last_qev = [None]
    last_qproj = [None]
    lk_pf = {}
    q_proj_last = {}
    hq = 0
    for grp in range(NG):
        p.wait("tensor", SA, p.n(SA))
        p.wait("tensor", SV, p.n(SV))
        wsl = wsl2[grp % 2]
        if grp == 0:
            lk = None
            for c in range(DC):
                lk = p.dma("gpsimd", lambda e, c=c, grp=grp, wsl=wsl: e.dma_start(
                    out=wsl[:, c * 512:(c + 1) * 512],
                    in_=wkt[c * 128:(c + 1) * 128, grp * 512:(grp + 1) * 512]), "kvw")
        else:
            lk = lk_pf[grp]
        p.wait("tensor", lk)
        for comb in range(16):
            hl, r = divmod(comb, NQB)
            h = grp * 4 + hl
            par = 4 + comb % 2
            if comb >= 2:
                p.wait("tensor", SA, ev_k[(grp, comb - 2)])
            p.wait("tensor", tr_done[4 * r + 3])
            t_mm = None
            for c in range(DC):
                t_mm = p.op("tensor", lambda e, par=par, c=c, hl=hl, r=r, wsl=wsl: e.matmul(
                    PS[par][:, :], wsl[:, c * 512 + hl * 128: c * 512 + (hl + 1) * 128],
                    y1_fm[:, c * S + r * 512: c * S + (r + 1) * 512],
                    start=(c == 0), stop=(c == DC - 1)),
                    inc=ST if c == DC - 1 else None)
            p.wait("scalar", ST, t_mm)
            ev_k[(grp, comb)] = p.op("scalar", lambda e, par=par, hl=hl, h=h, r=r: e.activation(
                out=kg[:, hl * S + r * 512: hl * S + (r + 1) * 512],
                in_=PS[par][:, :], func=ACT.Identity,
                bias=bk_s[:, h:h + 1], scale=sc_qk[:]), inc=SA)
            k_last = t_mm

        lv = None
        p.wait("gpsimd", ST, k_last)
        for c in range(DC):
            lv = p.dma("gpsimd", lambda e, c=c, grp=grp, wsl=wsl: e.dma_start(
                out=wsl[:, c * 512:(c + 1) * 512],
                in_=wvt[c * 128:(c + 1) * 128, grp * 512:(grp + 1) * 512]), "kvw")
        bvr_ld = p.dma("gpsimd", lambda e, grp=grp: e.dma_start(
            out=bv_row[:], in_=b_in["bv"].rearrange("(o d) -> o d", o=4)[grp:grp + 1, :]),
            "bvr")
        p.wait("tensor", SV, v_const)
        p.wait("tensor", bvr_ld)
        tb = p.op("tensor", lambda e: e.matmul(
            PS[7][:, :], ones_f[0:1, :], bv_row[0:1, :], start=True, stop=True), inc=ST)
        p.wait("vector", ST, tb)
        p.op("vector", lambda e: e.tensor_copy(out=bv_b[:], in_=PS[7][:, :]))
        v_bvb = p.sig("vector", SV)

        p.wait("tensor", lv)
        for t in range(NT):
            par = 4 + t % 2
            if t < 2:
                p.wait("tensor", SA, ev_k[(grp, 14 + t)])
            else:
                p.wait("tensor", SV, ev_v[(grp, t - 2)])
            t_mm = None
            for c in range(DC):
                t_mm = p.op("tensor", lambda e, par=par, c=c, t=t, wsl=wsl: e.matmul(
                    PS[par][:, :], y1_fm[:, c * S + t * 128: c * S + (t + 1) * 128],
                    wsl[:, c * 512:(c + 1) * 512],
                    start=(c == 0), stop=(c == DC - 1)),
                    inc=ST if c == DC - 1 else None)
            v_last = t_mm
            p.wait("vector", ST, t_mm)
            p.wait("vector", SV, v_bvb)
            p.op("vector", lambda e, par=par, t=t: e.tensor_tensor(
                out=vg[:, t * 512:(t + 1) * 512], in0=PS[par][:, :], in1=bv_b[:],
                op=ALU.add))
            ev_v[(grp, t)] = p.sig("vector", SV)

        vnk_prev = None
        for hl in range(4):
            h = grp * 4 + hl

            def vnkinit(e, h=h):
                e.memset(nk2[:, h:h + 1], 0.0)
                e.drain()
            p.op("vector", vnkinit)
            for r in range(NQB):
                p.wait("vector", SA, ev_k[(grp, hl * NQB + r)])
                if t_sqr[0] is not None:
                    p.wait("vector", ST, t_sqr[0])
                p.op("vector", lambda e, hl=hl, r=r: e.tensor_tensor(
                    out=sq_t[:], in0=kg[:, hl * S + r * 512: hl * S + (r + 1) * 512],
                    in1=kg[:, hl * S + r * 512: hl * S + (r + 1) * 512], op=ALU.mult))
                v_sq = p.sig("vector", SV)
                p.wait("tensor", SV, v_sq)
                if vnk_prev is not None:
                    p.wait("tensor", SV, vnk_prev)
                t_nk = p.op("tensor", lambda e: e.matmul(
                    PS[6][:, :], ones_f[:, :], sq_t[:], start=True, stop=True), inc=ST)
                t_sqr[0] = t_nk
                p.wait("vector", ST, t_nk)

                def vnkacc(e, h=h):
                    e.tensor_reduce(out=tinyA[:], in_=PS[6][:, :], axis=AX, op=ALU.max)
                    e.drain()
                    e.tensor_tensor(out=nk2[:, h:h + 1], in0=nk2[:, h:h + 1],
                                    in1=tinyA[:], op=ALU.max)
                    e.drain()
                p.op("vector", vnkacc)
                vnk_prev = p.sig("vector", SV)

        p.wait("gpsimd", ST, v_last)
        lq = None
        for c in range(DC):
            lq = p.dma("gpsimd", lambda e, c=c, grp=grp, wsl=wsl: e.dma_start(
                out=wsl[:, c * 512:(c + 1) * 512],
                in_=wqt[c * 128:(c + 1) * 128, grp * 512:(grp + 1) * 512]), "kvw")
        if grp + 1 < NG:
            if grp >= 1:
                p.wait("gpsimd", ST, q_proj_last[grp - 1])
            t_pf = None
            for c in range(DC):
                t_pf = p.dma("gpsimd", lambda e, c=c, grp=grp: e.dma_start(
                    out=wsl2[(grp + 1) % 2][:, c * 512:(c + 1) * 512],
                    in_=wkt[c * 128:(c + 1) * 128, (grp + 1) * 512:(grp + 2) * 512]),
                    "kvwP")
            lk_pf[grp + 1] = t_pf

        for _ in range(3):
            wq_emit_load(1)
            wq_emit_quant(1)

        p.wait("tensor", SV, p.n(SV))
        p.wait("tensor", SA, p.n(SA))
        p.wait("tensor", lq)
        for hl in range(4):
            h = grp * 4 + hl
            hh = grp * 4 + hl
            hpar = hh % 2
            qev = []
            for r4 in range(4):
                if r4 > 0:
                    p.wait("tensor", SV, qev[r4 - 1])
                elif last_qev[0] is not None:
                    p.wait("tensor", SV, last_qev[0])
                if hh >= 2 and r4 == 0:
                    last_hq = (hh - 2) * NQB + NQB - 1
                    ns2 = 2 * NQB
                    p.wait("tensor", ST, pv_mm[(last_hq, ns2 - 1)])
                t_qmm = None
                for c in range(DC):
                    t_qmm = p.op("tensor", lambda e, c=c, hl=hl, r4=r4, wsl=wsl: e.matmul(
                        PS[4][:, :],
                        wsl[:, c * 512 + hl * 128: c * 512 + (hl + 1) * 128],
                        y1_fm[:, c * S + r4 * 512: c * S + (r4 + 1) * 512],
                        start=(c == 0), stop=(c == DC - 1)),
                        inc=ST if c == DC - 1 else None)
                p.wait("vector", ST, t_qmm)
                if t_sqr[0] is not None:
                    p.wait("vector", ST, t_sqr[0])

                def vqev(e, hpar=hpar, h=h, r4=r4):
                    e.tensor_scalar(out=qh_t[hpar][:, r4 * 512:(r4 + 1) * 512],
                                    in0=PS[4][:, :], scalar1=sc_qk[:],
                                    scalar2=bq_s[:, h:h + 1], op0=ALU.mult, op1=ALU.add)
                    e.drain()
                    e.tensor_tensor(out=(sq_t if r4 % 2 == 0 else sq2_t)[:],
                                    in0=qh_t[hpar][:, r4 * 512:(r4 + 1) * 512],
                                    in1=qh_t[hpar][:, r4 * 512:(r4 + 1) * 512],
                                    op=ALU.mult)
                p.op("vector", vqev)
                qev.append(p.sig("vector", SV))
                p.wait("tensor", SV, qev[r4])
                if r4 == 0 and hh >= 1:
                    p.wait("tensor", SA, a_rec[hh * NQB - 1])
                if r4 == 0 and hh >= 2:
                    p.wait("tensor", SV, o_evac[hh * NQB - 2])
                t_nq = p.op("tensor", lambda e, r4=r4: e.matmul(
                    PS[5][:, :], ones_f[:, :], (sq_t if r4 % 2 == 0 else sq2_t)[:],
                    start=(r4 == 0), stop=(r4 == 3)), inc=ST)
                t_sqr[0] = t_nq
                last_qproj[0] = t_qmm
            last_qev[0] = qev[3]
            p.wait("vector", ST, t_nq)

            def vcap(e, h=h):
                e.tensor_reduce(out=tinyA[:], in_=PS[5][:, :], axis=AX, op=ALU.max)
                e.drain()
                e.tensor_tensor(out=tinyB[:], in0=tinyA[:], in1=nk2[:, h:h + 1],
                                op=ALU.mult)
                e.drain()
            p.op("vector", vcap)
            cap_v = p.sig("vector", SV)
            p.wait("scalar", SV, cap_v)

            def scap(e):
                e.activation(out=negC[:], in_=tinyB[:], func=ACT.Ln, bias=0.0, scale=1.0)
                e.drain()
                e.activation(out=negC[:], in_=negC[:], func=ACT.Exp, bias=0.0, scale=0.5)
                e.drain()
                e.activation(out=negC[:], in_=negC[:], func=ACT.Relu, bias=c60_t[:],
                             scale=1.0)
                e.drain()
                e.activation(out=negC[:], in_=negC[:], func=ACT.Identity, bias=0.0,
                             scale=-1.0)
                e.drain()
            p.op("scalar", scap)

            for qb in range(NQB):
                nk = 4 * (qb + 1)
                ns = nk // 2
                qpar = hq % 2
                ns_prev = 2 * ((hq - 1) % NQB + 1) if hq >= 1 else 0
                qsl = qh_t[hpar][:, qb * 512:(qb + 1) * 512]
                for u in range(ns):
                    sb = u % 2
                    if u >= 2:
                        p.wait("tensor", SA, exp_done[(hq, u - 2)])
                    elif hq >= 1 and ns_prev - 2 + sb >= 0:
                        p.wait("tensor", SA, exp_done[(hq - 1, ns_prev - 2 + sb)])
                    if u == 0 and hq >= 1:
                        p.wait("tensor", SA, a_rec[hq - 1])
                    if u == 0 and hq >= 2:
                        p.wait("tensor", SV, o_evac[hq - 2])
                    p.op("tensor", lambda e, sb=sb, hl=hl, u=u, qsl=qsl: e.matmul(
                        S2[sb][:, 0:512],
                        kg[:, hl * S + (2 * u) * 128: hl * S + (2 * u + 1) * 128],
                        qsl, start=True, stop=True))
                    t_smm = p.op("tensor", lambda e, sb=sb, hl=hl, u=u, qsl=qsl: e.matmul(
                        S2[sb][:, 512:1024],
                        kg[:, hl * S + (2 * u + 1) * 128: hl * S + (2 * u + 2) * 128],
                        qsl, start=True, stop=True), inc=ST)
                    p.wait("scalar", ST, t_smm)
                    if u >= 2:
                        p.wait("scalar", ST, pv_mm[(hq, u - 2)])
                    elif hq >= 1 and (hq - 1, ns_prev - 2 + sb) in pv_mm:
                        p.wait("scalar", ST, pv_mm[(hq - 1, ns_prev - 2 + sb)])
                    exp_done[(hq, u)] = p.op("scalar", lambda e, sb=sb: e.activation(
                        out=p_t[sb][:], in_=S2[sb][:, 0:1024], func=ACT.Exp,
                        bias=negC[:], scale=1.0), inc=SA)
                    diag = (2 * u + 1) >= nk - 4
                    if diag:
                        p.wait("vector", SA, exp_done[(hq, u)])
                        for half in range(2):
                            kc = 2 * u + half
                            if kc >= nk - 4:
                                m = kc - 4 * qb
                                p.op("vector", lambda e, sb=sb, m=m, half=half: e.tensor_tensor(
                                    out=p_t[sb][:, half * 512:(half + 1) * 512],
                                    in0=p_t[sb][:, half * 512:(half + 1) * 512],
                                    in1=strip[:, 384 - 128 * m: 896 - 128 * m],
                                    op=ALU.mult))
                        mask_done[(hq, u)] = p.sig("vector", SV)
                    p.wait("tensor", SA, exp_done[(hq, u)])
                    if diag:
                        p.wait("tensor", SV, mask_done[(hq, u)])
                    if u == 0 and qb == 0:
                        p.wait("tensor", SV, cap_v)
                    p.op("tensor", lambda e, sb=sb, u=u: e.matmul(
                        PS[5][:, :], ones_bf[:, :], p_t[sb][:, 0:512],
                        start=(u == 0), stop=False))
                    p.op("tensor", lambda e, sb=sb, u=u, ns=ns: e.matmul(
                        PS[5][:, :], ones_bf[:, :], p_t[sb][:, 512:1024],
                        start=False, stop=(u == ns - 1)))
                    p.op("tensor", lambda e, sb=sb, u=u, qpar=qpar, hl=hl: e.matmul(
                        PS[6 + qpar][:, :],
                        vg[:, (2 * u) * 512 + hl * 128: (2 * u) * 512 + (hl + 1) * 128],
                        p_t[sb][:, 0:512], start=(u == 0), stop=False))
                    pv_mm[(hq, u)] = p.op("tensor",
                        lambda e, sb=sb, u=u, qpar=qpar, hl=hl, ns=ns: e.matmul(
                            PS[6 + qpar][:, :],
                            vg[:, (2 * u + 1) * 512 + hl * 128: (2 * u + 1) * 512 + (hl + 1) * 128],
                            p_t[sb][:, 512:1024], start=False, stop=(u == ns - 1)),
                        inc=ST)

                p.wait("scalar", ST, pv_mm[(hq, ns - 1)])
                if hq >= 2:
                    p.wait("scalar", SV, o_evac[hq - 2])

                def srec(e, qpar=qpar):
                    e.activation(out=rcp_t[qpar][:], in_=PS[5][:, :], func=ACT.Ln,
                                 bias=e35_t[:], scale=1.0)
                    e.drain()
                    e.activation(out=rcp_t[qpar][:], in_=rcp_t[qpar][:], func=ACT.Exp,
                                 bias=0.0, scale=-1.0)
                    e.drain()
                p.op("scalar", srec)
                a_rec.append(p.sig("scalar", SA))
                p.wait("vector", SA, a_rec[hq])
                if hq >= 2:
                    p.wait("vector", o_spw[hq - 2])
                p.op("vector", lambda e, qpar=qpar: e.tensor_tensor(
                    out=ot_b[qpar][:], in0=PS[6 + qpar][:, :], in1=rcp_t[qpar][:],
                    op=ALU.mult))
                o_evac.append(p.sig("vector", SV))
                p.wait("gpsimd", SV, o_evac[hq])
                o_spw.append(p.dma("gpsimd", lambda e, h=h, qb=qb, qpar=qpar: e.dma_start(
                    out=o_sp[h, :, qb * 512:(qb + 1) * 512], in_=ot_b[qpar][:]),
                    f"osp{hq % 2}"))
                hq += 1
                if qb >= 1:
                    wq_emit_load(1)
                    wq_emit_quant(1)
        q_proj_last[grp] = last_qproj[0]

    while wq_state["q"] < len(wq_units):
        wq_emit_load(1)
        wq_emit_quant(1)
    o_all_sp = o_spw[-1]

    o_full = y1_fm
    o_rld = []
    for hh in range(H):
        p.wait("sync", o_spw[hh * NQB + NQB - 1])
        p.wait("sync", o_spw[hh * NQB + NQB - 2])
        p.wait("sync", ST, last_qproj[0])
        if hh >= 8:
            p.wait("sync", o_rld[hh - 8])
        o_rld.append(p.dma("sync", lambda e, hh=hh: e.dma_start(
            out=o_full[:, hh * S:(hh + 1) * S], in_=o_sp[hh]), f"orl{hh % 8}"))

    ph2.close()
    p.full_barrier(exclude=("orl",))

    ph3 = contextlib.ExitStack()
    wo_bf = sbuf("wo_bf", [128, DC * D], BF16, ph3)
    at_f = [sbuf(f"at_f{i}", [128, D], F32, ph3) for i in range(2)]
    xr_f = [sbuf(f"xr{i}", [128, D], BF16, ph3) for i in range(2)]
    x2t = [sbuf(f"x2t{i}", [128, D], BF16, ph3) for i in range(2)]
    bo_b = sbuf("bo_b", [128, D], F32, ph3)
    bo_1row = sbuf("bo_1row", [1, D], F32, ph3)
    stats3 = sbuf("stats3", [128, 24], F32, ph3)
    mv3 = sbuf("mv3", [128, 2], F32, ph3)
    rs3 = sbuf("rs3", [128, 1], F32, ph3)
    nb3 = sbuf("nb3", [128, 1], F32, ph3)

    bo_ld = p.dma("sync", lambda e: e.dma_start(
        out=bo_1row[:], in_=b_in["bo"].rearrange("(o d) -> o d", o=1)), "bol")
    wo_ld = []
    for c in range(DC):
        wo_ld.append(p.dma("sync", lambda e, c=c: e.dma_start(
            out=wo_bf[:, c * D:(c + 1) * D], in_=wo_sp[c * 128:(c + 1) * 128, :]),
            f"wol{c}"))
    p.wait("tensor", SA, p.n(SA))
    p.wait("tensor", SV, p.n(SV))
    p.wait("tensor", bo_ld)
    v_bob = None
    for rnd in range(2):
        t_bo2 = []
        for rr2 in range(2):
            rr = rnd * 2 + rr2
            t_bo2.append(p.op("tensor", lambda e, rr=rr, rr2=rr2: e.matmul(
                PS[6 + rr2][:, :], ones_f[0:1, :], bo_1row[0:1, rr * 512:(rr + 1) * 512],
                start=True, stop=True), inc=ST))
        for rr2 in range(2):
            rr = rnd * 2 + rr2
            p.wait("vector", ST, t_bo2[rr2])
            p.op("vector", lambda e, rr=rr, rr2=rr2: e.tensor_scalar(
                out=bo_b[:, rr * 512:(rr + 1) * 512], in0=PS[6 + rr2][:, :], scalar1=0.0,
                scalar2=None, op0=ALU.add))
        v_bob = p.sig("vector", SV)
        if rnd == 0:
            p.wait("tensor", SV, v_bob)
    v_bob = v_bob

    x2_spw, at_ev, vx2_t = [], {}, []
    for i in range(NT):
        if i >= 2:
            p.wait("gpsimd", SV, vx2_t[i - 2])
        xr_ld = p.dma("gpsimd", lambda e, i=i: e.dma_start(
            out=xr_f[i % 2][:], in_=x_sp[i * 128:(i + 1) * 128, :]), f"xrl{i % 2}")
        p.wait("tensor", SV, v_bob)
        for r in range(NQB):
            par = r % 2 + 4
            if i * NQB + r >= 2:
                p.wait("tensor", SV, at_ev[(i * NQB + r - 2)])
            t_mm = None
            for c in range(H):
                if i == 0 and r == 0:
                    p.wait("tensor", o_rld[c])
                    p.wait("tensor", wo_ld[c])
                t_mm = p.op("tensor", lambda e, par=par, c=c, i=i, r=r: e.matmul(
                    PS[par][:, :], o_full[:, c * S + i * 128: c * S + (i + 1) * 128],
                    wo_bf[:, c * D + r * 512: c * D + (r + 1) * 512],
                    start=(c == 0), stop=(c == H - 1)),
                    inc=ST if c == H - 1 else None)
            p.wait("vector", ST, t_mm)
            p.op("vector", lambda e, par=par, i=i, r=r: e.tensor_tensor(
                out=at_f[i % 2][:, r * 512:(r + 1) * 512], in0=PS[par][:, :],
                in1=bo_b[:, r * 512:(r + 1) * 512], op=ALU.add))
            at_ev[i * NQB + r] = p.sig("vector", SV)

        def vstats3(e, i=i):
            for gi in range(4):
                e.bn_stats(out=stats3[:, gi * 6:(gi + 1) * 6],
                           in_=at_f[i % 2][:, gi * 512:(gi + 1) * 512])
            e.drain()
            e.bn_aggr(out=mv3[:], in_=stats3[:])
            e.drain()
        p.op("vector", vstats3)
        v_st3 = p.sig("vector", SV)
        p.wait("scalar", SV, v_st3)

        def srs3(e):
            e.activation(out=rs3[:], in_=mv3[:, 1:2], func=ACT.Ln, bias=eps_t[:], scale=1.0)
            e.drain()
            e.activation(out=rs3[:], in_=rs3[:], func=ACT.Exp, bias=0.0, scale=-0.5)
            e.drain()
            e.activation(out=nb3[:], in_=mv3[:, 0:1], func=ACT.Identity, bias=0.0, scale=rs3[:])
            e.drain()
            e.activation(out=nb3[:], in_=nb3[:], func=ACT.Identity, bias=0.0, scale=-1.0)
            e.drain()
        p.op("scalar", srs3)
        a_rs3 = p.sig("scalar", SA)
        p.wait("vector", SA, a_rs3)
        p.wait("vector", xr_ld)
        if i >= 2:
            p.wait("vector", x2_spw[i - 2])

        def vx2(e, i=i):
            e.tensor_scalar(out=at_f[i % 2][:], in0=at_f[i % 2][:], scalar1=rs3[:],
                            scalar2=nb3[:], op0=ALU.mult, op1=ALU.add)
            e.drain()
            e.tensor_tensor(out=x2t[i % 2][:], in0=at_f[i % 2][:], in1=xr_f[i % 2][:],
                            op=ALU.add)
            e.drain()
        p.op("vector", vx2)
        vx2_t.append(p.sig("vector", SV))
        p.wait("gpsimd", SV, vx2_t[i])
        x2_spw.append(p.dma("gpsimd", lambda e, i=i: e.dma_start(
            out=x2_sp[i * 128:(i + 1) * 128, :], in_=x2t[i % 2][:]), f"x2s{i % 2}"))

    x2_all = x2_spw[-1]
    ph3.close()
    ph_y1.close()
    p.full_barrier()

    ph4 = contextlib.ExitStack()
    x2q = sbuf("x2q", [128, 4 * D], F32, ph4)
    qfm_ar = sbuf("qfm_ar", [128, IC * 512], BF16, ph4)
    q1fm = qfm_ar
    q2fm = qfm_ar
    vtmp = sbuf("vtmp", [128, D], F32, ph4)
    vtmp2 = sbuf("vtmp2", [128, D], F32, ph4)
    qtile = sbuf("qtile", [128, I], BF16, ph4)
    stats4 = sbuf("stats4", [128, 24], F32, ph4)
    mv4 = sbuf("mv4", [128, 2], F32, ph4)
    rs4 = sbuf("rs4", [128, 4], F32, ph4)
    sc4 = sbuf("sc4", [128, 4], F32, ph4)
    nb4 = sbuf("nb4", [128, 4], F32, ph4)
    m1c = sbuf("m1c", [128, 4], F32, ph4)
    a2f = sbuf("a2f", [128, 4], F32, ph4)
    m2a = sbuf("m2a", [128, 4], F32, ph4)
    f2 = sbuf("f2", [128, 4], F32, ph4)
    s2e = sbuf("s2e", [128, 1], F32, ph4)
    wsl_f = sbuf("wsl_f", [128, 3 * DC * 512], BF16, ph4)
    h2q = sbuf("h2q", [128, 4 * I], F32, ph4)
    out2q = h2q

    p.wait("tensor", SV, p.n(SV))
    p.wait("tensor", SA, p.n(SA))

    st_done = []
    last_q2tr = None
    tup_pre = None
    for qq in range(4):
        if qq == 0:
            p.wait("gpsimd", "x2s0", p.n("x2s0"))
            p.wait("gpsimd", "x2s1", p.n("x2s1"))
        x2_ld = []
        for t in range(4):
            if qq > 0:
                p.wait("gpsimd", f"out{t}", p.n(f"out{t}"))
            x2_ld.append(p.dma("gpsimd", lambda e, qq=qq, t=t: e.dma_start(
                out=x2q[:, t * D:(t + 1) * D],
                in_=x2_sp[(qq * 4 + t) * 128:(qq * 4 + t + 1) * 128, :]), f"x2l{t}"))

        if qq == 0:
            p.wait("sync", "stq0", p.n("stq0"))
            p.wait("sync", "stq1", p.n("stq1"))
            tup_pre = []
            for j in range(3):
                lt = None
                for c in range(DC):
                    lt = p.dma("sync", lambda e, j=j, c=c: e.dma_start(
                        out=wsl_f[:, j * SLOT + c * 512:
                                  j * SLOT + (c + 1) * 512],
                        in_=tup_sp[j, c]), f"tup{j}")
                tup_pre.append(lt)

        q1_tr = []
        for t in range(4):
            p.wait("vector", x2_ld[t])
            if last_q2tr is not None and t == 0:
                p.wait("vector", last_q2tr)

            def vq1(e, t=t):
                xa = x2q[:, t * D:(t + 1) * D]
                for gi in range(4):
                    e.bn_stats(out=stats4[:, gi * 6:(gi + 1) * 6],
                               in_=xa[:, gi * 512:(gi + 1) * 512])
                e.drain()
                e.bn_aggr(out=mv4[:], in_=stats4[:])
                e.drain()
                e.tensor_reduce(out=m1c[:, t:t + 1], in_=xa, axis=AX, op=ALU.max)
                e.tensor_reduce(out=m2a[:, t:t + 1], in_=xa, axis=AX, op=ALU.min)
                e.drain()
            p.op("vector", vq1)
            v_q1s = p.sig("vector", SV)
            p.wait("scalar", SV, v_q1s)

            def sq1(e, t=t):
                e.activation(out=rs4[:, t:t + 1], in_=mv4[:, 1:2], func=ACT.Ln,
                             bias=eps_t[:], scale=1.0)
                e.drain()
                e.activation(out=rs4[:, t:t + 1], in_=rs4[:, t:t + 1], func=ACT.Exp,
                             bias=0.0, scale=-0.5)
                e.drain()
            p.op("scalar", sq1)
            a_q1 = p.sig("scalar", SA)
            p.wait("vector", SA, a_q1)
            if t > 0:
                p.wait("vector", q1_tr[t - 1])

            def vq1b(e, t=t):
                e.tensor_tensor(out=tinyA[:], in0=m1c[:, t:t + 1], in1=mv4[:, 0:1],
                                op=ALU.subtract)
                e.tensor_tensor(out=tinyB[:], in0=mv4[:, 0:1], in1=m2a[:, t:t + 1],
                                op=ALU.subtract)
                e.drain()
                e.tensor_tensor(out=tinyA[:], in0=tinyA[:], in1=tinyB[:], op=ALU.max)
                e.drain()
                e.tensor_scalar(out=tinyA[:], in0=tinyA[:], scalar1=rs4[:, t:t + 1],
                                scalar2=None, op0=ALU.mult)
                e.drain()
                e.tensor_scalar(out=m1c[:, t:t + 1], in0=tinyA[:], scalar1=EPS,
                                scalar2=None, op0=ALU.max)
                e.drain()
                e.reciprocal(out=tinyB[:], in_=m1c[:, t:t + 1])
                e.drain()
                e.tensor_scalar(out=tinyB[:], in0=tinyB[:], scalar1=127.0, scalar2=None,
                                op0=ALU.mult)
                e.drain()
                e.tensor_scalar(out=sc4[:, t:t + 1], in0=tinyB[:],
                                scalar1=rs4[:, t:t + 1], scalar2=None, op0=ALU.mult)
                e.drain()
                e.tensor_tensor(out=nb4[:, t:t + 1], in0=mv4[:, 0:1],
                                in1=sc4[:, t:t + 1], op=ALU.mult)
                e.drain()
                e.tensor_scalar(out=nb4[:, t:t + 1], in0=nb4[:, t:t + 1], scalar1=-1.0,
                                scalar2=None, op0=ALU.mult)
                e.drain()
                e.tensor_tensor(out=tinyB[:], in0=m1c[:, t:t + 1], in1=s_upt[:], op=ALU.mult)
                e.drain()
                e.tensor_scalar(out=tinyB[:], in0=tinyB[:], scalar1=INV127, scalar2=None,
                                op0=ALU.mult)
                e.drain()
                e.tensor_tensor(out=a2f[:, t:t + 1], in0=tinyB[:], in1=tinyB[:], op=ALU.mult)
                e.drain()
            p.op("vector", vq1b)
            v_ch1 = p.sig("vector", SV)
            p.wait("scalar", SV, v_ch1)

            def sq1b(e, t=t):
                xa = x2q[:, t * D:(t + 1) * D]
                e.activation(out=vtmp2[:, 0:D], in_=xa, func=ACT.Identity,
                             bias=nb4[:, t:t + 1], scale=sc4[:, t:t + 1])
                e.drain()
                e.activation(out=vtmp2[:, 0:D], in_=vtmp2[:, 0:D], func=ACT.Identity,
                             bias=m64b[:], scale=1.0)
                e.drain()
                e.activation(out=vtmp2[:, 0:D], in_=vtmp2[:, 0:D], func=ACT.Identity,
                             bias=nm64b[:], scale=1.0)
                e.drain()
            p.op("scalar", sq1b)
            a_q1b = p.sig("scalar", SA)

            def vq1r(e, t=t):
                xa = x2q[:, t * D:(t + 1) * D]
                e.tensor_scalar(out=vtmp[:, 0:D], in0=xa, scalar1=sc4[:, t:t + 1],
                                scalar2=nb4[:, t:t + 1], op0=ALU.mult, op1=ALU.add)
                e.drain()
                e.tensor_scalar(out=vtmp[:, 0:D], in0=vtmp[:, 0:D], scalar1=M32,
                                scalar2=M32, op0=ALU.add, op1=ALU.subtract)
                e.drain()
            p.op("vector", vq1r)
            p.wait("vector", SA, a_q1b)

            def vq1c(e, t=t):
                e.scalar_tensor_tensor(out=qtile[:, 0:D], in0=vtmp[:, 0:D], scalar=mu[:],
                                       in1=vtmp2[:, 0:D], op0=ALU.mult, op1=ALU.add)
                e.drain()
            p.op("vector", vq1c)
            v_q1 = p.sig("vector", SV)
            p.wait("sync", SV, v_q1)
            tr = p.dma("sync", lambda e, t=t: e.dma_start_transpose(
                out=q1fm[:].rearrange("p (c s) -> p c s", s=512)[:, 0:DC, t * 128:(t + 1) * 128],
                in_=qtile[:, 0:D].rearrange("p (c f) -> p c f", f=128)), "ftr")
            q1_tr.append(tr)

        ffn1_mm = []
        ffn1_last_t = {}
        ev_h = {}
        q2_tr = []
        tdn_ld = {}
        tdn_last = {}

        def tdn_half_load(k):
            r, h = k // 2, k % 2
            s = SOF[k % 3]
            lt = None
            for i in range(16):
                lt = p.dma("sync", lambda e, r=r, h=h, i=i, s=s: e.dma_start(
                    out=wsl_f[:, s * SLOT + i * 512: s * SLOT + (i + 1) * 512],
                    in_=tdn_sp[r, h * 16 + i]), f"tdn{s}")
            tdn_ld[k] = lt

        for j in range(I // 512):
            if j < 3:
                tl = tup_pre[j]
            else:
                p.wait("sync", ST, ffn1_mm[j - 3])
                lt = None
                for c in range(DC):
                    lt = p.dma("sync", lambda e, j=j, c=c: e.dma_start(
                        out=wsl_f[:, (j % 3) * SLOT + c * 512:
                                  (j % 3) * SLOT + (c + 1) * 512],
                        in_=tup_sp[j, c]), f"tup{j % 3}")
                tl = lt
            p.wait("tensor", tl)
            for t in range(4):
                par = 4 + (j * 4 + t) % 4
                p.wait("tensor", q1_tr[t])
                if j * 4 + t >= 4:
                    p.wait("tensor", SV, ev_h[j * 4 + t - 4])
                t_mm = None
                for c in range(DC):
                    t_mm = p.op("tensor", lambda e, par=par, c=c, t=t, j=j: e.matmul(
                        PS[par][:, :],
                        q1fm[:, c * 512 + t * 128: c * 512 + (t + 1) * 128],
                        wsl_f[:, (j % 3) * SLOT + c * 512:
                              (j % 3) * SLOT + (c + 1) * 512],
                        start=(c == 0), stop=(c == DC - 1)),
                        inc=ST if c == DC - 1 else None)
                if t == 3:
                    ffn1_mm.append(t_mm)
                p.wait("vector", ST, t_mm)

                def vh2(e, par=par, t=t, j=j):
                    dst = h2q[:, t * I + j * 512: t * I + (j + 1) * 512]
                    e.tensor_copy(out=dst, in_=PS[par][:, :])
                    e.drain()
                    e.scalar_tensor_tensor(out=dst, in0=dst, scalar=0.0,
                                           in1=dst, op0=ALU.max, op1=ALU.mult)
                    e.drain()
                    e.tensor_reduce(out=tinyA[:], in_=dst, axis=AX, op=ALU.max)
                    e.drain()
                    if j > 0:
                        e.tensor_tensor(out=m2a[:, t:t + 1], in0=m2a[:, t:t + 1],
                                        in1=tinyA[:], op=ALU.max)
                    else:
                        e.tensor_copy(out=m2a[:, t:t + 1], in_=tinyA[:])
                    e.drain()
                p.op("vector", vh2)
                ev_h[j * 4 + t] = p.sig("vector", SV)

                if j != I // 512 - 1:
                    continue
                ffn1_last_t[t] = t_mm
                if t == 0:
                    p.wait("sync", ST, ffn1_mm[5])
                    tdn_half_load(1)
                    p.wait("sync", ST, ffn1_mm[6])
                    tdn_half_load(0)
                    p.wait("vector", q1_tr[3])
                else:
                    p.wait("vector", q2_tr[t - 1])

                def vq2a(e, t=t):
                    e.tensor_tensor(out=tinyA[:], in0=a2f[:, t:t + 1],
                                    in1=m2a[:, t:t + 1], op=ALU.mult)
                    e.drain()
                    e.tensor_scalar(out=tinyA[:], in0=tinyA[:], scalar1=EPS,
                                    scalar2=None, op0=ALU.max)
                    e.drain()
                    e.tensor_tensor(out=f2[:, t:t + 1], in0=tinyA[:], in1=s_dnt[:],
                                    op=ALU.mult)
                    e.drain()
                    e.tensor_scalar(out=f2[:, t:t + 1], in0=f2[:, t:t + 1],
                                    scalar1=INV127, scalar2=None, op0=ALU.mult)
                    e.reciprocal(out=tinyB[:], in_=tinyA[:])
                    e.drain()
                    e.tensor_scalar(out=tinyB[:], in0=tinyB[:], scalar1=127.0,
                                    scalar2=None, op0=ALU.mult)
                    e.drain()
                    e.tensor_tensor(out=s2e[:], in0=tinyB[:], in1=a2f[:, t:t + 1],
                                    op=ALU.mult)
                    e.drain()
                p.op("vector", vq2a)
                v_gate = p.sig("vector", SV)
                for hf in range(2):
                    p.wait("scalar", SV, v_gate)

                    def s2b(e, t=t, hf=hf):
                        ha = h2q[:, t * I + hf * D: t * I + (hf + 1) * D]
                        e.activation(out=vtmp2[:], in_=ha, func=ACT.Identity,
                                     bias=m64b[:], scale=s2e[:])
                        e.drain()
                        e.activation(out=vtmp2[:], in_=vtmp2[:], func=ACT.Identity,
                                     bias=nm64b[:], scale=1.0)
                        e.drain()
                    p.op("scalar", s2b)
                    a_b = p.sig("scalar", SA)

                    def v2a(e, t=t, hf=hf):
                        ha = h2q[:, t * I + hf * D: t * I + (hf + 1) * D]
                        e.tensor_scalar(out=vtmp[:], in0=ha, scalar1=s2e[:],
                                        scalar2=M32, op0=ALU.mult, op1=ALU.add)
                        e.drain()
                        e.tensor_scalar(out=vtmp[:], in0=vtmp[:], scalar1=M32,
                                        scalar2=None, op0=ALU.subtract)
                        e.drain()
                    p.op("vector", v2a)
                    p.wait("vector", SA, a_b)

                    def v2c(e, t=t, hf=hf):
                        e.scalar_tensor_tensor(out=qtile[:, hf * D:(hf + 1) * D],
                                               in0=vtmp[:], scalar=mu[:],
                                               in1=vtmp2[:], op0=ALU.mult, op1=ALU.add)
                        e.drain()
                    p.op("vector", v2c)
                    v_gate = p.sig("vector", SV)
                v_q2 = v_gate
                if t == 3:
                    p.wait("sync", ST, ffn1_last_t[3])
                    tdn_half_load(2)
                p.wait("sync", SV, v_q2)
                p.wait("sync", ST, ffn1_last_t[t])
                tr = p.dma("sync", lambda e, t=t: e.dma_start_transpose(
                    out=q2fm[:].rearrange("p (c s) -> p c s", s=512)[:, :, t * 128:(t + 1) * 128],
                    in_=qtile[:].rearrange("p (c f) -> p c f", f=128)), "ftr")
                q2_tr.append(tr)
        last_q2tr = q2_tr[3]

        ffn2_mm = []
        ev_o2 = {}
        for r in range(NQB):
            for h in range(2):
                k = 2 * r + h
                if k not in tdn_ld:
                    p.wait("sync", ST, tdn_last[k - 3])
                    tdn_half_load(k)
            p.wait("tensor", tdn_ld[2 * r])
            for t in range(4):
                par = 4 + (r * 4 + t) % 4
                p.wait("tensor", q2_tr[t])
                if r * 4 + t >= 4:
                    p.wait("tensor", SV, ev_o2[r * 4 + t - 4])
                t_mm = None
                for c in range(IC):
                    if c == 16 and t == 0:
                        p.wait("tensor", tdn_ld[2 * r + 1])
                    s = SOF[(2 * r + c // 16) % 3]
                    t_mm = p.op("tensor", lambda e, par=par, c=c, t=t, s=s: e.matmul(
                        PS[par][:, :],
                        q2fm[:, c * 512 + t * 128: c * 512 + (t + 1) * 128],
                        wsl_f[:, s * SLOT + (c % 16) * 512:
                              s * SLOT + (c % 16 + 1) * 512],
                        start=(c == 0), stop=(c == IC - 1)),
                        inc=ST if (c == IC - 1 or (t == 3 and c == 15)) else None)
                    if t == 3 and c == 15:
                        tdn_last[2 * r] = t_mm
                if t == 3:
                    ffn2_mm.append(t_mm)
                    tdn_last[2 * r + 1] = t_mm
                p.wait("vector", ST, t_mm)
                p.op("vector", lambda e, par=par, t=t, r=r: e.tensor_scalar(
                    out=out2q[:, t * D + r * 512: t * D + (r + 1) * 512],
                    in0=PS[par][:, :], scalar1=f2[:, t:t + 1], scalar2=None, op0=ALU.mult))
                ev_o2[r * 4 + t] = p.sig("vector", SV)
                if r == NQB - 1:
                    def vst4(e, t=t):
                        oa = out2q[:, t * D:(t + 1) * D]
                        for gi in range(4):
                            e.bn_stats(out=stats4[:, gi * 6:(gi + 1) * 6],
                                       in_=oa[:, gi * 512:(gi + 1) * 512])
                        e.drain()
                        e.bn_aggr(out=mv4[:], in_=stats4[:])
                        e.drain()
                    p.op("vector", vst4)
                    v_s4 = p.sig("vector", SV)
                    p.wait("scalar", SV, v_s4)

                    def srs4(e, t=t):
                        e.activation(out=rs4[:, t:t + 1], in_=mv4[:, 1:2], func=ACT.Ln,
                                     bias=eps_t[:], scale=1.0)
                        e.drain()
                        e.activation(out=rs4[:, t:t + 1], in_=rs4[:, t:t + 1],
                                     func=ACT.Exp, bias=0.0, scale=-0.5)
                        e.drain()
                        e.activation(out=nb4[:, t:t + 1], in_=mv4[:, 0:1],
                                     func=ACT.Identity, bias=0.0, scale=rs4[:, t:t + 1])
                        e.drain()
                        e.activation(out=nb4[:, t:t + 1], in_=nb4[:, t:t + 1],
                                     func=ACT.Identity, bias=0.0, scale=-1.0)
                        e.drain()
                    p.op("scalar", srs4)
                    a_s4 = p.sig("scalar", SA)
                    p.wait("vector", SA, a_s4)

                    def vfin(e, t=t):
                        oa = out2q[:, t * D:(t + 1) * D]
                        e.tensor_scalar(out=oa, in0=oa, scalar1=rs4[:, t:t + 1],
                                        scalar2=nb4[:, t:t + 1], op0=ALU.mult, op1=ALU.add)
                        e.drain()
                        e.tensor_tensor(out=x2q[:, t * D:(t + 1) * D], in0=oa,
                                        in1=x2q[:, t * D:(t + 1) * D], op=ALU.add)
                        e.drain()
                    p.op("vector", vfin)
                    v_fin = p.sig("vector", SV)
                    p.wait("sync", SV, v_fin)
                    st_done.append(p.dma("sync", lambda e, qq=qq, t=t: e.dma_start(
                        out=out[(qq * 4 + t) * 128:(qq * 4 + t + 1) * 128, :],
                        in_=x2q[:, t * D:(t + 1) * D]), f"out{t}"))

        if qq < 3:
            tup_pre = []
            for j, kfree in ((0, 6), (1, 5), (2, 7)):
                p.wait("sync", ST, tdn_last[kfree])
                lt = None
                for c in range(DC):
                    lt = p.dma("sync", lambda e, j=j, c=c: e.dma_start(
                        out=wsl_f[:, j * SLOT + c * 512:
                                  j * SLOT + (c + 1) * 512],
                        in_=tup_sp[j, c]), f"tup{j}")
                tup_pre.append(lt)

    for slot in range(4):
        p.wait("sync", f"out{slot}", p.n(f"out{slot}"))
    ph4.close()

    block = es.enter_context(nc.Block())
    for eng in Prog.ENGINES:
        ops = p.ops[eng]
        getattr(block, eng)(lambda e, ops=ops: [fn(e) for fn in ops])
    es.close()
    return nc


_NC_CACHE = {}


def kernel(L0_attn_X, L0_attn_Wq, L0_attn_bq, L0_attn_Wk, L0_attn_bk,
           L0_attn_Wv, L0_attn_bv, L0_attn_Wo, L0_attn_bo,
           L0_ffn_w_up, L0_ffn_w_down):
    X = np.ascontiguousarray(np.asarray(L0_attn_X, dtype=np.float32))
    wq_t = np.ascontiguousarray(np.asarray(L0_attn_Wq, np.float32).T)
    wk_t = np.ascontiguousarray(np.asarray(L0_attn_Wk, np.float32).T)
    wv_t = np.ascontiguousarray(np.asarray(L0_attn_Wv, np.float32).T)
    wo_t = np.ascontiguousarray(np.asarray(L0_attn_Wo, np.float32).T)
    wup_t = np.ascontiguousarray(np.asarray(L0_ffn_w_up, np.float32).T)
    wdn_t = np.ascontiguousarray(np.asarray(L0_ffn_w_down, np.float32).T)
    bqv = np.ascontiguousarray(np.asarray(L0_attn_bq, np.float32))
    bkv = np.ascontiguousarray(np.asarray(L0_attn_bk, np.float32))
    bvv = np.ascontiguousarray(np.asarray(L0_attn_bv, np.float32))
    bov = np.ascontiguousarray(np.asarray(L0_attn_bo, np.float32))

    if "nc" not in _NC_CACHE:
        _NC_CACHE["nc"] = build_core()
    nc = _NC_CACHE["nc"]
    in_maps = []
    for c in range(8):
        in_maps.append({
            "x": np.ascontiguousarray(X[c]),
            "wqt": wq_t, "wkt": wk_t, "wvt": wv_t, "wot": wo_t,
            "wupt": wup_t, "wdnt": wdn_t,
            "bq": bqv, "bk": bkv, "bv": bvv, "bo": bov,
        })
    res = run_bass_kernel_spmd(nc, in_maps, core_ids=list(range(8)))
    return np.stack([np.asarray(r["out"], np.float32) for r in res.results], axis=0)
